# version 33
# baseline (speedup 1.0000x reference)
import numpy as np
import ml_dtypes
import concourse.bass as bass
import concourse.mybir as mybir
from concourse.bass_utils import run_bass_kernel_spmd
from contextlib import ExitStack

F32 = mybir.dt.float32
BF16 = mybir.dt.bfloat16
I32 = mybir.dt.int32
AF = mybir.ActivationFunctionType
ALU = mybir.AluOpType
AX = mybir.AxisListType

T = 4096
E = 1024
NT = T // 128
NE = E // 128
NEG = -1.0e30

ENGINES = ["sync", "vector", "scalar", "tensor", "gpsimd"]
DMA_ISSUERS = ["sync", "scalar", "gpsimd"]
DMA_SEMS_PER_QUEUE = 6


def _span(ap):
    lo = ap.offset
    hi = ap.offset
    for step, count in ap.ap:
        if count > 1:
            if step >= 0:
                hi += step * (count - 1)
            else:
                lo += step * (count - 1)
    return (ap.tensor.name, lo, hi + 1)


class FW:
    def __init__(self, nc):
        self.nc = nc
        self.prog = {e: [] for e in ENGINES}
        self.n = {e: 0 for e in ENGINES}
        self.marked = {e: set() for e in ENGINES}
        self.dma_sems = {e: [f"d_{e}_{k}" for k in range(DMA_SEMS_PER_QUEUE)]
                         for e in DMA_ISSUERS}
        self.dma_rr = {e: 0 for e in DMA_ISSUERS}
        self.dma_n = {s: 0 for e in DMA_ISSUERS for s in self.dma_sems[e]}
        self.mem = {}
        self.wm = {e: {} for e in ENGINES}
        self.fence_pending = []
        self.tensor_fences = {}
        self._finalized = False

    def _edges_for(self, reads, writes):
        edges = []

        def overlap(lst, lo, hi):
            return [(d, i) for (l, h, d, i) in lst if not (hi <= l or h <= lo)]

        spans_r = [_span(ap) for ap in reads]
        spans_w = [_span(ap) for ap in writes]
        for name, lo, hi in spans_r:
            rec = self.mem.get(name)
            if rec:
                edges += overlap(rec["w"], lo, hi)
        for name, lo, hi in spans_w:
            rec = self.mem.get(name)
            if rec:
                edges += overlap(rec["w"], lo, hi)
                edges += overlap(rec["r"], lo, hi)
        return edges, spans_r, spans_w

    def _record_access(self, spans_r, spans_w, domain, idx):
        for name, lo, hi in spans_r:
            rec = self.mem.setdefault(name, {"w": [], "r": []})
            rec["r"].append((lo, hi, domain, idx))
        for name, lo, hi in spans_w:
            rec = self.mem.setdefault(name, {"w": [], "r": []})
            rec["w"] = [e for e in rec["w"] if not (lo <= e[0] and e[1] <= hi)]
            rec["r"] = [e for e in rec["r"] if not (lo <= e[0] and e[1] <= hi)]
            rec["w"].append((lo, hi, domain, idx))

    def _resolve_edges(self, eng, edges):
        waits = []
        best = {}
        for d, i in edges:
            if d == "tensor" and eng == "tensor":
                continue
            if best.get(d, -1) < i:
                best[d] = i
        for d, i in sorted(best.items()):
            if self.wm[eng].get(d, -1) >= i:
                continue
            self.wm[eng][d] = i
            if d in ENGINES:
                self.marked[d].add(i)
            waits.append((d, i))
        return waits

    def emit(self, eng, emitfn, reads=(), writes=(), is_dma=False):
        assert not self._finalized
        edges, spans_r, spans_w = self._edges_for(reads, writes)
        if self.fence_pending:
            edges = list(edges) + list(self.fence_pending)
        if self.tensor_fences:
            for sp in spans_r + spans_w:
                f = self.tensor_fences.get(sp[0])
                if f:
                    edges = list(edges) + f
        waits = self._resolve_edges(eng, edges)
        if is_dma:
            k = self.dma_rr[eng]
            self.dma_rr[eng] = (k + 1) % DMA_SEMS_PER_QUEUE
            q = self.dma_sems[eng][k]
            if self.dma_n[q] > 0:
                waits += self._resolve_edges(eng, [(q, self.dma_n[q])])
            self.dma_n[q] += 16
            domain, idx = q, self.dma_n[q]
        else:
            domain, idx = eng, self.n[eng]
        self.prog[eng].append((waits, emitfn, {"domain": domain, "idx": idx,
                                               "is_dma": is_dma}))
        if not is_dma:
            self.n[eng] += 1
        self._record_access(spans_r, spans_w, domain, idx)
        return (domain, idx)

    def all_dma_tokens(self):
        return [(q, n) for q, n in self.dma_n.items() if n > 0]

    def fence(self):
        toks = self.fence_tokens()
        self.fence_pending = toks

    def fence_tokens(self):
        toks = [(e, self.n[e] - 1) for e in ENGINES if self.n[e] > 0]
        toks += self.all_dma_tokens()
        return toks

    def finalize_and_register(self, block, tail_wait=None):
        self._finalized = True
        nc = self.nc
        sems = {}
        for e in ENGINES:
            if self.marked[e]:
                sems[e] = nc.alloc_semaphore("sem_" + e)
        for q in self.dma_n:
            if self.dma_n[q]:
                sems[q] = nc.alloc_semaphore("sem_" + q)
        incval = {}
        for e in ENGINES:
            cum = {}
            c = 0
            for i in sorted(self.marked[e]):
                c += 1
                cum[i] = c
            incval[e] = cum

        def wait_value(d, i):
            if d in ENGINES:
                return incval[d][i]
            return i

        def make_section(ename):
            stream = self.prog[ename]
            marked = self.marked[ename]

            def section(engobj):
                for waits, emitfn, meta in stream:
                    for (d, i) in waits:
                        engobj.wait_ge(sems[d], wait_value(d, i))
                    if emitfn is None:
                        continue
                    inst = emitfn(engobj)
                    if meta["is_dma"]:
                        inst.then_inc(sems[meta["domain"]], 16)
                    elif meta["idx"] in marked:
                        inst.then_inc(sems[ename], 1)
                if tail_wait and tail_wait[0] == ename:
                    for (d, i) in tail_wait[1]:
                        engobj.wait_ge(sems[d], wait_value(d, i))
            return section

        for ename in ENGINES:
            if self.prog[ename] or (tail_wait and tail_wait[0] == ename):
                getattr(block, ename)(make_section(ename))


class B:

    def __init__(self, name):
        self.nc = bass.Bass("TRN2", target_bir_lowering=False)
        self.fw = FW(self.nc)
        self.stack = ExitStack()
        self.scopes = [self.stack]
        self.names = set()
        self.n_inst = 0

    def push(self):
        st = ExitStack()
        self.scopes.append(st)

    def pop(self):
        self._lazy_fence = self.fw.fence_tokens()
        self.scopes.pop().close()

    def _nm(self, base):
        i = 0
        nm = base
        while nm in self.names:
            i += 1
            nm = f"{base}_{i}"
        self.names.add(nm)
        return nm

    def inp(self, name, shape, dt=F32):
        return self.nc.dram_tensor(name, list(shape), dt, kind="ExternalInput").ap()

    def outp(self, name, shape, dt=F32):
        return self.nc.dram_tensor(name, list(shape), dt, kind="ExternalOutput").ap()

    def dram(self, name, shape, dt=F32):
        return self.nc.dram_tensor(name, list(shape), dt, kind="Internal").ap()

    def sb(self, name, shape, dt=F32):
        h = self.scopes[-1].enter_context(self.nc.sbuf_tensor(self._nm(name), list(shape), dt))
        if getattr(self, "_lazy_fence", None):
            self.fw.tensor_fences[h.name] = list(self._lazy_fence)
        return h

    def ps(self, name, shape, dt=F32):
        h = self.scopes[-1].enter_context(self.nc.psum_tensor(self._nm(name), list(shape), dt))
        if getattr(self, "_lazy_fence", None):
            self.fw.tensor_fences[h.name] = list(self._lazy_fence)
        return h

    def dma(self, eng, out, in_):
        return self.fw.emit(eng, lambda e: e.dma_start(out=out, in_=in_),
                            reads=[in_], writes=[out], is_dma=True)

    def mm(self, out, lhsT, rhs, start=True, stop=True, tile_position=None):
        return self.fw.emit("tensor",
                            lambda e: e.matmul(out, lhsT, rhs, start=start, stop=stop,
                                               tile_position=tile_position),
                            reads=[lhsT, rhs], writes=[out])

    def trans(self, out_ps, in_, ident):
        return self.fw.emit("tensor", lambda e: e.transpose(out_ps, in_, ident),
                            reads=[in_, ident], writes=[out_ps])

    def act(self, eng, out, in_, func, bias=0.0, scale=1.0, accum=None):
        r = [in_]
        w = [out]
        if isinstance(bias, bass.AP):
            r.append(bias)
        if isinstance(scale, bass.AP):
            r.append(scale)
        if accum is not None:
            w.append(accum)
        return self.fw.emit(eng, lambda e: e.activation(out, in_, func, bias=bias,
                                                        scale=scale, accum_out=accum),
                            reads=r, writes=w)

    def copy(self, eng, out, in_):
        if eng == "scalar":
            return self.fw.emit(eng, lambda e: e.copy(out, in_), reads=[in_], writes=[out])
        return self.fw.emit(eng, lambda e: e.tensor_copy(out, in_), reads=[in_], writes=[out])

    def tt(self, eng, out, in0, in1, op):
        return self.fw.emit(eng, lambda e: e.tensor_tensor(out, in0, in1, op=op),
                            reads=[in0, in1], writes=[out])

    def ts(self, eng, out, in0, s1, s2=None, op0=ALU.mult, op1=ALU.bypass):
        r = [in0]
        if isinstance(s1, bass.AP):
            r.append(s1)
        if isinstance(s2, bass.AP):
            r.append(s2)
        if s2 is None:
            return self.fw.emit(eng, lambda e: e.tensor_single_scalar(out, in0, s1, op=op0),
                                reads=r, writes=[out])
        return self.fw.emit(eng, lambda e: e.tensor_scalar(out, in0, s1, s2, op0=op0, op1=op1),
                            reads=r, writes=[out])

    def stt(self, eng, out, in0, s, in1, op0, op1):
        r = [in0, in1]
        if isinstance(s, bass.AP):
            r.append(s)
        return self.fw.emit(eng, lambda e: e.scalar_tensor_tensor(out, in0, s, in1,
                                                                  op0=op0, op1=op1),
                            reads=r, writes=[out])

    def red(self, eng, out, in_, op=ALU.max, absval=False, neg=False):
        return self.fw.emit(eng, lambda e: e.tensor_reduce(out, in_, axis=AX.X, op=op,
                                                           apply_absolute_value=absval,
                                                           negate=neg),
                            reads=[in_], writes=[out])

    def memset(self, eng, ap, val):
        return self.fw.emit(eng, lambda e: e.memset(ap, val), reads=[], writes=[ap])

    def smul(self, out, in_, c):
        return self.fw.emit("scalar", lambda e, o=out, i=in_: e.mul(o, i, c),
                            reads=[in_], writes=[out])

    def cpred(self, out, mask, data):
        return self.fw.emit("vector",
                            lambda e, o=out, m=mask, d=data: e.copy_predicated(o, m, d),
                            reads=[mask, data], writes=[out])

    def recip(self, out, in_):
        return self.fw.emit("vector", lambda e: e.reciprocal(out, in_),
                            reads=[in_], writes=[out])

    def done(self, tail_eng="sync"):
        while len(self.scopes) > 1:
            self.scopes.pop().close()
        block = self.stack.enter_context(self.nc.Block())
        self.fw.finalize_and_register(block, tail_wait=(tail_eng, self.fw.all_dma_tokens()))
        self.n_inst = sum(len(v) for v in self.fw.prog.values())
        self.stack.close()
        return self.nc


def alloc_xnT(b):
    xn_all = b.sb("xnT_all", (128, NE * T), BF16)
    return [xn_all[:, T * i:T * (i + 1)] for i in range(NE)]


def emit_ln(b, x_dram, xn_tiles, consts, fp32_sink=None, x_tiles_out=None):
    identb = consts["identb"]
    S = []
    for s in range(2):
        d = {}
        d["x_t"] = b.sb("ln_x", (128, E))
        d["sqs"] = b.sb("ln_sq", (128, E), BF16)
        d["ssq"] = b.sb("ln_ssq", (128, 1))
        d["srow"] = b.sb("ln_s", (128, 1))
        d["m"] = b.sb("ln_m", (128, 1))
        d["msq"] = b.sb("ln_msq", (128, 1))
        d["var"] = b.sb("ln_var", (128, 1))
        d["sd"] = b.sb("ln_sd", (128, 1))
        d["rs"] = b.sb("ln_rs", (128, 1))
        d["bl"] = b.sb("ln_bl", (128, 1))
        if xn_tiles is not None:
            d["xn"] = b.sb("ln_xn", (128, E), BF16)
        if fp32_sink:
            d["xnf"] = b.sb("ln_xnf", (128, E))
        S.append(d)
    eps_t = b.sb("ln_eps", (128, 1))
    b.memset("vector", eps_t[:], 1e-5)
    ps_tr = b.ps("ln_pstr", (128, 512), BF16) if xn_tiles is not None else None
    for t in range(NT):
        d = S[t % 2]
        x_t, sqs, ssq, srow = d["x_t"], d["sqs"], d["ssq"], d["srow"]
        m, msq, var, sd, rs, bl = d["m"], d["msq"], d["var"], d["sd"], d["rs"], d["bl"]
        b.dma("sync", x_t[:], x_dram[128 * t:128 * (t + 1), :])
        if x_tiles_out is not None:
            b.dma("scalar", x_tiles_out[128 * t:128 * (t + 1), :], x_t[:])
        b.act("scalar", sqs[:], x_t[:], AF.Square, accum=ssq[:, 0:1])
        b.red("vector", srow[:, 0:1], x_t[:], op=ALU.add)
        b.ts("vector", m[:], srow[:], 1.0 / E)
        b.tt("vector", msq[:], m[:], m[:], ALU.mult)
        b.stt("vector", var[:], ssq[:], 1.0 / E, msq[:], ALU.mult, ALU.subtract)
        b.act("scalar", sd[:], var[:], AF.Sqrt, bias=eps_t[:, 0:1])
        b.recip(rs[:], sd[:])
        b.stt("vector", bl[:], m[:], -1.0, rs[:], ALU.mult, ALU.mult)
        if xn_tiles is not None:
            xn = d["xn"]
            b.act("scalar", xn[:], x_t[:], AF.Identity, bias=bl[:, 0:1], scale=rs[:, 0:1])
        if fp32_sink:
            xnf = d["xnf"]
            b.act("scalar", xnf[:], x_t[:], AF.Identity, bias=bl[:, 0:1], scale=rs[:, 0:1])
            fp32_sink(t, xnf)
        if xn_tiles is not None:
            xn_all = xn_tiles[0].tensor
            for g4 in range(2):
                for slot in range(4):
                    i = 4 * g4 + slot
                    b.trans(ps_tr[:, 128 * slot:128 * (slot + 1)],
                            xn[:, 128 * i:128 * (i + 1)], identb[:])
                dst = bass.AP(xn_all, 4 * g4 * T + 128 * t,
                              [xn_tiles[0].ap[0], [T, 4], [1, 128]])
                b.fw.emit("scalar", lambda e, d=dst, s=ps_tr[:, 0:512]: e.copy(d, s),
                          reads=[ps_tr[:, 0:512]], writes=[dst])


def load_cast(b, dst_bf, src_dram, tmp, eng="gpsimd", scale=None):
    b.dma("sync", tmp[:], src_dram)
    if scale is None:
        b.copy(eng, dst_bf[:], tmp[:])
    else:
        b.smul(dst_bf[:], tmp[:], scale)


def emit_consts(b, rb=True):
    c = {}
    ident_in = b.inp("c_ident", (128, 128))
    c["identf"] = b.sb("identf", (128, 128))
    c["identb"] = b.sb("identb", (128, 128), BF16)
    b.dma("sync", c["identf"][:], ident_in)
    b.copy("gpsimd", c["identb"][:], c["identf"][:])
    if rb:
        rbkq_in = b.inp("c_rbkq", (128, 128))
        c["rbkqf"] = b.sb("rbkqf", (128, 128))
        c["rbkqb"] = b.sb("rbkqb", (128, 128), BF16)
        c["rbqkb"] = b.sb("rbqkb", (128, 128), BF16)
        c["fullnegb"] = b.sb("fullnegb", (128, 128), BF16)
        b.dma("sync", c["rbkqf"][:], rbkq_in)
        b.copy("gpsimd", c["rbkqb"][:], c["rbkqf"][:])
        b.memset("vector", c["fullnegb"][:], NEG)
        b.push()
        ps = b.ps("c_ps", (128, 128), BF16)
        b.trans(ps[:], c["rbkqb"][:], c["identb"][:])
        b.copy("scalar", c["rbqkb"][:], ps[:])
        b.pop()
    c["ones1"] = b.sb("ones1", (1, 128), BF16)
    b.memset("vector", c["ones1"][:], 1.0)
    return c


def host_consts(rb=True):
    ident = np.eye(128, dtype=np.float32)
    d = {"c_ident": ident}
    if rb:
        d["c_rbkq"] = np.where(np.arange(128)[:, None] > np.arange(128)[None, :],
                               np.float32(NEG), np.float32(0.0))
    return d


def flash_bufs(b):
    return {"mcol": [b.sb("fl_mcol", (128, 128)), b.sb("fl_mcol2", (128, 128))],
            "PT": [b.sb("fl_pt", (128, 1024), BF16), b.sb("fl_pt2", (128, 1024), BF16)],
            "rd": (lambda _r=b.sb("fl_rd", (1, 512)): [_r, _r])(),
            "onesf": (lambda _o=b.sb("fl_onesf", (1, 64)):
                      (b.memset("vector", _o[:], 1.0), _o)[1])(),
            "rsb": (lambda _r=b.sb("fl_rsb", (64, 512)): [_r, _r])()}


def flash_kmajor(b, consts, bufs, h_pair, ql_of, latKT_of, latV_of, kmax_of,
                 ctx_sink, sample_of, psR, ps_ctx2, sample_scale, qg_filler=None):
    identb = consts["identb"]
    rbkqb = consts["rbkqb"]
    mcol, PT, rd, rsb = bufs["mcol"], bufs["PT"], bufs["rd"], bufs["rsb"]
    onesf = bufs["onesf"]
    slots = list(enumerate(h_pair))
    ql = {}
    kts = {}
    for s, h in slots:
        ql[s] = ql_of(h)
        kts[s] = latKT_of(h)
    for qg in range(8):
        for tq in range(4):
            for s, h in slots:
                gt = 4 * qg + tq
                slot = 2 * tq + s
                reg = psR[0][:, 128 * slot:128 * (slot + 1)]
                s_lhsT, s_rhs, masked = sample_of(h, gt)
                b.mm(reg, s_lhsT, s_rhs, start=True, stop=not masked)
                if masked:
                    b.mm(reg, identb[:], consts["rbqkb"][:], start=False, stop=True)
                b.red("vector", mcol[s][:, 32 * tq:32 * tq + 1], reg,
                      op=ALU.max, neg=True)
        for s, h in slots:
            b.trans(psR[1][:, 128 * s:128 * (s + 1)], mcol[s][:], consts["identf"][:])
            for tq in range(4):
                b.copy("vector",
                       ql[s][64:65, 512 * qg + 128 * tq:512 * qg + 128 * (tq + 1)],
                       psR[1][32 * tq:32 * tq + 1, 128 * s:128 * (s + 1)])
    pending = []

    def flush_pending():
        while pending:
            pending.pop(0)()

    for qg in range(8):
        kmax = kmax_of(qg)
        jobs = []
        kt = 0
        while kt < kmax:
            g = min(2, kmax - kt)
            for s, _h in slots:
                jobs.append((s, kt, g))
            kt += g
        J = len(jobs)

        def st(j, qg=qg):
            s, kt0, g = jobs[j]
            ps = psR[j % 3]
            for jj in range(g):
                ktj = kt0 + jj
                d = ktj - 4 * qg
                extras = []
                if 0 <= d <= 3:
                    extras.append((128 * d, rbkqb))
                    for tq in range(max(0, d)):
                        extras.append((128 * tq, consts["fullnegb"]))
                b.mm(ps[:, 512 * jj:512 * (jj + 1)],
                     kts[s][:, 128 * ktj:128 * (ktj + 1)],
                     ql[s][:, 512 * qg:512 * (qg + 1)], start=True, stop=not extras)
                for ei, (off, mk) in enumerate(extras):
                    b.mm(ps[:, 512 * jj + off:512 * jj + off + 128],
                         identb[:], mk[:], start=False, stop=(ei == len(extras) - 1))

        def exp_pv(j, qg=qg, kmaxl=None):
            s, kt0, g = jobs[j]
            ps = psR[j % 3]
            pt = PT[s]
            h = h_pair[s]
            b.act("scalar", pt[:, 0:512 * g], ps[:, 0:512 * g], AF.Exp,
                  scale=sample_scale)
            for jj in range(g):
                ktj = kt0 + jj
                b.mm(ps_ctx2[s][0:65, :], latV_of(h, ktj), pt[:, 512 * jj:512 * (jj + 1)],
                     start=(ktj == 0), stop=(ktj == kmax - 1))

        st(0)
        if J > 1:
            st(1)
        flush_pending()
        if qg_filler is not None and qg == 4:
            qg_filler()
        if J > 2:
            st(2)
        for j in range(J):
            exp_pv(j)
            if j + 3 < J:
                st(j + 3)

        def mknorm(s, h, qg=qg):
            def norm():
                b.recip(rd[s][:], ps_ctx2[s][64:65, :])
                b.mm(ps_ctx2[s][64:128, :], onesf[0:1, 0:64], rd[s][:],
                     tile_position=(0, 64))
                b.copy("vector", rsb[s][:], ps_ctx2[s][64:128, :])
                ctx_sink(h, qg, ps_ctx2[s][0:64, :], rsb[s][:])
            return norm
        for s, h in slots:
            pending.append(mknorm(s, h))
    flush_pending()


def build_l0_attn(b=None, consts=None, x_in=None, x_out=None):
    standalone = b is None
    if standalone:
        b = B("l0a")
        x_in = b.inp("x", (T, E))
        x_out = b.outp("y", (T, E))
    wqT_in = b.inp("wqT", (E, E))
    wkvT_in = b.inp("wkvT", (E, 128))
    apk_in = b.inp("apk", (8, 128, 128))
    vup_in = b.inp("vup", (8, 128, 128))
    woutT_in = b.inp("woutT", (E, E))
    qlat_dram = b.dram("qlat_st", (16, 64, T), BF16)

    if consts is None:
        consts = emit_consts(b)
    wo = [b.sb(f"wo{k}", (128, E), BF16) for k in range(NE)]
    vup = b.sb("vup_outer", (128, 1024), BF16)
    latKT = b.sb("latKT", (65, T), BF16)
    latV = b.sb("latV", (128, 65 * NT), BF16)

    b.push()
    xn = alloc_xnT(b)
    emit_ln(b, x_in, xn, consts)

    wtmp = b.sb("wtmp", (128, E))
    wq = [b.sb(f"wq{k}", (128, E), BF16) for k in range(NE)]
    for k in range(NE):
        load_cast(b, wq[k], wqT_in[128 * k:128 * (k + 1), :], wtmp)
    wkv = [b.sb(f"wkv{k}", (128, 128), BF16) for k in range(NE)]
    wtmp2 = b.sb("wtmp2", (128, 128))
    for k in range(NE):
        load_cast(b, wkv[k], wkvT_in[128 * k:128 * (k + 1), :], wtmp2)
    apk = b.sb("apk_s", (128, 1024), BF16)
    atmp = b.sb("atmp", (128, 1024))
    for p in range(8):
        b.dma("sync", atmp[:, 128 * p:128 * (p + 1)], apk_in[p])
    b.smul(apk[:], atmp[:], 0.125)
    for p in range(8):
        b.dma("sync", atmp[:, 128 * p:128 * (p + 1)], vup_in[p])
    b.copy("gpsimd", vup[:], atmp[:])
    for k in range(NE):
        load_cast(b, wo[k], woutT_in[128 * k:128 * (k + 1), :], wtmp)

    ps_q = [b.ps("ps_q", (128, 512)), b.ps("ps_qb", (128, 512))]
    ps_l = [b.ps("ps_l", (128, 512)), b.ps("ps_lb", (128, 512))]
    qt_pair = [b.sb("qt_pair", (128, 512), BF16), b.sb("qt_pairb", (128, 512), BF16)]
    ql_sb = [b.sb("ql_sb", (128, 512), BF16), b.sb("ql_sbb", (128, 512), BF16)]
    for p in range(8):
        for n in range(8):
            w = n % 2
            for k in range(NE):
                b.mm(ps_q[w][:], wq[k][:, 128 * p:128 * (p + 1)],
                     xn[k][:, 512 * n:512 * (n + 1)], start=(k == 0), stop=(k == NE - 1))
            b.copy("vector", qt_pair[w][:], ps_q[w][:])
            b.mm(ps_l[w][:], apk[:, 128 * p:128 * (p + 1)], qt_pair[w][:])
            b.copy("vector", ql_sb[w][:], ps_l[w][:])
            b.dma("scalar", qlat_dram[2 * p, :, 512 * n:512 * (n + 1)], ql_sb[w][0:64, :])
            b.dma("scalar", qlat_dram[2 * p + 1, :, 512 * n:512 * (n + 1)], ql_sb[w][64:128, :])

    latVf = b.sb("latVf", (64, T), BF16)
    b.memset("vector", latKT[64:65, :], 1.0)
    b.memset("vector", latV[:], 1.0)
    ps_kv = [b.ps("ps_kv", (128, 512)), b.ps("ps_kvb", (128, 512))]
    for n in range(8):
        w = n % 2
        for k in range(NE):
            b.mm(ps_kv[w][:], wkv[k][:], xn[k][:, 512 * n:512 * (n + 1)],
                 start=(k == 0), stop=(k == NE - 1))
        b.copy("scalar", latKT[0:64, 512 * n:512 * (n + 1)], ps_kv[w][0:64, :])
        b.copy("scalar", latVf[:, 512 * n:512 * (n + 1)], ps_kv[w][64:128, :])
    ps_tv = b.ps("ps_tv", (128, 128), BF16)
    for t in range(NT):
        b.trans(ps_tv[:, 0:64], latVf[:, 128 * t:128 * (t + 1)], consts["identb"][0:64, 0:64])
        b.copy("scalar", latV[:, 65 * t:65 * t + 64], ps_tv[:, 0:64])

    b.pop()
    b.push()
    headsT = [b.sb(f"headsT{p}", (128, T), BF16) for p in range(8)]
    b.push()
    ql_buf = [b.sb(f"ql_buf{i}", (65, T), BF16) for i in range(4)]
    psR = [b.ps("psR0", (128, 1024)), b.ps("psR1", (128, 1024)),
           b.ps("psR2", (128, 1024))]
    ps_ctx2 = [b.ps("ps_ctx", (128, 512)), b.ps("ps_ctxB", (128, 512))]
    ctxn = b.sb("ctxn", (128, T), BF16)
    ps_vu = ps_ctx2[0]

    loaded = set()

    def load_h(h):
        if h in loaded or h > 15:
            return
        loaded.add(h)
        b.dma("sync", ql_buf[h % 4][0:64, :], qlat_dram[h])

    def ql_of(h):
        load_h(h)
        return ql_buf[h % 4]

    def sample_of(h, gt):
        buf = ql_buf[h % 4]
        stride = max(1, gt)
        lhsT = buf[0:64, 128 * gt:128 * (gt + 1)]
        rhs = latKT[0:64, 0:128 * stride:stride] if gt > 0 else latKT[0:64, 0:128]
        return lhsT, rhs, gt == 0

    def ctx_sink(h, qg, ctx_rows, rsb):
        dst = ctxn[64 * (h % 2):64 * (h % 2) + 64, 512 * qg:512 * (qg + 1)]
        b.tt("vector", dst, ctx_rows, rsb[:], ALU.mult)

    fbufs = flash_bufs(b)
    for p in range(8):
        flash_kmajor(b, consts, fbufs, [2 * p, 2 * p + 1], ql_of,
                     lambda hh: latKT,
                     lambda hh, kt: latV[:, 65 * kt:65 * (kt + 1)],
                     lambda qg: 4 * (qg + 1), ctx_sink, sample_of,
                     psR, ps_ctx2, 1.0,
                     qg_filler=(lambda p=p: (load_h(2 * p + 2), load_h(2 * p + 3))))
        for n in range(8):
            b.mm(ps_vu[:], vup[:, 128 * p:128 * (p + 1)], ctxn[:, 512 * n:512 * (n + 1)])
            b.copy("vector", headsT[p][:, 512 * n:512 * (n + 1)], ps_vu[:])

    b.pop()
    b.push()
    x_t2 = [b.sb("x_t2", (128, E)), b.sb("x_t2b", (128, E))]
    ps_o = [b.ps(f"ps_o{i}", (128, 512)) for i in range(4)]
    for t in range(NT):
        w = t % 2
        b.dma("sync", x_t2[w][:], x_in[128 * t:128 * (t + 1), :])
        for k in range(NE):
            b.mm(ps_o[2 * w][:], headsT[k][:, 128 * t:128 * (t + 1)], wo[k][:, 0:512],
                 start=(k == 0), stop=(k == NE - 1))
            b.mm(ps_o[2 * w + 1][:], headsT[k][:, 128 * t:128 * (t + 1)], wo[k][:, 512:1024],
                 start=(k == 0), stop=(k == NE - 1))
        b.tt("vector", x_t2[w][:, 0:512], x_t2[w][:, 0:512], ps_o[2 * w][:], ALU.add)
        b.tt("vector", x_t2[w][:, 512:1024], x_t2[w][:, 512:1024], ps_o[2 * w + 1][:], ALU.add)
        b.dma("scalar", x_out[128 * t:128 * (t + 1), :], x_t2[w][:])
    b.pop()
    if standalone:
        return b.done()
    return None


def l0_attn_inputs(x_core, inputs):
    return {
        "x": np.ascontiguousarray(x_core),
        "wqT": np.ascontiguousarray(inputs["L0_attn_Wq"].T),
        "wkvT": np.ascontiguousarray(np.concatenate(
            [inputs["L0_attn_Wk_down"].T, inputs["L0_attn_Wv_down"].T], axis=1)),
        "apk": pack_pairs_stack(inputs["L0_attn_q_to_latent"]),
        "vup": pack_pairs_diag(inputs["L0_attn_v_up"]),
        "woutT": np.ascontiguousarray(inputs["L0_attn_Wout"].T),
        **host_consts(),
    }


def pack_pairs_stack(A):
    out = np.zeros((8, 128, 128), dtype=np.float32)
    for p in range(8):
        out[p, 0:64, 0:64] = A[2 * p]
        out[p, 64:128, 64:128] = A[2 * p + 1]
    return out


def pack_pairs_diag(V):
    return pack_pairs_stack(V)




def build_l0_ffn(b=None, consts=None, x_in=None, x_out=None):
    standalone = b is None
    if standalone:
        b = B("l0f")
        x_in = b.inp("x", (T, E))
        x_out = b.outp("y", (T, E))
    wins = {}
    for pfx in "qkv":
        wins[pfx] = {
            "w1T": b.inp(f"{pfx}_w1T", (E, E)),
            "w2": b.inp(f"{pfx}_w2", (E, E)),
            "nwT": b.inp(f"{pfx}_nwT", (E, 15)),
            "nb": b.inp(f"{pfx}_nb", (1, 15)),
            "b1": b.inp(f"{pfx}_b1", (1, E)),
            "b2": b.inp(f"{pfx}_b2", (16, E)),
        }
    iota16_in = b.inp("iota16", (16, 1))
    e2_in = b.inp("e2", (16, E))
    hid_dram = b.dram("hid_st", (NE, 128, T), BF16)
    qT_dram = b.dram("qT_st", (NE, 128, T), BF16)
    kT_dram = b.dram("kT_st", (NE, 128, T), BF16)
    vT_dram = b.dram("vT_st", (NE, 128, T), BF16)

    if consts is None:
        consts = emit_consts(b, rb=False)
    onesrow_b = b.sb("onesrow_b", (1, 512), BF16)
    b.memset("vector", onesrow_b[:], 1.0)
    onesrow_f = b.sb("onesrow_f", (1, 512))
    b.memset("vector", onesrow_f[:], 1.0)
    onescol_b = b.sb("onescol_b", (128, 1), BF16)
    b.memset("vector", onescol_b[:], 1.0)
    iota16 = b.sb("iota16_s", (16, 1))
    b.dma("sync", iota16[:], iota16_in)
    kTs = [b.sb(f"kTs{k}", (128, 128), BF16) for k in range(NE)]

    b.push()
    xn = alloc_xnT(b)
    S_tok45 = b.sb("stok45", (128, 45 * NT))
    onehotT = {p: b.sb(f"oh_{p}", (16, T), BF16)[:, :] for p in "qkv"}
    e2b = b.sb("e2b", (16, E), BF16)
    e2t = b.sb("e2t", (16, E))
    b.dma("sync", e2t[:], e2_in)
    b.copy("gpsimd", e2b[:], e2t[:])
    b.push()
    nwT = [b.sb(f"nwTall{k}", (128, 45)) for k in range(NE)]
    for k in range(NE):
        for fi, pfx in enumerate("qkv"):
            b.dma("sync", nwT[k][:, 15 * fi:15 * (fi + 1)],
                  wins[pfx]["nwT"][128 * k:128 * (k + 1), :])
    nb_all = b.sb("nb_all", (1, 45))
    for fi, pfx in enumerate("qkv"):
        b.dma("sync", nb_all[0:1, 15 * fi:15 * (fi + 1)], wins[pfx]["nb"])
    xnTf = [[b.sb(f"xnTf{k}", (128, 512)) for k in range(NE)],
            [b.sb(f"xnTfb{k}", (128, 512)) for k in range(NE)]]
    ps_trf = b.ps("ps_trf", (128, 512))
    ps_sc = b.ps("ps_sc", (128, 512))
    sc_ev = b.sb("sc_ev", (45, 512))

    def fp32_sink(t, xnf):
        n = t // 4
        j = t % 4
        w = (t // 4) % 2
        for i in range(NE):
            slot = i % 4
            b.trans(ps_trf[:, 128 * slot:128 * (slot + 1)],
                    xnf[:, 128 * i:128 * (i + 1)], consts["identf"][:])
            b.copy("scalar", xnTf[w][i][:, 128 * j:128 * (j + 1)],
                   ps_trf[:, 128 * slot:128 * (slot + 1)])
        if j == 3:
            for k in range(NE):
                b.mm(ps_sc[0:45, 0:512], nwT[k][:], xnTf[w][k][:],
                     start=(k == 0), stop=False)
            b.mm(ps_sc[0:45, 0:512], nb_all[:], onesrow_f[:],
                 start=False, stop=True)
            b.copy("scalar", sc_ev[:], ps_sc[0:45, 0:512])
            for j2 in range(4):
                tt = 4 * n + j2
                b.trans(ps_trf[0:128, 0:45], sc_ev[:, 128 * j2:128 * (j2 + 1)],
                        consts["identf"][0:45, 0:45])
                b.copy("scalar", S_tok45[:, 45 * tt:45 * (tt + 1)],
                       ps_trf[0:128, 0:45])
    emit_ln(b, x_in, xn, consts, fp32_sink=fp32_sink)

    lf_dram = b.dram("lf_st", (3, 1, T), F32)
    g = [b.sb(f"rt_g{i}", (128, 32)) for i in range(4)]
    gi = [b.sb(f"rt_gi{i}", (128, 32), I32) for i in range(3)]
    sel1 = b.sb("rt_s1", (128, 32))
    sel2 = b.sb("rt_s2", (128, 32))
    sel3 = b.sb("rt_s3", (128, 32))
    tmpa = b.sb("rt_a", (128, 32))
    tmpb = b.sb("rt_b", (128, 32))
    leaf = b.sb("rt_leaf", (128, 32))
    lf_t = b.sb("rt_lft", (32, 128))
    lfrow = b.sb("rt_lfrow", (1, T))
    lfrow_b = b.sb("rt_lfrowb", (1, T), BF16)
    ps_rt = b.ps("ps_rt", (128, 512))

    def sel(out, cond_i32, on_true, on_false):
        b.copy("vector", out, on_false)
        b.cpred(out, cond_i32, on_true)

    for fi, pfx in enumerate("qkv"):
        sc = lambda i: S_tok45[:, 15 * fi + i::45]
        b.ts("vector", g[0][:], sc(0), 0.0, op0=ALU.is_ge)
        b.ts("vector", gi[0][:], sc(0), 0.0, op0=ALU.is_ge)
        sel(sel1[:], gi[0][:], sc(2), sc(1))
        b.ts("vector", g[1][:], sel1[:], 0.0, op0=ALU.is_ge)
        b.ts("vector", gi[1][:], sel1[:], 0.0, op0=ALU.is_ge)
        sel(tmpa[:], gi[0][:], sc(6), sc(4))
        sel(tmpb[:], gi[0][:], sc(5), sc(3))
        sel(sel2[:], gi[1][:], tmpa[:], tmpb[:])
        b.ts("vector", g[2][:], sel2[:], 0.0, op0=ALU.is_ge)
        b.ts("vector", gi[2][:], sel2[:], 0.0, op0=ALU.is_ge)
        sel(tmpa[:], gi[2][:], sc(8), sc(7))
        sel(tmpb[:], gi[2][:], sc(10), sc(9))
        sel(sel3[:], gi[1][:], tmpb[:], tmpa[:])
        sel(tmpa[:], gi[2][:], sc(12), sc(11))
        sel(tmpb[:], gi[2][:], sc(14), sc(13))
        sel(sel1[:], gi[1][:], tmpb[:], tmpa[:])
        sel(sel2[:], gi[0][:], sel1[:], sel3[:])
        b.ts("vector", g[3][:], sel2[:], 0.0, op0=ALU.is_ge)
        b.stt("vector", leaf[:], g[0][:], 2.0, g[1][:], ALU.mult, ALU.add)
        b.stt("vector", leaf[:], leaf[:], 2.0, g[2][:], ALU.mult, ALU.add)
        b.stt("vector", leaf[:], leaf[:], 2.0, g[3][:], ALU.mult, ALU.add)
        b.trans(ps_rt[0:32, 0:128], leaf[:], consts["identf"][:])
        b.copy("scalar", lf_t[:], ps_rt[0:32, 0:128])
        b.dma("scalar", lf_dram[fi].rearrange("o (a b) -> (o a) b", a=32), lf_t[:])
        b.dma("sync", lfrow[:], lf_dram[fi])
        b.copy("vector", lfrow_b[:], lfrow[:])
        for n in range(8):
            b.mm(ps_rt[0:16, 0:512], consts["ones1"][0:1, 0:16],
                 lfrow_b[0:1, 512 * n:512 * (n + 1)])
            b.ts("vector", onehotT[pfx][:, 512 * n:512 * (n + 1)],
                 ps_rt[0:16, 0:512], iota16[:, 0:1], op0=ALU.is_equal)
    b.pop()

    hb = {}
    for pfx in "qkv":
        hb[pfx] = {}
        bb = hb[pfx]
        bb["b1hi"] = b.sb(f"b1hi_{pfx}", (1, E), BF16)
        bb["b1lo"] = b.sb(f"b1lo_{pfx}", (1, E), BF16)
        bb["b2hi"] = b.sb(f"b2hi_{pfx}", (16, E), BF16)
        bb["b2lo"] = b.sb(f"b2lo_{pfx}", (16, E), BF16)
    b.push()
    btmp = b.sb("btmp", (16, E))
    btmp2 = b.sb("btmp2", (16, E))
    for pfx in "qkv":
        bb = hb[pfx]
        b.dma("sync", btmp[0:1, :], wins[pfx]["b1"])
        b.copy("vector", bb["b1hi"][:], btmp[0:1, :])
        b.tt("vector", btmp2[0:1, :], btmp[0:1, :], bb["b1hi"][:], ALU.subtract)
        b.copy("vector", bb["b1lo"][:], btmp2[0:1, :])
        b.dma("sync", btmp[:], wins[pfx]["b2"])
        b.copy("vector", bb["b2hi"][:], btmp[:])
        b.tt("vector", btmp2[:], btmp[:], bb["b2hi"][:], ALU.subtract)
        b.copy("vector", bb["b2lo"][:], btmp2[:])
    b.pop()

    ps_p = [b.ps(f"ps_p{i}", (128, 512)) for i in range(NE)]
    w1set = [[b.sb(f"w1a{k}", (128, E), BF16) for k in range(NE)],
             [b.sb(f"w1b{k}", (128, E), BF16) for k in range(NE)]]
    w2 = [b.sb(f"w2s{k}", (128, E), BF16) for k in range(NE)]
    hev = [b.sb(f"hev{i}", (128, 512), BF16) for i in range(4)]
    hch = [[b.sb(f"hch{k}", (128, 512), BF16) for k in range(NE)],
           [b.sb(f"hchb{k}", (128, 512), BF16) for k in range(NE)]]
    pev = [b.sb("pev", (128, 512), BF16), b.sb("pevb", (128, 512), BF16)]
    for fi, pfx in enumerate("qkv"):
        w1 = w1set[fi % 2]
        for k in range(NE):
            b.dma("gpsimd", w1[k][:], wins[pfx]["w1T"][128 * k:128 * (k + 1), :])
        for k in range(NE):
            b.dma("gpsimd", w2[k][:], wins[pfx]["w2"][128 * k:128 * (k + 1), :])
        for n in range(8):
            for i in range(NE):
                w = (n * NE + i) % 4
                for k in range(NE):
                    b.mm(ps_p[w][:], w1[k][:, 128 * i:128 * (i + 1)],
                         xn[k][:, 512 * n:512 * (n + 1)], start=(k == 0), stop=False)
                b.mm(ps_p[w][:], hb[pfx]["b1hi"][0:1, 128 * i:128 * (i + 1)], onesrow_b[:],
                     start=False, stop=False)
                b.mm(ps_p[w][:], e2b[:, 128 * i:128 * (i + 1)],
                     onehotT[pfx][:, 512 * n:512 * (n + 1)], start=False, stop=True)
                b.act("scalar", hev[w][:], ps_p[w][:], AF.Relu)
                b.dma("scalar", hid_dram[i, :, 512 * n:512 * (n + 1)], hev[w][:])
        for n in range(8):
            w = n % 2
            for k in range(NE):
                b.dma("sync", hch[w][k][:], hid_dram[k, :, 512 * n:512 * (n + 1)])
            for i in range(NE):
                for k in range(NE):
                    b.mm(ps_p[i][:], w2[k][:, 128 * i:128 * (i + 1)], hch[w][k][:],
                         start=(k == 0), stop=False)
                b.mm(ps_p[i][:], hb[pfx]["b2hi"][:, 128 * i:128 * (i + 1)],
                     onehotT[pfx][:, 512 * n:512 * (n + 1)], start=False, stop=True)
            for i in range(NE):
                if pfx == "q":
                    b.smul(pev[w][:], ps_p[i][:], 0.03125)
                    b.dma("scalar", qT_dram[i, :, 512 * n:512 * (n + 1)], pev[w][:])
                elif pfx == "k":
                    b.copy("scalar", pev[w][:], ps_p[i][:])
                    b.dma("scalar", kT_dram[i, :, 512 * n:512 * (n + 1)], pev[w][:])
                    b.copy("vector", kTs[i][:, 16 * n:16 * (n + 1)], pev[w][:, 0:512:32])
                else:
                    b.copy("scalar", pev[w][:], ps_p[i][:])
                    b.dma("scalar", vT_dram[i, :, 512 * n:512 * (n + 1)], pev[w][:])

    b.pop()

    b.push()
    v_tok = b.sb("v_tok", (128, NT * E), BF16)
    b.push()
    vch = b.sb("vch", (128, T), BF16)
    ps_tv2 = b.ps("ps_tv2", (128, 128), BF16)
    for i in range(NE):
        b.dma("sync", vch[:], vT_dram[i])
        for tt in range(NT):
            b.trans(ps_tv2[:], vch[:, 128 * tt:128 * (tt + 1)], consts["identb"][:])
            b.copy("scalar",
                   v_tok[:, 1024 * tt + 128 * i:1024 * tt + 128 * (i + 1)],
                   ps_tv2[:])
    b.pop()

    qTg2 = [[b.sb(f"qTg{k}", (128, 512), BF16) for k in range(NE)],
            [b.sb(f"qTgb{k}", (128, 512), BF16) for k in range(NE)]]
    kTg = [b.sb(f"kTg{k}", (128, T), BF16) for k in range(NE)]
    mcol = b.sb("at_mcol", (128, 128))
    negm = b.sb("at_negm", (1, 512), BF16)
    PT_half2 = [b.sb("at_pthalf", (128, 512 * 16), BF16),
                b.sb("at_pthalf2", (128, 512 * 16), BF16)]
    ps_bigA = b.ps("at_psbigA", (128, 1024))
    ps_bigB = b.ps("at_psbigB", (128, 1024))
    ps_ctx = b.ps("at_psctx", (128, 1280))
    ps_sm = b.ps("at_pssm", (128, 512))
    rr = b.sb("at_rr", (128, 1))
    ctx_acc = b.sb("at_ctxacc", (128, 4100), BF16)
    x_t32 = [b.sb("at_xt", (128, E)), b.sb("at_xt2", (128, E))]
    for k in range(NE):
        b.dma("sync", kTg[k][:], kT_dram[k])
    for qg in range(8):
        qTg = qTg2[qg % 2]
        for k in range(NE):
            b.dma("sync", qTg[k][:], qT_dram[k, :, 512 * qg:512 * (qg + 1)])
        for tq in range(2):
            t_glob = 4 * qg + tq
            b.dma("gpsimd", x_t32[tq % 2][:], x_in[128 * t_glob:128 * (t_glob + 1), :])
        for tq in range(4):
            for k in range(NE):
                b.mm(ps_sm[:, 128 * tq:128 * (tq + 1)],
                     qTg[k][:, 128 * tq:128 * (tq + 1)], kTs[k][:],
                     start=(k == 0), stop=(k == NE - 1))
            b.red("vector", mcol[:, 32 * tq:32 * tq + 1],
                  ps_sm[:, 128 * tq:128 * (tq + 1)], op=ALU.max, neg=True)
        b.trans(ps_ctx[:, 1152:1280], mcol[:], consts["identf"][:])
        for tq in range(4):
            b.copy("scalar", negm[0:1, 128 * tq:128 * (tq + 1)],
                   ps_ctx[32 * tq:32 * tq + 1, 1152:1280])
        for half in range(2):
            kt0 = 16 * half
            PT_half = PT_half2[half]

            def emit_st(gi):
                ps = ps_bigA if gi % 2 == 0 else ps_bigB
                for j in range(2):
                    ktj = kt0 + 2 * gi + j
                    for k in range(NE):
                        b.mm(ps[:, 512 * j:512 * (j + 1)],
                             kTg[k][:, 128 * ktj:128 * (ktj + 1)], qTg[k][:],
                             start=(k == 0), stop=False)
                    b.mm(ps[:, 512 * j:512 * (j + 1)], onesrow_b[0:1, 0:128],
                         negm[:], start=False, stop=True)

            def emit_exp(gi):
                ps = ps_bigA if gi % 2 == 0 else ps_bigB
                b.act("scalar", PT_half[:, 1024 * gi:1024 * (gi + 1)],
                      ps[:, 0:1024], AF.Exp)

            emit_st(0)
            for gi in range(8):
                if gi + 1 < 8:
                    emit_st(gi + 1)
                emit_exp(gi)
            for tq in range(4):
                for kt in range(kt0, kt0 + 16):
                    lhsT = PT_half[:, 512 * (kt - kt0) + 128 * tq:
                                   512 * (kt - kt0) + 128 * (tq + 1)]
                    va = v_tok[:, 1024 * kt:1024 * (kt + 1)]
                    b.mm(ps_ctx[:, 0:512], lhsT, va[:, 0:512],
                         start=(kt == kt0), stop=(kt == kt0 + 15))
                    b.mm(ps_ctx[:, 512:1024], lhsT, va[:, 512:1024],
                         start=(kt == kt0), stop=(kt == kt0 + 15))
                    b.mm(ps_ctx[:, 1024:1025], lhsT, onescol_b[:, 0:1],
                         start=(kt == kt0), stop=(kt == kt0 + 15))
                if half == 0:
                    b.copy("vector", ctx_acc[:, 1025 * tq:1025 * (tq + 1)],
                           ps_ctx[:, 0:1025])
                else:
                    b.tt("vector", ctx_acc[:, 1025 * tq:1025 * (tq + 1)],
                         ctx_acc[:, 1025 * tq:1025 * (tq + 1)], ps_ctx[:, 0:1025],
                         ALU.add)
        for tq in range(4):
            t_glob = 4 * qg + tq
            xb = x_t32[tq % 2]
            b.recip(rr[:], ctx_acc[:, 1025 * tq + 1024:1025 * tq + 1025])
            b.stt("vector", xb[:], ctx_acc[:, 1025 * tq:1025 * tq + 1024],
                  rr[:, 0:1], xb[:], ALU.mult, ALU.add)
            b.dma("scalar", x_out[128 * t_glob:128 * (t_glob + 1), :], xb[:])
            if tq + 2 < 4:
                b.dma("gpsimd", x_t32[tq % 2][:],
                      x_in[128 * (t_glob + 2):128 * (t_glob + 3), :])
    b.pop()
    if standalone:
        return b.done()
    return None


def l0_ffn_inputs(x_core, inputs):
    BIG = 4096.0
    ind = (np.arange(16)[:, None] == (np.arange(E) // 64)[None, :])
    e2 = (BIG * (ind.astype(np.float32) - 1.0)).astype(np.float32)
    d = {"x": np.ascontiguousarray(x_core),
         "iota16": np.arange(16, dtype=np.float32).reshape(16, 1),
         "e2": e2, **host_consts(rb=False)}
    for pfx in "qkv":
        d[f"{pfx}_w1T"] = np.ascontiguousarray(
            inputs[f"L0_ffn_{pfx}_w1"].transpose(1, 0, 2).reshape(E, E))
        d[f"{pfx}_w2"] = np.ascontiguousarray(
            inputs[f"L0_ffn_{pfx}_w2"].reshape(E, E))
        d[f"{pfx}_nwT"] = np.ascontiguousarray(inputs[f"L0_ffn_{pfx}_node_w"].T)
        d[f"{pfx}_nb"] = np.ascontiguousarray(
            inputs[f"L0_ffn_{pfx}_node_b"].reshape(1, 15))
        d[f"{pfx}_b1"] = np.ascontiguousarray(
            inputs[f"L0_ffn_{pfx}_b1"].reshape(1, E))
        d[f"{pfx}_b2"] = np.ascontiguousarray(inputs[f"L0_ffn_{pfx}_b2"])
    return d




def build_l1_attn(b=None, consts=None, x_in=None, x_out=None):
    standalone = b is None
    if standalone:
        b = B("l1a")
        x_in = b.inp("x", (T, E))
        x_out = b.outp("y", (T, E))
    wattn_in = b.inp("wattn", (E, 3 * E))
    battn_in = b.inp("battn", (1, 3 * E))
    wproj_in = b.inp("wproj", (E, E))
    bproj_in = b.inp("bproj", (1, E))
    cosf_in = b.inp("cosf", (128, 512))
    sinf_in = b.inp("sinf", (128, 512))
    qh_dram = b.dram("qh_st", (16, 64, T), BF16)
    kh_dram = b.dram("kh_st", (16, 64, T), BF16)
    v_dram = b.dram("v_st", (T, E), BF16)

    if consts is None:
        consts = emit_consts(b)
    onesrow_b = b.sb("onesrow_b", (1, 512), BF16)
    b.memset("vector", onesrow_b[:], 1.0)

    b.push()
    xn = alloc_xnT(b)
    b.push()
    emit_ln(b, x_in, xn, consts)
    b.pop()

    b.push()
    brot_hi = b.sb("brot_hi", (1, 3 * E), BF16)
    brot_lo = b.sb("brot_lo", (1, 3 * E), BF16)
    cosf = b.sb("cosf_s", (128, 512))
    sinf = b.sb("sinf_s", (128, 512))
    b.dma("sync", cosf[:], cosf_in)
    b.dma("sync", sinf[:], sinf_in)
    wb = [b.sb(f"wb{k}", (128, 3 * E), BF16) for k in range(NE)]
    wld = b.sb("wld", (128, 3 * E))
    t1 = b.sb("rot_t1", (128, 512))
    t2 = b.sb("rot_t2", (128, 512))
    t3 = b.sb("rot_t3", (128, 512))
    t4 = b.sb("rot_t4", (128, 512))
    for k in range(NE):
        b.dma("sync", wld[:], wattn_in[128 * k:128 * (k + 1), :])
        for half in range(2):
            o0 = 1024 * half
            ev = wld[:, o0:o0 + 1024:2]
            od = wld[:, o0 + 1:o0 + 1024:2]
            b.tt("vector", t1[:], ev, cosf[:], ALU.mult)
            b.tt("vector", t2[:], od, sinf[:], ALU.mult)
            b.tt("vector", t3[:], ev, sinf[:], ALU.mult)
            b.tt("vector", t4[:], od, cosf[:], ALU.mult)
            b.tt("vector", wb[k][:, o0:o0 + 1024:2], t1[:], t2[:], ALU.subtract)
            b.tt("vector", wb[k][:, o0 + 1:o0 + 1024:2], t3[:], t4[:], ALU.add)
        b.copy("gpsimd", wb[k][:, 2048:3072], wld[:, 2048:3072])
    bld = b.sb("bld", (1, 3 * E))
    brot = b.sb("brot", (1, 3 * E))
    blo = b.sb("blo", (1, 3 * E))
    b.dma("sync", bld[:], battn_in)
    for half in range(2):
        o0 = 1024 * half
        ev = bld[0:1, o0:o0 + 1024:2]
        od = bld[0:1, o0 + 1:o0 + 1024:2]
        b.tt("vector", t1[0:1, :], ev, cosf[0:1, :], ALU.mult)
        b.tt("vector", t2[0:1, :], od, sinf[0:1, :], ALU.mult)
        b.tt("vector", t3[0:1, :], ev, sinf[0:1, :], ALU.mult)
        b.tt("vector", t4[0:1, :], od, cosf[0:1, :], ALU.mult)
        b.tt("vector", brot[0:1, o0:o0 + 1024:2], t1[0:1, :], t2[0:1, :], ALU.subtract)
        b.tt("vector", brot[0:1, o0 + 1:o0 + 1024:2], t3[0:1, :], t4[0:1, :], ALU.add)
    b.copy("vector", brot[0:1, 2048:3072], bld[0:1, 2048:3072])
    b.copy("vector", brot_hi[:], brot[:])
    b.tt("vector", blo[:], brot[:], brot_hi[:], ALU.subtract)
    b.copy("vector", brot_lo[:], blo[:])

    ps_qk = [b.ps(f"ps_qk{i}", (128, 512)) for i in range(3)]
    qev = [b.sb("qev", (128, 512), BF16), b.sb("qev2", (128, 512), BF16),
           b.sb("qev3", (128, 512), BF16)]
    rr_i = [0]
    for ot in range(16):
        for n in range(8):
            w = rr_i[0] % 3
            rr_i[0] += 1
            for k in range(NE):
                b.mm(ps_qk[w][:], wb[k][:, 128 * ot:128 * (ot + 1)],
                     xn[k][:, 512 * n:512 * (n + 1)], start=(k == 0), stop=False)
            b.mm(ps_qk[w][:], brot_hi[0:1, 128 * ot:128 * (ot + 1)], onesrow_b[:],
                 start=False, stop=True)
            if ot < 8:
                b.smul(qev[w][:], ps_qk[w][:], 0.125)
                b.dma("scalar", qh_dram[2 * ot, :, 512 * n:512 * (n + 1)], qev[w][0:64, :])
                b.dma("scalar", qh_dram[2 * ot + 1, :, 512 * n:512 * (n + 1)], qev[w][64:128, :])
            else:
                b.copy("scalar", qev[w][:], ps_qk[w][:])
                h0 = 2 * (ot - 8)
                b.dma("scalar", kh_dram[h0, :, 512 * n:512 * (n + 1)], qev[w][0:64, :])
                b.dma("scalar", kh_dram[h0 + 1, :, 512 * n:512 * (n + 1)], qev[w][64:128, :])
    vev = [b.sb("vev", (128, 512), BF16), b.sb("vevb", (128, 512), BF16),
           b.sb("vevc", (128, 512), BF16)]
    for t in range(NT):
        for c in range(2):
            w = rr_i[0] % 3
            rr_i[0] += 1
            for k in range(NE):
                b.mm(ps_qk[w][:], xn[k][:, 128 * t:128 * (t + 1)],
                     wb[k][:, 2048 + 512 * c:2048 + 512 * (c + 1)],
                     start=(k == 0), stop=False)
            b.mm(ps_qk[w][:], consts["ones1"][0:1, 0:128],
                 brot_hi[0:1, 2048 + 512 * c:2048 + 512 * (c + 1)],
                 start=False, stop=True)
            b.copy("scalar", vev[w][:], ps_qk[w][:])
            b.dma("scalar", v_dram[128 * t:128 * (t + 1), 512 * c:512 * (c + 1)], vev[w][:])
    b.pop()
    b.pop()

    b.push()
    ctxn = [b.sb(f"ctxn{p}", (128, T), BF16) for p in range(8)]
    b.push()
    v_aug = b.sb("v_aug", (128, NT * 1040), BF16)
    vaug_v = v_aug[:].rearrange("p (a c) -> p a c", c=65)
    b.memset("vector", vaug_v[:, :, 64:65], 1.0)
    for kt in range(NT):
        dst = v_aug[:, 1040 * kt:1040 * (kt + 1)].rearrange(
            "p (h c) -> p h c", c=65)[:, :, 0:64]
        b.dma("sync", dst, v_dram[128 * kt:128 * (kt + 1), :].rearrange(
            "p (h d) -> p h d", d=64))
    ql_buf = [b.sb(f"ql_buf{i}", (65, T), BF16) for i in range(4)]
    kl_buf = [b.sb(f"kl_buf{i}", (65, T), BF16) for i in range(4)]
    for i in range(4):
        b.memset("vector", kl_buf[i][64:65, :], 1.0)
    psR = [b.ps("psR0", (128, 1024)), b.ps("psR1", (128, 1024)),
           b.ps("psR2", (128, 1024))]
    ps_ctx2 = [b.ps("ps_ctx", (128, 512)), b.ps("ps_ctxB", (128, 512))]
    fbufs = flash_bufs(b)
    loaded = set()

    def load_h(h):
        if h in loaded or h > 15:
            return
        loaded.add(h)
        b.dma("sync", ql_buf[h % 4][0:64, :], qh_dram[h])
        b.dma("sync", kl_buf[h % 4][0:64, :], kh_dram[h])

    def ql_of(h):
        load_h(h)
        return ql_buf[h % 4]

    def latKT_of(h):
        return kl_buf[h % 4]

    def latV_of(h, kt):
        return v_aug[:, 1040 * kt + 65 * h:1040 * kt + 65 * (h + 1)]

    def sample_of(h, gt):
        buf = ql_buf[h % 4]
        kb = kl_buf[h % 4]
        stride = max(1, gt)
        lhsT = buf[0:64, 128 * gt:128 * (gt + 1)]
        rhs = kb[0:64, 0:128 * stride:stride] if gt > 0 else kb[0:64, 0:128]
        return lhsT, rhs, gt == 0

    def ctx_sink(h, qg, ctx_rows, rsb):
        dst = ctxn[h // 2][64 * (h % 2):64 * (h % 2) + 64, 512 * qg:512 * (qg + 1)]
        b.tt("vector", dst, ctx_rows, rsb[:], ALU.mult)

    for p in range(8):
        flash_kmajor(b, consts, fbufs, [2 * p, 2 * p + 1], ql_of, latKT_of,
                     latV_of, lambda qg: 4 * (qg + 1), ctx_sink, sample_of,
                     psR, ps_ctx2, 1.0,
                     qg_filler=(lambda p=p: (load_h(2 * p + 2), load_h(2 * p + 3))))
    b.pop()

    b.push()
    wp = [b.sb(f"wp{k}", (128, E), BF16) for k in range(NE)]
    wldp = b.sb("wldp", (128, E))
    for k in range(NE):
        load_cast(b, wp[k], wproj_in[128 * k:128 * (k + 1), :], wldp)
    bp_hi = b.sb("bp_hi", (1, E), BF16)
    bp_lo = b.sb("bp_lo", (1, E), BF16)
    bpl = b.sb("bpl", (1, E))
    bpl2 = b.sb("bpl2", (1, E))
    b.dma("sync", bpl[:], bproj_in)
    b.copy("vector", bp_hi[:], bpl[:])
    b.tt("vector", bpl2[:], bpl[:], bp_hi[:], ALU.subtract)
    b.copy("vector", bp_lo[:], bpl2[:])
    x_t2 = [b.sb("x_t2", (128, E)), b.sb("x_t2b", (128, E))]
    ps_o = [b.ps(f"ps_o{i}", (128, 512)) for i in range(4)]
    for t in range(NT):
        w = t % 2
        b.dma("sync", x_t2[w][:], x_in[128 * t:128 * (t + 1), :])
        for c in range(2):
            pso = ps_o[2 * w + c]
            for k in range(NE):
                b.mm(pso[:], ctxn[k][:, 128 * t:128 * (t + 1)],
                     wp[k][:, 512 * c:512 * (c + 1)], start=(k == 0), stop=False)
            b.mm(pso[:], consts["ones1"][0:1, 0:128],
                 bp_hi[0:1, 512 * c:512 * (c + 1)], start=False, stop=True)
            b.tt("vector", x_t2[w][:, 512 * c:512 * (c + 1)],
                 x_t2[w][:, 512 * c:512 * (c + 1)], pso[:], ALU.add)
        b.dma("scalar", x_out[128 * t:128 * (t + 1), :], x_t2[w][:])
    b.pop()
    if standalone:
        return b.done()
    return None


def l1_attn_inputs(x_core, inputs):
    H, hd = 16, 64
    theta = 1.0 / 10000.0 ** (np.arange(0, hd, 2, dtype=np.float64) / hd)
    ang = (np.arange(H, dtype=np.float64)[:, None] * theta[None, :])
    cos = np.cos(ang).astype(np.float32).reshape(1, 512)
    sin = np.sin(ang).astype(np.float32).reshape(1, 512)
    return {"x": np.ascontiguousarray(x_core),
            "wattn": np.ascontiguousarray(inputs["L1_attn_W_attn"]),
            "battn": np.ascontiguousarray(inputs["L1_attn_b_attn"].reshape(1, -1)),
            "wproj": np.ascontiguousarray(inputs["L1_attn_W_proj"]),
            "bproj": np.ascontiguousarray(inputs["L1_attn_b_proj"].reshape(1, -1)),
            "cosf": np.broadcast_to(cos, (128, 512)).copy(),
            "sinf": np.broadcast_to(sin, (128, 512)).copy(),
            **host_consts()}




def emit_quant(b, dst_bf, src, nblk, qb):
    w = 16 * nblk
    am, sc, rsc = qb["am"], qb["sc"], qb["rsc"]
    xn, sq, ca, cb, cb2 = (qb["xn"], qb["sq"], qb["ca"], qb["cb"], qb["cb2"])
    blk = lambda ap: ap.rearrange("p (a c) -> p a c", c=16)
    b.red("vector", am[:, 0:nblk].unsqueeze(2), blk(src), op=ALU.max, absval=True)
    b.ts("vector", sc[:, 0:nblk], am[:, 0:nblk], 1.0 / 6.0, 1e-12, op0=ALU.mult, op1=ALU.max)
    b.recip(rsc[:, 0:nblk], sc[:, 0:nblk])
    b.act("scalar", cb[:, 0:w], src, AF.Sign)
    b.tt("vector", blk(xn[:, 0:w]), blk(src),
         rsc[:, 0:nblk].unsqueeze(2).broadcast_to((128, nblk, 16)), ALU.mult)
    b.tt("vector", sq[:, 0:w], xn[:, 0:w], xn[:, 0:w], ALU.mult)
    THR = [(0.0625, 0.5), (0.5625, 0.5), (1.5625, 0.5), (3.0625, 0.5),
           (6.25, 1.0), (12.25, 1.0), (25.0, 2.0)]
    t0_, w0_ = THR[0]
    b.ts("vector", ca[:, 0:w], sq[:, 0:w], t0_, w0_, op0=ALU.is_gt, op1=ALU.mult)
    for ti, wi in THR[1:]:
        b.ts("vector", cb2[:, 0:w], sq[:, 0:w], ti, wi, op0=ALU.is_gt, op1=ALU.mult)
        b.tt("vector", ca[:, 0:w], ca[:, 0:w], cb2[:, 0:w], ALU.add)
    b.tt("vector", ca[:, 0:w], ca[:, 0:w], cb[:, 0:w], ALU.mult)
    b.tt("vector", blk(dst_bf), blk(ca[:, 0:w]),
         sc[:, 0:nblk].unsqueeze(2).broadcast_to((128, nblk, 16)), ALU.mult)


def quant_bufs(b, maxw=2816):
    nb = maxw // 16
    return {"am": b.sb("q_am", (128, nb)), "sc": b.sb("q_sc", (128, nb)),
            "rsc": b.sb("q_rsc", (128, nb)), "xn": b.sb("q_xn", (128, maxw), BF16),
            "sq": b.sb("q_sq", (128, maxw), BF16), "ca": b.sb("q_ca", (128, maxw), BF16),
            "cb": b.sb("q_cb", (128, maxw), BF16),
            "cb2": b.sb("q_cb2", (128, maxw), BF16)}


def build_l1_ffn(b=None, consts=None, x_in=None, x_out=None):
    standalone = b is None
    I = 2816
    NI = I // 128
    if standalone:
        b = B("l1f")
        x_in = b.inp("x", (T, E))
        x_out = b.outp("y", (T, E))
    wg_in = b.inp("wg", (I, E))
    wu_in = b.inp("wu", (I, E))
    wd_in = b.inp("wd", (E, I))
    xqT_dram = b.dram("xqT_st", (NT, 128, E), BF16)
    hqT_dram = b.dram("hqT_st", (NT, 128, I), BF16)
    wdqT_dram = b.dram("wdqT_st", (NI, 128, E), BF16)

    if consts is None:
        consts = emit_consts(b, rb=False)
    qb = quant_bufs(b)
    wgqT = [b.sb(f"wgqT{k}", (128, I), BF16) for k in range(NE)]
    wuqT = [b.sb(f"wuqT{k}", (128, I), BF16) for k in range(NE)]

    b.push()
    qb2 = [quant_bufs(b, maxw=E), quant_bufs(b, maxw=E)]
    xq_t = [b.sb("xq_t", (128, E), BF16), b.sb("xq_t2", (128, E), BF16)]
    xq_acc = [b.sb("xq_acc", (128, E), BF16), b.sb("xq_acc2", (128, E), BF16)]
    ps_tr = b.ps("ps_tr", (128, 512), BF16)

    xpost = [None]

    def x_sink(t, xnf):
        s = t % 2
        if xpost[0] is not None:
            xpost[0]()
            xpost[0] = None
        emit_quant(b, xq_t[s][:], xnf[:], E // 16, qb2[s])

        def mk(t=t, s=s):
            def fn():
                for g4 in range(2):
                    for slot in range(4):
                        i = 4 * g4 + slot
                        b.trans(ps_tr[:, 128 * slot:128 * (slot + 1)],
                                xq_t[s][:, 128 * i:128 * (i + 1)], consts["identb"][:])
                    b.copy("scalar", xq_acc[s][:, 512 * g4:512 * (g4 + 1)], ps_tr[:, 0:512])
                b.dma("scalar", xqT_dram[t], xq_acc[s][:])
            return fn
        xpost[0] = mk()
        if t == NT - 1:
            xpost[0]()
            xpost[0] = None
    emit_ln(b, x_in, None, consts, fp32_sink=x_sink)
    b.pop()

    b.push()
    qbw = [quant_bufs(b), quant_bufs(b)]
    wld = [b.sb("wld", (128, I)), b.sb("wld2", (128, I))]
    wqt = [qbw[0]["xn"], qbw[1]["xn"]]
    ps_trw = b.ps("ps_trw", (128, 512), BF16)
    for wi, (wsrc, wdstT) in enumerate(((wg_in, wgqT), (wu_in, wuqT))):
        for j in range(NI):
            s = j % 2
            b.dma("sync", wld[s][:, 0:E], wsrc[128 * j:128 * (j + 1), :])
            emit_quant(b, wqt[s][:, 0:E], wld[s][:, 0:E], E // 16, qbw[s])
            for k in range(NE):
                slot = k % 4
                b.trans(ps_trw[:, 128 * slot:128 * (slot + 1)],
                        wqt[s][:, 128 * k:128 * (k + 1)], consts["identb"][:])
                b.copy("scalar", wdstT[k][:, 128 * j:128 * (j + 1)],
                       ps_trw[:, 128 * slot:128 * (slot + 1)])
    _wa = b.sb("wdq_acc", (128, I), BF16)
    wdq_acc = [_wa, _wa]
    for j in range(NE):
        s = j % 2
        b.dma("sync", wld[s][:], wd_in[128 * j:128 * (j + 1), :])
        emit_quant(b, wqt[s][:], wld[s][:], I // 16, qbw[s])
        for kk in range(NI):
            slot = kk % 4
            b.trans(ps_trw[:, 128 * slot:128 * (slot + 1)],
                    wqt[s][:, 128 * kk:128 * (kk + 1)], consts["identb"][:])
            b.copy("scalar", wdq_acc[s][:, 128 * kk:128 * (kk + 1)],
                   ps_trw[:, 128 * slot:128 * (slot + 1)])
        for kk in range(NI):
            b.dma("gpsimd", wdqT_dram[kk, :, 128 * j:128 * (j + 1)],
                  wdq_acc[s][:, 128 * kk:128 * (kk + 1)])
    b.pop()

    b.push()
    CH = [(0, 512), (512, 1024), (1024, 1536), (1536, 2048), (2048, 2560), (2560, 2816)]
    qbc = [quant_bufs(b), quant_bufs(b)]
    xq_l = [b.sb("xq_l", (128, E), BF16), b.sb("xq_l2", (128, E), BF16)]
    ps_g = [b.ps(f"ps_g{i}", (128, 512)) for i in range(3)]
    ps_u = [b.ps(f"ps_u{i}", (128, 512)) for i in range(3)]
    ps_trh = b.ps("ps_trh", (128, 512), BF16)
    hsil = [b.sb("hsil", (128, 512)), b.sb("hsil2", (128, 512))]
    usq = [b.sb("usq", (128, 512)), b.sb("usq2", (128, 512))]
    _ht = b.sb("h_t", (128, I))
    h_t = [_ht, _ht]
    _ha = b.sb("hq_acc", (128, I), BF16)
    hq_acc = [_ha, _ha]
    post = [None, None]
    for t in range(NT):
        s = t % 2
        b.dma("sync", xq_l[s][:], xqT_dram[t])
        for half in range(2):
            chs = CH[3 * half:3 * half + 3]
            for k in range(NE):
                lhsT = xq_l[s][:, 128 * k:128 * (k + 1)]
                for ci, (c0, c1) in enumerate(chs):
                    b.mm(ps_g[ci][:, 0:c1 - c0], lhsT, wgqT[k][:, c0:c1],
                         start=(k == 0), stop=(k == NE - 1))
                    b.mm(ps_u[ci][:, 0:c1 - c0], lhsT, wuqT[k][:, c0:c1],
                         start=(k == 0), stop=(k == NE - 1))
            if half == 0 and post[0] is not None:
                post[0]()
                post[0] = None
            for ci, (c0, c1) in enumerate(chs):
                w = c1 - c0
                b.act("scalar", hsil[ci % 2][:, 0:w], ps_g[ci][:, 0:w], AF.Silu)
                b.copy("scalar", usq[ci % 2][:, 0:w], ps_u[ci][:, 0:w])
                b.tt("gpsimd", h_t[s][:, c0:c1], hsil[ci % 2][:, 0:w], usq[ci % 2][:, 0:w], ALU.mult)
        hq_t = qbc[s]["xn"]
        emit_quant(b, hq_t[:, 0:I], h_t[s][:], I // 16, qbc[s])

        def mkpost(t=t, s=s, hq_t=hq_t):
            def post_fn():
                for kk0 in range(0, NI, 4):
                    n4 = min(4, NI - kk0)
                    for slot in range(n4):
                        b.trans(ps_trh[:, 128 * slot:128 * (slot + 1)],
                                hq_t[:, 128 * (kk0 + slot):128 * (kk0 + slot + 1)],
                                consts["identb"][:])
                    b.copy("scalar", hq_acc[s][:, 128 * kk0:128 * (kk0 + n4)],
                           ps_trh[:, 0:128 * n4])
                b.dma("scalar", hqT_dram[t], hq_acc[s][:])
            return post_fn
        post[0], post[1] = post[1], mkpost()
    for p in post:
        if p is not None:
            p()
    b.pop()

    b.push()
    wdqT = [b.sb(f"wdqT{kk}", (128, E), BF16) for kk in range(NI)]
    for kk in range(NI):
        b.dma("sync", wdqT[kk][:], wdqT_dram[kk])
    hq_l = [b.sb("hq_l", (128, I), BF16), b.sb("hq_l2", (128, I), BF16)]
    x_t2 = [b.sb("x_t2", (128, E)), b.sb("x_t2b", (128, E))]
    ps_o0 = b.ps("ps_o0", (128, 512))
    ps_o1 = b.ps("ps_o1", (128, 512))
    for t in range(NT):
        s = t % 2
        b.dma("sync", hq_l[s][:], hqT_dram[t])
        b.dma("sync", x_t2[s][:], x_in[128 * t:128 * (t + 1), :])
        for kk in range(NI):
            lhsT = hq_l[s][:, 128 * kk:128 * (kk + 1)]
            b.mm(ps_o0[:], lhsT, wdqT[kk][:, 0:512], start=(kk == 0), stop=(kk == NI - 1))
            b.mm(ps_o1[:], lhsT, wdqT[kk][:, 512:1024], start=(kk == 0), stop=(kk == NI - 1))
        b.tt("vector", x_t2[s][:, 0:512], x_t2[s][:, 0:512], ps_o0[:], ALU.add)
        b.tt("vector", x_t2[s][:, 512:1024], x_t2[s][:, 512:1024], ps_o1[:], ALU.add)
        b.dma("scalar", x_out[128 * t:128 * (t + 1), :], x_t2[s][:])
    b.pop()
    if standalone:
        return b.done()
    return None


def l1_ffn_inputs(x_core, inputs):
    return {"x": np.ascontiguousarray(x_core),
            "wg": np.ascontiguousarray(inputs["L1_ffn_w_gate"]),
            "wu": np.ascontiguousarray(inputs["L1_ffn_w_up"]),
            "wd": np.ascontiguousarray(inputs["L1_ffn_w_down"]),
            **host_consts(rb=False)}


def build_module():
    b = B("mod")
    x0 = b.inp("x", (T, E))
    y = b.outp("y", (T, E))
    x1 = b.dram("x1_st", (T, E))
    x2 = b.dram("x2_st", (T, E))
    x3 = b.dram("x3_st", (T, E))
    consts = emit_consts(b, rb=True)
    for fn, xi, xo in ((build_l0_attn, x0, x1), (build_l0_ffn, x1, x2),
                       (build_l1_attn, x2, x3), (build_l1_ffn, x3, y)):
        depth = len(b.scopes)
        b.push()
        fn(b, consts, xi, xo)
        while len(b.scopes) > depth:
            b.pop()
    return b.done()


def module_inputs(x_core, inputs):
    d = {}
    d.update(l0_attn_inputs(x_core, inputs))
    d.update(l0_ffn_inputs(x_core, inputs))
    d.update(l1_attn_inputs(x_core, inputs))
    d.update(l1_ffn_inputs(x_core, inputs))
    d["x"] = np.ascontiguousarray(x_core)
    return d


def kernel(**inputs):
    x = np.ascontiguousarray(inputs["L0_attn_hidden_states"], dtype=np.float32)
    n_cores = 8
    nc = build_module()
    in_maps = [module_inputs(x[c], inputs) for c in range(n_cores)]
    res = run_bass_kernel_spmd(nc, in_maps, core_ids=list(range(n_cores)))
    return np.stack([np.asarray(res.results[c]["y"]) for c in range(n_cores)])


# version 34
# speedup vs baseline: 1.0021x; 1.0021x over previous
import numpy as np
import ml_dtypes
import concourse.bass as bass
import concourse.mybir as mybir
from concourse.bass_utils import run_bass_kernel_spmd
from contextlib import ExitStack

F32 = mybir.dt.float32
BF16 = mybir.dt.bfloat16
I32 = mybir.dt.int32
AF = mybir.ActivationFunctionType
ALU = mybir.AluOpType
AX = mybir.AxisListType

T = 4096
E = 1024
NT = T // 128
NE = E // 128
NEG = -1.0e30

ENGINES = ["sync", "vector", "scalar", "tensor", "gpsimd"]
DMA_ISSUERS = ["sync", "scalar", "gpsimd"]
DMA_SEMS_PER_QUEUE = 6


def _span(ap):
    lo = ap.offset
    hi = ap.offset
    for step, count in ap.ap:
        if count > 1:
            if step >= 0:
                hi += step * (count - 1)
            else:
                lo += step * (count - 1)
    return (ap.tensor.name, lo, hi + 1)


class FW:
    def __init__(self, nc):
        self.nc = nc
        self.prog = {e: [] for e in ENGINES}
        self.n = {e: 0 for e in ENGINES}
        self.marked = {e: set() for e in ENGINES}
        self.dma_sems = {e: [f"d_{e}_{k}" for k in range(DMA_SEMS_PER_QUEUE)]
                         for e in DMA_ISSUERS}
        self.dma_rr = {e: 0 for e in DMA_ISSUERS}
        self.dma_n = {s: 0 for e in DMA_ISSUERS for s in self.dma_sems[e]}
        self.mem = {}
        self.wm = {e: {} for e in ENGINES}
        self.fence_pending = []
        self.tensor_fences = {}
        self._finalized = False

    def _edges_for(self, reads, writes):
        edges = []

        def overlap(lst, lo, hi):
            return [(d, i) for (l, h, d, i) in lst if not (hi <= l or h <= lo)]

        spans_r = [_span(ap) for ap in reads]
        spans_w = [_span(ap) for ap in writes]
        for name, lo, hi in spans_r:
            rec = self.mem.get(name)
            if rec:
                edges += overlap(rec["w"], lo, hi)
        for name, lo, hi in spans_w:
            rec = self.mem.get(name)
            if rec:
                edges += overlap(rec["w"], lo, hi)
                edges += overlap(rec["r"], lo, hi)
        return edges, spans_r, spans_w

    def _record_access(self, spans_r, spans_w, domain, idx):
        for name, lo, hi in spans_r:
            rec = self.mem.setdefault(name, {"w": [], "r": []})
            rec["r"].append((lo, hi, domain, idx))
        for name, lo, hi in spans_w:
            rec = self.mem.setdefault(name, {"w": [], "r": []})
            rec["w"] = [e for e in rec["w"] if not (lo <= e[0] and e[1] <= hi)]
            rec["r"] = [e for e in rec["r"] if not (lo <= e[0] and e[1] <= hi)]
            rec["w"].append((lo, hi, domain, idx))

    def _resolve_edges(self, eng, edges):
        waits = []
        best = {}
        for d, i in edges:
            if d == "tensor" and eng == "tensor":
                continue
            if best.get(d, -1) < i:
                best[d] = i
        for d, i in sorted(best.items()):
            if self.wm[eng].get(d, -1) >= i:
                continue
            self.wm[eng][d] = i
            if d in ENGINES:
                self.marked[d].add(i)
            waits.append((d, i))
        return waits

    def emit(self, eng, emitfn, reads=(), writes=(), is_dma=False):
        assert not self._finalized
        edges, spans_r, spans_w = self._edges_for(reads, writes)
        if self.fence_pending:
            edges = list(edges) + list(self.fence_pending)
        if self.tensor_fences:
            for sp in spans_r + spans_w:
                f = self.tensor_fences.get(sp[0])
                if f:
                    edges = list(edges) + f
        waits = self._resolve_edges(eng, edges)
        if is_dma:
            k = self.dma_rr[eng]
            self.dma_rr[eng] = (k + 1) % DMA_SEMS_PER_QUEUE
            q = self.dma_sems[eng][k]
            if self.dma_n[q] > 0:
                waits += self._resolve_edges(eng, [(q, self.dma_n[q])])
            self.dma_n[q] += 16
            domain, idx = q, self.dma_n[q]
        else:
            domain, idx = eng, self.n[eng]
        self.prog[eng].append((waits, emitfn, {"domain": domain, "idx": idx,
                                               "is_dma": is_dma}))
        if not is_dma:
            self.n[eng] += 1
        self._record_access(spans_r, spans_w, domain, idx)
        return (domain, idx)

    def all_dma_tokens(self):
        return [(q, n) for q, n in self.dma_n.items() if n > 0]

    def fence(self):
        toks = self.fence_tokens()
        self.fence_pending = toks

    def fence_tokens(self):
        toks = [(e, self.n[e] - 1) for e in ENGINES if self.n[e] > 0]
        toks += self.all_dma_tokens()
        return toks

    def finalize_and_register(self, block, tail_wait=None):
        self._finalized = True
        nc = self.nc
        sems = {}
        for e in ENGINES:
            if self.marked[e]:
                sems[e] = nc.alloc_semaphore("sem_" + e)
        for q in self.dma_n:
            if self.dma_n[q]:
                sems[q] = nc.alloc_semaphore("sem_" + q)
        incval = {}
        for e in ENGINES:
            cum = {}
            c = 0
            for i in sorted(self.marked[e]):
                c += 1
                cum[i] = c
            incval[e] = cum

        def wait_value(d, i):
            if d in ENGINES:
                return incval[d][i]
            return i

        def make_section(ename):
            stream = self.prog[ename]
            marked = self.marked[ename]

            def section(engobj):
                for waits, emitfn, meta in stream:
                    for (d, i) in waits:
                        engobj.wait_ge(sems[d], wait_value(d, i))
                    if emitfn is None:
                        continue
                    inst = emitfn(engobj)
                    if meta["is_dma"]:
                        inst.then_inc(sems[meta["domain"]], 16)
                    elif meta["idx"] in marked:
                        inst.then_inc(sems[ename], 1)
                if tail_wait and tail_wait[0] == ename:
                    for (d, i) in tail_wait[1]:
                        engobj.wait_ge(sems[d], wait_value(d, i))
            return section

        for ename in ENGINES:
            if self.prog[ename] or (tail_wait and tail_wait[0] == ename):
                getattr(block, ename)(make_section(ename))


class B:

    def __init__(self, name):
        self.nc = bass.Bass("TRN2", target_bir_lowering=False)
        self.fw = FW(self.nc)
        self.stack = ExitStack()
        self.scopes = [self.stack]
        self.names = set()
        self.n_inst = 0

    def push(self):
        st = ExitStack()
        self.scopes.append(st)

    def pop(self):
        self._lazy_fence = self.fw.fence_tokens()
        self.scopes.pop().close()

    def _nm(self, base):
        i = 0
        nm = base
        while nm in self.names:
            i += 1
            nm = f"{base}_{i}"
        self.names.add(nm)
        return nm

    def inp(self, name, shape, dt=F32):
        return self.nc.dram_tensor(name, list(shape), dt, kind="ExternalInput").ap()

    def outp(self, name, shape, dt=F32):
        return self.nc.dram_tensor(name, list(shape), dt, kind="ExternalOutput").ap()

    def dram(self, name, shape, dt=F32):
        return self.nc.dram_tensor(name, list(shape), dt, kind="Internal").ap()

    def sb(self, name, shape, dt=F32):
        h = self.scopes[-1].enter_context(self.nc.sbuf_tensor(self._nm(name), list(shape), dt))
        if getattr(self, "_lazy_fence", None):
            self.fw.tensor_fences[h.name] = list(self._lazy_fence)
        return h

    def ps(self, name, shape, dt=F32):
        h = self.scopes[-1].enter_context(self.nc.psum_tensor(self._nm(name), list(shape), dt))
        if getattr(self, "_lazy_fence", None):
            self.fw.tensor_fences[h.name] = list(self._lazy_fence)
        return h

    def dma(self, eng, out, in_):
        return self.fw.emit(eng, lambda e: e.dma_start(out=out, in_=in_),
                            reads=[in_], writes=[out], is_dma=True)

    def mm(self, out, lhsT, rhs, start=True, stop=True, tile_position=None):
        return self.fw.emit("tensor",
                            lambda e: e.matmul(out, lhsT, rhs, start=start, stop=stop,
                                               tile_position=tile_position),
                            reads=[lhsT, rhs], writes=[out])

    def trans(self, out_ps, in_, ident):
        return self.fw.emit("tensor", lambda e: e.transpose(out_ps, in_, ident),
                            reads=[in_, ident], writes=[out_ps])

    def act(self, eng, out, in_, func, bias=0.0, scale=1.0, accum=None):
        r = [in_]
        w = [out]
        if isinstance(bias, bass.AP):
            r.append(bias)
        if isinstance(scale, bass.AP):
            r.append(scale)
        if accum is not None:
            w.append(accum)
        return self.fw.emit(eng, lambda e: e.activation(out, in_, func, bias=bias,
                                                        scale=scale, accum_out=accum),
                            reads=r, writes=w)

    def copy(self, eng, out, in_):
        if eng == "scalar":
            return self.fw.emit(eng, lambda e: e.copy(out, in_), reads=[in_], writes=[out])
        return self.fw.emit(eng, lambda e: e.tensor_copy(out, in_), reads=[in_], writes=[out])

    def tt(self, eng, out, in0, in1, op):
        return self.fw.emit(eng, lambda e: e.tensor_tensor(out, in0, in1, op=op),
                            reads=[in0, in1], writes=[out])

    def ts(self, eng, out, in0, s1, s2=None, op0=ALU.mult, op1=ALU.bypass):
        r = [in0]
        if isinstance(s1, bass.AP):
            r.append(s1)
        if isinstance(s2, bass.AP):
            r.append(s2)
        if s2 is None:
            return self.fw.emit(eng, lambda e: e.tensor_single_scalar(out, in0, s1, op=op0),
                                reads=r, writes=[out])
        return self.fw.emit(eng, lambda e: e.tensor_scalar(out, in0, s1, s2, op0=op0, op1=op1),
                            reads=r, writes=[out])

    def stt(self, eng, out, in0, s, in1, op0, op1):
        r = [in0, in1]
        if isinstance(s, bass.AP):
            r.append(s)
        return self.fw.emit(eng, lambda e: e.scalar_tensor_tensor(out, in0, s, in1,
                                                                  op0=op0, op1=op1),
                            reads=r, writes=[out])

    def red(self, eng, out, in_, op=ALU.max, absval=False, neg=False):
        return self.fw.emit(eng, lambda e: e.tensor_reduce(out, in_, axis=AX.X, op=op,
                                                           apply_absolute_value=absval,
                                                           negate=neg),
                            reads=[in_], writes=[out])

    def memset(self, eng, ap, val):
        return self.fw.emit(eng, lambda e: e.memset(ap, val), reads=[], writes=[ap])

    def smul(self, out, in_, c):
        return self.fw.emit("scalar", lambda e, o=out, i=in_: e.mul(o, i, c),
                            reads=[in_], writes=[out])

    def cpred(self, out, mask, data):
        return self.fw.emit("vector",
                            lambda e, o=out, m=mask, d=data: e.copy_predicated(o, m, d),
                            reads=[mask, data], writes=[out])

    def recip(self, out, in_):
        return self.fw.emit("vector", lambda e: e.reciprocal(out, in_),
                            reads=[in_], writes=[out])

    def done(self, tail_eng="sync"):
        while len(self.scopes) > 1:
            self.scopes.pop().close()
        block = self.stack.enter_context(self.nc.Block())
        self.fw.finalize_and_register(block, tail_wait=(tail_eng, self.fw.all_dma_tokens()))
        self.n_inst = sum(len(v) for v in self.fw.prog.values())
        self.stack.close()
        return self.nc


def alloc_xnT(b):
    xn_all = b.sb("xnT_all", (128, NE * T), BF16)
    return [xn_all[:, T * i:T * (i + 1)] for i in range(NE)]


def emit_ln(b, x_dram, xn_tiles, consts, fp32_sink=None, x_tiles_out=None):
    identb = consts["identb"]
    S = []
    for s in range(2):
        d = {}
        d["x_t"] = b.sb("ln_x", (128, E))
        d["sqs"] = b.sb("ln_sq", (128, E), BF16)
        d["ssq"] = b.sb("ln_ssq", (128, 1))
        d["srow"] = b.sb("ln_s", (128, 1))
        d["m"] = b.sb("ln_m", (128, 1))
        d["msq"] = b.sb("ln_msq", (128, 1))
        d["var"] = b.sb("ln_var", (128, 1))
        d["sd"] = b.sb("ln_sd", (128, 1))
        d["rs"] = b.sb("ln_rs", (128, 1))
        d["bl"] = b.sb("ln_bl", (128, 1))
        if xn_tiles is not None:
            d["xn"] = b.sb("ln_xn", (128, E), BF16)
        if fp32_sink:
            d["xnf"] = b.sb("ln_xnf", (128, E))
        S.append(d)
    eps_t = b.sb("ln_eps", (128, 1))
    b.memset("vector", eps_t[:], 1e-5)
    ps_tr = b.ps("ln_pstr", (128, 512), BF16) if xn_tiles is not None else None
    for t in range(NT):
        d = S[t % 2]
        x_t, sqs, ssq, srow = d["x_t"], d["sqs"], d["ssq"], d["srow"]
        m, msq, var, sd, rs, bl = d["m"], d["msq"], d["var"], d["sd"], d["rs"], d["bl"]
        b.dma("sync", x_t[:], x_dram[128 * t:128 * (t + 1), :])
        if x_tiles_out is not None:
            b.dma("scalar", x_tiles_out[128 * t:128 * (t + 1), :], x_t[:])
        b.act("scalar", sqs[:], x_t[:], AF.Square, accum=ssq[:, 0:1])
        b.red("vector", srow[:, 0:1], x_t[:], op=ALU.add)
        b.ts("vector", m[:], srow[:], 1.0 / E)
        b.tt("vector", msq[:], m[:], m[:], ALU.mult)
        b.stt("vector", var[:], ssq[:], 1.0 / E, msq[:], ALU.mult, ALU.subtract)
        b.act("scalar", sd[:], var[:], AF.Sqrt, bias=eps_t[:, 0:1])
        b.recip(rs[:], sd[:])
        b.stt("vector", bl[:], m[:], -1.0, rs[:], ALU.mult, ALU.mult)
        if xn_tiles is not None:
            xn = d["xn"]
            b.act("scalar", xn[:], x_t[:], AF.Identity, bias=bl[:, 0:1], scale=rs[:, 0:1])
        if fp32_sink:
            xnf = d["xnf"]
            b.act("scalar", xnf[:], x_t[:], AF.Identity, bias=bl[:, 0:1], scale=rs[:, 0:1])
            fp32_sink(t, xnf)
        if xn_tiles is not None:
            xn_all = xn_tiles[0].tensor
            for g4 in range(2):
                for slot in range(4):
                    i = 4 * g4 + slot
                    b.trans(ps_tr[:, 128 * slot:128 * (slot + 1)],
                            xn[:, 128 * i:128 * (i + 1)], identb[:])
                dst = bass.AP(xn_all, 4 * g4 * T + 128 * t,
                              [xn_tiles[0].ap[0], [T, 4], [1, 128]])
                b.fw.emit("scalar", lambda e, d=dst, s=ps_tr[:, 0:512]: e.copy(d, s),
                          reads=[ps_tr[:, 0:512]], writes=[dst])


def load_cast(b, dst_bf, src_dram, tmp, eng="gpsimd", scale=None):
    b.dma("sync", tmp[:], src_dram)
    if scale is None:
        b.copy(eng, dst_bf[:], tmp[:])
    else:
        b.smul(dst_bf[:], tmp[:], scale)


def emit_consts(b, rb=True):
    c = {}
    ident_in = b.inp("c_ident", (128, 128))
    c["identf"] = b.sb("identf", (128, 128))
    c["identb"] = b.sb("identb", (128, 128), BF16)
    b.dma("sync", c["identf"][:], ident_in)
    b.copy("gpsimd", c["identb"][:], c["identf"][:])
    if rb:
        rbkq_in = b.inp("c_rbkq", (128, 128))
        c["rbkqf"] = b.sb("rbkqf", (128, 128))
        c["rbkqb"] = b.sb("rbkqb", (128, 128), BF16)
        c["rbqkb"] = b.sb("rbqkb", (128, 128), BF16)
        c["fullnegb"] = b.sb("fullnegb", (128, 128), BF16)
        b.dma("sync", c["rbkqf"][:], rbkq_in)
        b.copy("gpsimd", c["rbkqb"][:], c["rbkqf"][:])
        b.memset("vector", c["fullnegb"][:], NEG)
        b.push()
        ps = b.ps("c_ps", (128, 128), BF16)
        b.trans(ps[:], c["rbkqb"][:], c["identb"][:])
        b.copy("scalar", c["rbqkb"][:], ps[:])
        b.pop()
    c["ones1"] = b.sb("ones1", (1, 128), BF16)
    b.memset("vector", c["ones1"][:], 1.0)
    return c


def host_consts(rb=True):
    ident = np.eye(128, dtype=np.float32)
    d = {"c_ident": ident}
    if rb:
        d["c_rbkq"] = np.where(np.arange(128)[:, None] > np.arange(128)[None, :],
                               np.float32(NEG), np.float32(0.0))
    return d


def flash_bufs(b):
    return {"mcol": [b.sb("fl_mcol", (128, 128)), b.sb("fl_mcol2", (128, 128))],
            "PT": [b.sb("fl_pt", (128, 1024), BF16), b.sb("fl_pt2", (128, 1024), BF16)],
            "rd": (lambda _r=b.sb("fl_rd", (1, 512)): [_r, _r])(),
            "rdb": (lambda _r=b.sb("fl_rdb", (1, 512), BF16): [_r, _r])(),
            "rsb": (lambda _r=b.sb("fl_rsb", (64, 512)): [_r, _r])()}


def flash_kmajor(b, consts, bufs, h_pair, ql_of, latKT_of, latV_of, kmax_of,
                 ctx_sink, sample_of, psR, ps_ctx2, sample_scale, qg_filler=None):
    identb = consts["identb"]
    rbkqb = consts["rbkqb"]
    mcol, PT, rd, rdb, rsb = (bufs["mcol"], bufs["PT"], bufs["rd"], bufs["rdb"],
                              bufs["rsb"])
    slots = list(enumerate(h_pair))
    ql = {}
    kts = {}
    for s, h in slots:
        ql[s] = ql_of(h)
        kts[s] = latKT_of(h)
    for qg in range(8):
        for tq in range(4):
            for s, h in slots:
                gt = 4 * qg + tq
                slot = 2 * tq + s
                reg = psR[0][:, 128 * slot:128 * (slot + 1)]
                s_lhsT, s_rhs, masked = sample_of(h, gt)
                b.mm(reg, s_lhsT, s_rhs, start=True, stop=not masked)
                if masked:
                    b.mm(reg, identb[:], consts["rbqkb"][:], start=False, stop=True)
                b.red("vector", mcol[s][:, 32 * tq:32 * tq + 1], reg,
                      op=ALU.max, neg=True)
        for s, h in slots:
            b.trans(psR[1][:, 128 * s:128 * (s + 1)], mcol[s][:], consts["identf"][:])
            for tq in range(4):
                b.copy("vector",
                       ql[s][64:65, 512 * qg + 128 * tq:512 * qg + 128 * (tq + 1)],
                       psR[1][32 * tq:32 * tq + 1, 128 * s:128 * (s + 1)])
    pending = []

    def flush_pending():
        while pending:
            pending.pop(0)()

    for qg in range(8):
        kmax = kmax_of(qg)
        jobs = []
        kt = 0
        while kt < kmax:
            g = min(2, kmax - kt)
            for s, _h in slots:
                jobs.append((s, kt, g))
            kt += g
        J = len(jobs)

        def st(j, qg=qg):
            s, kt0, g = jobs[j]
            ps = psR[j % 3]
            for jj in range(g):
                ktj = kt0 + jj
                d = ktj - 4 * qg
                extras = []
                if 0 <= d <= 3:
                    extras.append((128 * d, rbkqb))
                    for tq in range(max(0, d)):
                        extras.append((128 * tq, consts["fullnegb"]))
                b.mm(ps[:, 512 * jj:512 * (jj + 1)],
                     kts[s][:, 128 * ktj:128 * (ktj + 1)],
                     ql[s][:, 512 * qg:512 * (qg + 1)], start=True, stop=not extras)
                for ei, (off, mk) in enumerate(extras):
                    b.mm(ps[:, 512 * jj + off:512 * jj + off + 128],
                         identb[:], mk[:], start=False, stop=(ei == len(extras) - 1))

        def exp_pv(j, qg=qg, kmaxl=None):
            s, kt0, g = jobs[j]
            ps = psR[j % 3]
            pt = PT[s]
            h = h_pair[s]
            b.act("scalar", pt[:, 0:512 * g], ps[:, 0:512 * g], AF.Exp,
                  scale=sample_scale)
            for jj in range(g):
                ktj = kt0 + jj
                b.mm(ps_ctx2[s][0:65, :], latV_of(h, ktj), pt[:, 512 * jj:512 * (jj + 1)],
                     start=(ktj == 0), stop=(ktj == kmax - 1))

        st(0)
        if J > 1:
            st(1)
        flush_pending()
        if qg_filler is not None and qg == 4:
            qg_filler()
        if J > 2:
            st(2)
        for j in range(J):
            exp_pv(j)
            if j + 3 < J:
                st(j + 3)

        def mknorm(s, h, qg=qg):
            def norm():
                b.recip(rd[s][:], ps_ctx2[s][64:65, :])
                b.copy("vector", rdb[s][:], rd[s][:])
                b.mm(ps_ctx2[s][64:128, :], consts["ones1"][0:1, 0:64], rdb[s][:],
                     tile_position=(0, 64))
                b.copy("vector", rsb[s][:], ps_ctx2[s][64:128, :])
                ctx_sink(h, qg, ps_ctx2[s][0:64, :], rsb[s][:])
            return norm
        for s, h in slots:
            pending.append(mknorm(s, h))
    flush_pending()


def build_l0_attn(b=None, consts=None, x_in=None, x_out=None):
    standalone = b is None
    if standalone:
        b = B("l0a")
        x_in = b.inp("x", (T, E))
        x_out = b.outp("y", (T, E))
    wqT_in = b.inp("wqT", (E, E))
    wkvT_in = b.inp("wkvT", (E, 128))
    apk_in = b.inp("apk", (8, 128, 128))
    vup_in = b.inp("vup", (8, 128, 128))
    woutT_in = b.inp("woutT", (E, E))
    qlat_dram = b.dram("qlat_st", (16, 64, T), BF16)

    if consts is None:
        consts = emit_consts(b)
    wo = [b.sb(f"wo{k}", (128, E), BF16) for k in range(NE)]
    vup = b.sb("vup_outer", (128, 1024), BF16)
    latKT = b.sb("latKT", (65, T), BF16)
    latV = b.sb("latV", (128, 65 * NT), BF16)

    b.push()
    xn = alloc_xnT(b)
    emit_ln(b, x_in, xn, consts)

    wtmp = b.sb("wtmp", (128, E))
    wq = [b.sb(f"wq{k}", (128, E), BF16) for k in range(NE)]
    for k in range(NE):
        load_cast(b, wq[k], wqT_in[128 * k:128 * (k + 1), :], wtmp)
    wkv = [b.sb(f"wkv{k}", (128, 128), BF16) for k in range(NE)]
    wtmp2 = b.sb("wtmp2", (128, 128))
    for k in range(NE):
        load_cast(b, wkv[k], wkvT_in[128 * k:128 * (k + 1), :], wtmp2)
    apk = b.sb("apk_s", (128, 1024), BF16)
    atmp = b.sb("atmp", (128, 1024))
    for p in range(8):
        b.dma("sync", atmp[:, 128 * p:128 * (p + 1)], apk_in[p])
    b.smul(apk[:], atmp[:], 0.125)
    for p in range(8):
        b.dma("sync", atmp[:, 128 * p:128 * (p + 1)], vup_in[p])
    b.copy("gpsimd", vup[:], atmp[:])
    for k in range(NE):
        load_cast(b, wo[k], woutT_in[128 * k:128 * (k + 1), :], wtmp)

    ps_q = [b.ps("ps_q", (128, 512)), b.ps("ps_qb", (128, 512))]
    ps_l = [b.ps("ps_l", (128, 512)), b.ps("ps_lb", (128, 512))]
    qt_pair = [b.sb("qt_pair", (128, 512), BF16), b.sb("qt_pairb", (128, 512), BF16)]
    ql_sb = [b.sb("ql_sb", (128, 512), BF16), b.sb("ql_sbb", (128, 512), BF16)]
    for p in range(8):
        for n in range(8):
            w = n % 2
            for k in range(NE):
                b.mm(ps_q[w][:], wq[k][:, 128 * p:128 * (p + 1)],
                     xn[k][:, 512 * n:512 * (n + 1)], start=(k == 0), stop=(k == NE - 1))
            b.copy("vector", qt_pair[w][:], ps_q[w][:])
            b.mm(ps_l[w][:], apk[:, 128 * p:128 * (p + 1)], qt_pair[w][:])
            b.copy("vector", ql_sb[w][:], ps_l[w][:])
            b.dma("scalar", qlat_dram[2 * p, :, 512 * n:512 * (n + 1)], ql_sb[w][0:64, :])
            b.dma("scalar", qlat_dram[2 * p + 1, :, 512 * n:512 * (n + 1)], ql_sb[w][64:128, :])

    latVf = b.sb("latVf", (64, T), BF16)
    b.memset("vector", latKT[64:65, :], 1.0)
    b.memset("vector", latV[:], 1.0)
    ps_kv = [b.ps("ps_kv", (128, 512)), b.ps("ps_kvb", (128, 512))]
    for n in range(8):
        w = n % 2
        for k in range(NE):
            b.mm(ps_kv[w][:], wkv[k][:], xn[k][:, 512 * n:512 * (n + 1)],
                 start=(k == 0), stop=(k == NE - 1))
        b.copy("scalar", latKT[0:64, 512 * n:512 * (n + 1)], ps_kv[w][0:64, :])
        b.copy("scalar", latVf[:, 512 * n:512 * (n + 1)], ps_kv[w][64:128, :])
    ps_tv = b.ps("ps_tv", (128, 128), BF16)
    for t in range(NT):
        b.trans(ps_tv[:, 0:64], latVf[:, 128 * t:128 * (t + 1)], consts["identb"][0:64, 0:64])
        b.copy("scalar", latV[:, 65 * t:65 * t + 64], ps_tv[:, 0:64])

    b.pop()
    b.push()
    headsT = [b.sb(f"headsT{p}", (128, T), BF16) for p in range(8)]
    b.push()
    ql_buf = [b.sb(f"ql_buf{i}", (65, T), BF16) for i in range(4)]
    psR = [b.ps("psR0", (128, 1024)), b.ps("psR1", (128, 1024)),
           b.ps("psR2", (128, 1024))]
    ps_ctx2 = [b.ps("ps_ctx", (128, 512)), b.ps("ps_ctxB", (128, 512))]
    ctxn = b.sb("ctxn", (128, T), BF16)
    ps_vu = ps_ctx2[0]

    loaded = set()

    def load_h(h):
        if h in loaded or h > 15:
            return
        loaded.add(h)
        b.dma("sync", ql_buf[h % 4][0:64, :], qlat_dram[h])

    def ql_of(h):
        load_h(h)
        return ql_buf[h % 4]

    def sample_of(h, gt):
        buf = ql_buf[h % 4]
        stride = max(1, gt)
        lhsT = buf[0:64, 128 * gt:128 * (gt + 1)]
        rhs = latKT[0:64, 0:128 * stride:stride] if gt > 0 else latKT[0:64, 0:128]
        return lhsT, rhs, gt == 0

    def ctx_sink(h, qg, ctx_rows, rsb):
        dst = ctxn[64 * (h % 2):64 * (h % 2) + 64, 512 * qg:512 * (qg + 1)]
        b.tt("vector", dst, ctx_rows, rsb[:], ALU.mult)

    fbufs = flash_bufs(b)
    for p in range(8):
        flash_kmajor(b, consts, fbufs, [2 * p, 2 * p + 1], ql_of,
                     lambda hh: latKT,
                     lambda hh, kt: latV[:, 65 * kt:65 * (kt + 1)],
                     lambda qg: 4 * (qg + 1), ctx_sink, sample_of,
                     psR, ps_ctx2, 1.0,
                     qg_filler=(lambda p=p: (load_h(2 * p + 2), load_h(2 * p + 3))))
        for n in range(8):
            b.mm(ps_vu[:], vup[:, 128 * p:128 * (p + 1)], ctxn[:, 512 * n:512 * (n + 1)])
            b.copy("vector", headsT[p][:, 512 * n:512 * (n + 1)], ps_vu[:])

    b.pop()
    b.push()
    x_t2 = [b.sb("x_t2", (128, E)), b.sb("x_t2b", (128, E))]
    ps_o = [b.ps(f"ps_o{i}", (128, 512)) for i in range(4)]
    for t in range(NT):
        w = t % 2
        b.dma("sync", x_t2[w][:], x_in[128 * t:128 * (t + 1), :])
        for k in range(NE):
            b.mm(ps_o[2 * w][:], headsT[k][:, 128 * t:128 * (t + 1)], wo[k][:, 0:512],
                 start=(k == 0), stop=(k == NE - 1))
            b.mm(ps_o[2 * w + 1][:], headsT[k][:, 128 * t:128 * (t + 1)], wo[k][:, 512:1024],
                 start=(k == 0), stop=(k == NE - 1))
        b.tt("vector", x_t2[w][:, 0:512], x_t2[w][:, 0:512], ps_o[2 * w][:], ALU.add)
        b.tt("vector", x_t2[w][:, 512:1024], x_t2[w][:, 512:1024], ps_o[2 * w + 1][:], ALU.add)
        b.dma("scalar", x_out[128 * t:128 * (t + 1), :], x_t2[w][:])
    b.pop()
    if standalone:
        return b.done()
    return None


def l0_attn_inputs(x_core, inputs):
    return {
        "x": np.ascontiguousarray(x_core),
        "wqT": np.ascontiguousarray(inputs["L0_attn_Wq"].T),
        "wkvT": np.ascontiguousarray(np.concatenate(
            [inputs["L0_attn_Wk_down"].T, inputs["L0_attn_Wv_down"].T], axis=1)),
        "apk": pack_pairs_stack(inputs["L0_attn_q_to_latent"]),
        "vup": pack_pairs_diag(inputs["L0_attn_v_up"]),
        "woutT": np.ascontiguousarray(inputs["L0_attn_Wout"].T),
        **host_consts(),
    }


def pack_pairs_stack(A):
    out = np.zeros((8, 128, 128), dtype=np.float32)
    for p in range(8):
        out[p, 0:64, 0:64] = A[2 * p]
        out[p, 64:128, 64:128] = A[2 * p + 1]
    return out


def pack_pairs_diag(V):
    return pack_pairs_stack(V)




def build_l0_ffn(b=None, consts=None, x_in=None, x_out=None):
    standalone = b is None
    if standalone:
        b = B("l0f")
        x_in = b.inp("x", (T, E))
        x_out = b.outp("y", (T, E))
    wins = {}
    for pfx in "qkv":
        wins[pfx] = {
            "w1T": b.inp(f"{pfx}_w1T", (E, E)),
            "w2": b.inp(f"{pfx}_w2", (E, E)),
            "nwT": b.inp(f"{pfx}_nwT", (E, 15)),
            "nb": b.inp(f"{pfx}_nb", (1, 15)),
            "b1": b.inp(f"{pfx}_b1", (1, E)),
            "b2": b.inp(f"{pfx}_b2", (16, E)),
        }
    iota16_in = b.inp("iota16", (16, 1))
    e2_in = b.inp("e2", (16, E))
    hid_dram = b.dram("hid_st", (NE, 128, T), BF16)
    qT_dram = b.dram("qT_st", (NE, 128, T), BF16)
    kT_dram = b.dram("kT_st", (NE, 128, T), BF16)
    vT_dram = b.dram("vT_st", (NE, 128, T), BF16)

    if consts is None:
        consts = emit_consts(b, rb=False)
    onesrow_b = b.sb("onesrow_b", (1, 512), BF16)
    b.memset("vector", onesrow_b[:], 1.0)
    onesrow_f = b.sb("onesrow_f", (1, 512))
    b.memset("vector", onesrow_f[:], 1.0)
    onescol_b = b.sb("onescol_b", (128, 1), BF16)
    b.memset("vector", onescol_b[:], 1.0)
    iota16 = b.sb("iota16_s", (16, 1))
    b.dma("sync", iota16[:], iota16_in)
    kTs = [b.sb(f"kTs{k}", (128, 128), BF16) for k in range(NE)]

    b.push()
    xn = alloc_xnT(b)
    S_tok45 = b.sb("stok45", (128, 45 * NT))
    onehotT = {p: b.sb(f"oh_{p}", (16, T), BF16)[:, :] for p in "qkv"}
    e2b = b.sb("e2b", (16, E), BF16)
    e2t = b.sb("e2t", (16, E))
    b.dma("sync", e2t[:], e2_in)
    b.copy("gpsimd", e2b[:], e2t[:])
    b.push()
    nwT = [b.sb(f"nwTall{k}", (128, 45)) for k in range(NE)]
    for k in range(NE):
        for fi, pfx in enumerate("qkv"):
            b.dma("sync", nwT[k][:, 15 * fi:15 * (fi + 1)],
                  wins[pfx]["nwT"][128 * k:128 * (k + 1), :])
    nb_all = b.sb("nb_all", (1, 45))
    for fi, pfx in enumerate("qkv"):
        b.dma("sync", nb_all[0:1, 15 * fi:15 * (fi + 1)], wins[pfx]["nb"])
    xnTf = [[b.sb(f"xnTf{k}", (128, 512)) for k in range(NE)],
            [b.sb(f"xnTfb{k}", (128, 512)) for k in range(NE)]]
    ps_trf = b.ps("ps_trf", (128, 512))
    ps_sc = b.ps("ps_sc", (128, 512))
    sc_ev = b.sb("sc_ev", (45, 512))

    def fp32_sink(t, xnf):
        n = t // 4
        j = t % 4
        w = (t // 4) % 2
        for i in range(NE):
            slot = i % 4
            b.trans(ps_trf[:, 128 * slot:128 * (slot + 1)],
                    xnf[:, 128 * i:128 * (i + 1)], consts["identf"][:])
            b.copy("scalar", xnTf[w][i][:, 128 * j:128 * (j + 1)],
                   ps_trf[:, 128 * slot:128 * (slot + 1)])
        if j == 3:
            for k in range(NE):
                b.mm(ps_sc[0:45, 0:512], nwT[k][:], xnTf[w][k][:],
                     start=(k == 0), stop=False)
            b.mm(ps_sc[0:45, 0:512], nb_all[:], onesrow_f[:],
                 start=False, stop=True)
            b.copy("scalar", sc_ev[:], ps_sc[0:45, 0:512])
            for j2 in range(4):
                tt = 4 * n + j2
                b.trans(ps_trf[0:128, 0:45], sc_ev[:, 128 * j2:128 * (j2 + 1)],
                        consts["identf"][0:45, 0:45])
                b.copy("scalar", S_tok45[:, 45 * tt:45 * (tt + 1)],
                       ps_trf[0:128, 0:45])
    emit_ln(b, x_in, xn, consts, fp32_sink=fp32_sink)

    lf_dram = b.dram("lf_st", (3, 1, T), F32)
    g = [b.sb(f"rt_g{i}", (128, 32)) for i in range(4)]
    gi = [b.sb(f"rt_gi{i}", (128, 32), I32) for i in range(3)]
    sel1 = b.sb("rt_s1", (128, 32))
    sel2 = b.sb("rt_s2", (128, 32))
    sel3 = b.sb("rt_s3", (128, 32))
    tmpa = b.sb("rt_a", (128, 32))
    tmpb = b.sb("rt_b", (128, 32))
    leaf = b.sb("rt_leaf", (128, 32))
    lf_t = b.sb("rt_lft", (32, 128))
    lfrow = b.sb("rt_lfrow", (1, T))
    lfrow_b = b.sb("rt_lfrowb", (1, T), BF16)
    ps_rt = b.ps("ps_rt", (128, 512))

    def sel(out, cond_i32, on_true, on_false):
        b.copy("vector", out, on_false)
        b.cpred(out, cond_i32, on_true)

    for fi, pfx in enumerate("qkv"):
        sc = lambda i: S_tok45[:, 15 * fi + i::45]
        b.ts("vector", g[0][:], sc(0), 0.0, op0=ALU.is_ge)
        b.ts("vector", gi[0][:], sc(0), 0.0, op0=ALU.is_ge)
        sel(sel1[:], gi[0][:], sc(2), sc(1))
        b.ts("vector", g[1][:], sel1[:], 0.0, op0=ALU.is_ge)
        b.ts("vector", gi[1][:], sel1[:], 0.0, op0=ALU.is_ge)
        sel(tmpa[:], gi[0][:], sc(6), sc(4))
        sel(tmpb[:], gi[0][:], sc(5), sc(3))
        sel(sel2[:], gi[1][:], tmpa[:], tmpb[:])
        b.ts("vector", g[2][:], sel2[:], 0.0, op0=ALU.is_ge)
        b.ts("vector", gi[2][:], sel2[:], 0.0, op0=ALU.is_ge)
        sel(tmpa[:], gi[2][:], sc(8), sc(7))
        sel(tmpb[:], gi[2][:], sc(10), sc(9))
        sel(sel3[:], gi[1][:], tmpb[:], tmpa[:])
        sel(tmpa[:], gi[2][:], sc(12), sc(11))
        sel(tmpb[:], gi[2][:], sc(14), sc(13))
        sel(sel1[:], gi[1][:], tmpb[:], tmpa[:])
        sel(sel2[:], gi[0][:], sel1[:], sel3[:])
        b.ts("vector", g[3][:], sel2[:], 0.0, op0=ALU.is_ge)
        b.stt("vector", leaf[:], g[0][:], 2.0, g[1][:], ALU.mult, ALU.add)
        b.stt("vector", leaf[:], leaf[:], 2.0, g[2][:], ALU.mult, ALU.add)
        b.stt("vector", leaf[:], leaf[:], 2.0, g[3][:], ALU.mult, ALU.add)
        b.trans(ps_rt[0:32, 0:128], leaf[:], consts["identf"][:])
        b.copy("scalar", lf_t[:], ps_rt[0:32, 0:128])
        b.dma("scalar", lf_dram[fi].rearrange("o (a b) -> (o a) b", a=32), lf_t[:])
        b.dma("sync", lfrow[:], lf_dram[fi])
        b.copy("vector", lfrow_b[:], lfrow[:])
        for n in range(8):
            b.mm(ps_rt[0:16, 0:512], consts["ones1"][0:1, 0:16],
                 lfrow_b[0:1, 512 * n:512 * (n + 1)])
            b.ts("vector", onehotT[pfx][:, 512 * n:512 * (n + 1)],
                 ps_rt[0:16, 0:512], iota16[:, 0:1], op0=ALU.is_equal)
    b.pop()

    hb = {}
    for pfx in "qkv":
        hb[pfx] = {}
        bb = hb[pfx]
        bb["b1hi"] = b.sb(f"b1hi_{pfx}", (1, E), BF16)
        bb["b1lo"] = b.sb(f"b1lo_{pfx}", (1, E), BF16)
        bb["b2hi"] = b.sb(f"b2hi_{pfx}", (16, E), BF16)
        bb["b2lo"] = b.sb(f"b2lo_{pfx}", (16, E), BF16)
    b.push()
    btmp = b.sb("btmp", (16, E))
    btmp2 = b.sb("btmp2", (16, E))
    for pfx in "qkv":
        bb = hb[pfx]
        b.dma("sync", btmp[0:1, :], wins[pfx]["b1"])
        b.copy("vector", bb["b1hi"][:], btmp[0:1, :])
        b.tt("vector", btmp2[0:1, :], btmp[0:1, :], bb["b1hi"][:], ALU.subtract)
        b.copy("vector", bb["b1lo"][:], btmp2[0:1, :])
        b.dma("sync", btmp[:], wins[pfx]["b2"])
        b.copy("vector", bb["b2hi"][:], btmp[:])
        b.tt("vector", btmp2[:], btmp[:], bb["b2hi"][:], ALU.subtract)
        b.copy("vector", bb["b2lo"][:], btmp2[:])
    b.pop()

    ps_p = [b.ps(f"ps_p{i}", (128, 512)) for i in range(NE)]
    w1set = [[b.sb(f"w1a{k}", (128, E), BF16) for k in range(NE)],
             [b.sb(f"w1b{k}", (128, E), BF16) for k in range(NE)]]
    w2 = [b.sb(f"w2s{k}", (128, E), BF16) for k in range(NE)]
    hev = [b.sb(f"hev{i}", (128, 512), BF16) for i in range(4)]
    hch = [[b.sb(f"hch{k}", (128, 512), BF16) for k in range(NE)],
           [b.sb(f"hchb{k}", (128, 512), BF16) for k in range(NE)]]
    pev = [b.sb("pev", (128, 512), BF16), b.sb("pevb", (128, 512), BF16)]
    for fi, pfx in enumerate("qkv"):
        w1 = w1set[fi % 2]
        for k in range(NE):
            b.dma("gpsimd", w1[k][:], wins[pfx]["w1T"][128 * k:128 * (k + 1), :])
        for k in range(NE):
            b.dma("gpsimd", w2[k][:], wins[pfx]["w2"][128 * k:128 * (k + 1), :])
        for n in range(8):
            for i in range(NE):
                w = (n * NE + i) % 4
                for k in range(NE):
                    b.mm(ps_p[w][:], w1[k][:, 128 * i:128 * (i + 1)],
                         xn[k][:, 512 * n:512 * (n + 1)], start=(k == 0), stop=False)
                b.mm(ps_p[w][:], hb[pfx]["b1hi"][0:1, 128 * i:128 * (i + 1)], onesrow_b[:],
                     start=False, stop=False)
                b.mm(ps_p[w][:], e2b[:, 128 * i:128 * (i + 1)],
                     onehotT[pfx][:, 512 * n:512 * (n + 1)], start=False, stop=True)
                b.act("scalar", hev[w][:], ps_p[w][:], AF.Relu)
                b.dma("scalar", hid_dram[i, :, 512 * n:512 * (n + 1)], hev[w][:])
        for n in range(8):
            w = n % 2
            for k in range(NE):
                b.dma("sync", hch[w][k][:], hid_dram[k, :, 512 * n:512 * (n + 1)])
            for i in range(NE):
                for k in range(NE):
                    b.mm(ps_p[i][:], w2[k][:, 128 * i:128 * (i + 1)], hch[w][k][:],
                         start=(k == 0), stop=False)
                b.mm(ps_p[i][:], hb[pfx]["b2hi"][:, 128 * i:128 * (i + 1)],
                     onehotT[pfx][:, 512 * n:512 * (n + 1)], start=False, stop=True)
            for i in range(NE):
                if pfx == "q":
                    b.smul(pev[w][:], ps_p[i][:], 0.03125)
                    b.dma("scalar", qT_dram[i, :, 512 * n:512 * (n + 1)], pev[w][:])
                elif pfx == "k":
                    b.copy("scalar", pev[w][:], ps_p[i][:])
                    b.dma("scalar", kT_dram[i, :, 512 * n:512 * (n + 1)], pev[w][:])
                    b.copy("vector", kTs[i][:, 16 * n:16 * (n + 1)], pev[w][:, 0:512:32])
                else:
                    b.copy("scalar", pev[w][:], ps_p[i][:])
                    b.dma("scalar", vT_dram[i, :, 512 * n:512 * (n + 1)], pev[w][:])

    b.pop()

    b.push()
    v_tok = b.sb("v_tok", (128, NT * E), BF16)
    b.push()
    vch = b.sb("vch", (128, T), BF16)
    ps_tv2 = b.ps("ps_tv2", (128, 128), BF16)
    for i in range(NE):
        b.dma("sync", vch[:], vT_dram[i])
        for tt in range(NT):
            b.trans(ps_tv2[:], vch[:, 128 * tt:128 * (tt + 1)], consts["identb"][:])
            b.copy("scalar",
                   v_tok[:, 1024 * tt + 128 * i:1024 * tt + 128 * (i + 1)],
                   ps_tv2[:])
    b.pop()

    qTg2 = [[b.sb(f"qTg{k}", (128, 512), BF16) for k in range(NE)],
            [b.sb(f"qTgb{k}", (128, 512), BF16) for k in range(NE)]]
    kTg = [b.sb(f"kTg{k}", (128, T), BF16) for k in range(NE)]
    mcol = b.sb("at_mcol", (128, 128))
    negm = b.sb("at_negm", (1, 512), BF16)
    PT_half2 = [b.sb("at_pthalf", (128, 512 * 16), BF16),
                b.sb("at_pthalf2", (128, 512 * 16), BF16)]
    ps_bigA = b.ps("at_psbigA", (128, 1024))
    ps_bigB = b.ps("at_psbigB", (128, 1024))
    ps_ctx = b.ps("at_psctx", (128, 1280))
    ps_sm = b.ps("at_pssm", (128, 512))
    rr = b.sb("at_rr", (128, 1))
    ctx_acc = b.sb("at_ctxacc", (128, 4100), BF16)
    x_t32 = [b.sb("at_xt", (128, E)), b.sb("at_xt2", (128, E))]
    for k in range(NE):
        b.dma("sync", kTg[k][:], kT_dram[k])
    pend = [None]
    for tq in range(2):
        b.dma("gpsimd", x_t32[tq][:], x_in[128 * tq:128 * (tq + 1), :])
    for qg in range(8):
        qTg = qTg2[qg % 2]
        for k in range(NE):
            b.dma("sync", qTg[k][:], qT_dram[k, :, 512 * qg:512 * (qg + 1)])
        for tq in range(4):
            for k in range(NE):
                b.mm(ps_sm[:, 128 * tq:128 * (tq + 1)],
                     qTg[k][:, 128 * tq:128 * (tq + 1)], kTs[k][:],
                     start=(k == 0), stop=(k == NE - 1))
            b.red("vector", mcol[:, 32 * tq:32 * tq + 1],
                  ps_sm[:, 128 * tq:128 * (tq + 1)], op=ALU.max, neg=True)
        if pend[0] is not None:
            pend[0]()
            pend[0] = None
        b.trans(ps_ctx[:, 1152:1280], mcol[:], consts["identf"][:])
        for tq in range(4):
            b.copy("scalar", negm[0:1, 128 * tq:128 * (tq + 1)],
                   ps_ctx[32 * tq:32 * tq + 1, 1152:1280])
        for half in range(2):
            kt0 = 16 * half
            PT_half = PT_half2[half]

            def emit_st(gi):
                ps = ps_bigA if gi % 2 == 0 else ps_bigB
                for j in range(2):
                    ktj = kt0 + 2 * gi + j
                    for k in range(NE):
                        b.mm(ps[:, 512 * j:512 * (j + 1)],
                             kTg[k][:, 128 * ktj:128 * (ktj + 1)], qTg[k][:],
                             start=(k == 0), stop=False)
                    b.mm(ps[:, 512 * j:512 * (j + 1)], onesrow_b[0:1, 0:128],
                         negm[:], start=False, stop=True)

            def emit_exp(gi):
                ps = ps_bigA if gi % 2 == 0 else ps_bigB
                b.act("scalar", PT_half[:, 1024 * gi:1024 * (gi + 1)],
                      ps[:, 0:1024], AF.Exp)

            emit_st(0)
            for gi in range(8):
                if gi + 1 < 8:
                    emit_st(gi + 1)
                emit_exp(gi)
            for tq in range(4):
                for kt in range(kt0, kt0 + 16):
                    lhsT = PT_half[:, 512 * (kt - kt0) + 128 * tq:
                                   512 * (kt - kt0) + 128 * (tq + 1)]
                    va = v_tok[:, 1024 * kt:1024 * (kt + 1)]
                    b.mm(ps_ctx[:, 0:512], lhsT, va[:, 0:512],
                         start=(kt == kt0), stop=(kt == kt0 + 15))
                    b.mm(ps_ctx[:, 512:1024], lhsT, va[:, 512:1024],
                         start=(kt == kt0), stop=(kt == kt0 + 15))
                    b.mm(ps_ctx[:, 1024:1025], lhsT, onescol_b[:, 0:1],
                         start=(kt == kt0), stop=(kt == kt0 + 15))
                if half == 0:
                    b.copy("vector", ctx_acc[:, 1025 * tq:1025 * (tq + 1)],
                           ps_ctx[:, 0:1025])
                else:
                    b.tt("vector", ctx_acc[:, 1025 * tq:1025 * (tq + 1)],
                         ctx_acc[:, 1025 * tq:1025 * (tq + 1)], ps_ctx[:, 0:1025],
                         ALU.add)
        def mktail(qg=qg):
            def tail():
                for tq in range(4):
                    t_glob = 4 * qg + tq
                    xb = x_t32[tq % 2]
                    b.recip(rr[:], ctx_acc[:, 1025 * tq + 1024:1025 * tq + 1025])
                    b.stt("vector", xb[:], ctx_acc[:, 1025 * tq:1025 * tq + 1024],
                          rr[:, 0:1], xb[:], ALU.mult, ALU.add)
                    b.dma("scalar", x_out[128 * t_glob:128 * (t_glob + 1), :], xb[:])
                    nt = t_glob + 2
                    if nt < T // 128:
                        b.dma("gpsimd", x_t32[tq % 2][:],
                              x_in[128 * nt:128 * (nt + 1), :])
            return tail
        pend[0] = mktail()
    pend[0]()
    b.pop()
    if standalone:
        return b.done()
    return None


def l0_ffn_inputs(x_core, inputs):
    BIG = 4096.0
    ind = (np.arange(16)[:, None] == (np.arange(E) // 64)[None, :])
    e2 = (BIG * (ind.astype(np.float32) - 1.0)).astype(np.float32)
    d = {"x": np.ascontiguousarray(x_core),
         "iota16": np.arange(16, dtype=np.float32).reshape(16, 1),
         "e2": e2, **host_consts(rb=False)}
    for pfx in "qkv":
        d[f"{pfx}_w1T"] = np.ascontiguousarray(
            inputs[f"L0_ffn_{pfx}_w1"].transpose(1, 0, 2).reshape(E, E))
        d[f"{pfx}_w2"] = np.ascontiguousarray(
            inputs[f"L0_ffn_{pfx}_w2"].reshape(E, E))
        d[f"{pfx}_nwT"] = np.ascontiguousarray(inputs[f"L0_ffn_{pfx}_node_w"].T)
        d[f"{pfx}_nb"] = np.ascontiguousarray(
            inputs[f"L0_ffn_{pfx}_node_b"].reshape(1, 15))
        d[f"{pfx}_b1"] = np.ascontiguousarray(
            inputs[f"L0_ffn_{pfx}_b1"].reshape(1, E))
        d[f"{pfx}_b2"] = np.ascontiguousarray(inputs[f"L0_ffn_{pfx}_b2"])
    return d




def build_l1_attn(b=None, consts=None, x_in=None, x_out=None):
    standalone = b is None
    if standalone:
        b = B("l1a")
        x_in = b.inp("x", (T, E))
        x_out = b.outp("y", (T, E))
    wattn_in = b.inp("wattn", (E, 3 * E))
    battn_in = b.inp("battn", (1, 3 * E))
    wproj_in = b.inp("wproj", (E, E))
    bproj_in = b.inp("bproj", (1, E))
    cosf_in = b.inp("cosf", (128, 512))
    sinf_in = b.inp("sinf", (128, 512))
    qh_dram = b.dram("qh_st", (16, 64, T), BF16)
    kh_dram = b.dram("kh_st", (16, 64, T), BF16)
    v_dram = b.dram("v_st", (T, E), BF16)

    if consts is None:
        consts = emit_consts(b)
    onesrow_b = b.sb("onesrow_b", (1, 512), BF16)
    b.memset("vector", onesrow_b[:], 1.0)

    b.push()
    xn = alloc_xnT(b)
    b.push()
    emit_ln(b, x_in, xn, consts)
    b.pop()

    b.push()
    brot_hi = b.sb("brot_hi", (1, 3 * E), BF16)
    brot_lo = b.sb("brot_lo", (1, 3 * E), BF16)
    cosf = b.sb("cosf_s", (128, 512))
    sinf = b.sb("sinf_s", (128, 512))
    b.dma("sync", cosf[:], cosf_in)
    b.dma("sync", sinf[:], sinf_in)
    wb = [b.sb(f"wb{k}", (128, 3 * E), BF16) for k in range(NE)]
    wld = b.sb("wld", (128, 3 * E))
    t1 = b.sb("rot_t1", (128, 512))
    t2 = b.sb("rot_t2", (128, 512))
    t3 = b.sb("rot_t3", (128, 512))
    t4 = b.sb("rot_t4", (128, 512))
    for k in range(NE):
        b.dma("sync", wld[:], wattn_in[128 * k:128 * (k + 1), :])
        for half in range(2):
            o0 = 1024 * half
            ev = wld[:, o0:o0 + 1024:2]
            od = wld[:, o0 + 1:o0 + 1024:2]
            b.tt("vector", t1[:], ev, cosf[:], ALU.mult)
            b.tt("vector", t2[:], od, sinf[:], ALU.mult)
            b.tt("vector", t3[:], ev, sinf[:], ALU.mult)
            b.tt("vector", t4[:], od, cosf[:], ALU.mult)
            b.tt("vector", wb[k][:, o0:o0 + 1024:2], t1[:], t2[:], ALU.subtract)
            b.tt("vector", wb[k][:, o0 + 1:o0 + 1024:2], t3[:], t4[:], ALU.add)
        b.copy("gpsimd", wb[k][:, 2048:3072], wld[:, 2048:3072])
    bld = b.sb("bld", (1, 3 * E))
    brot = b.sb("brot", (1, 3 * E))
    blo = b.sb("blo", (1, 3 * E))
    b.dma("sync", bld[:], battn_in)
    for half in range(2):
        o0 = 1024 * half
        ev = bld[0:1, o0:o0 + 1024:2]
        od = bld[0:1, o0 + 1:o0 + 1024:2]
        b.tt("vector", t1[0:1, :], ev, cosf[0:1, :], ALU.mult)
        b.tt("vector", t2[0:1, :], od, sinf[0:1, :], ALU.mult)
        b.tt("vector", t3[0:1, :], ev, sinf[0:1, :], ALU.mult)
        b.tt("vector", t4[0:1, :], od, cosf[0:1, :], ALU.mult)
        b.tt("vector", brot[0:1, o0:o0 + 1024:2], t1[0:1, :], t2[0:1, :], ALU.subtract)
        b.tt("vector", brot[0:1, o0 + 1:o0 + 1024:2], t3[0:1, :], t4[0:1, :], ALU.add)
    b.copy("vector", brot[0:1, 2048:3072], bld[0:1, 2048:3072])
    b.copy("vector", brot_hi[:], brot[:])
    b.tt("vector", blo[:], brot[:], brot_hi[:], ALU.subtract)
    b.copy("vector", brot_lo[:], blo[:])

    ps_qk = [b.ps(f"ps_qk{i}", (128, 512)) for i in range(3)]
    qev = [b.sb("qev", (128, 512), BF16), b.sb("qev2", (128, 512), BF16),
           b.sb("qev3", (128, 512), BF16)]
    rr_i = [0]
    for ot in range(16):
        for n in range(8):
            w = rr_i[0] % 3
            rr_i[0] += 1
            for k in range(NE):
                b.mm(ps_qk[w][:], wb[k][:, 128 * ot:128 * (ot + 1)],
                     xn[k][:, 512 * n:512 * (n + 1)], start=(k == 0), stop=False)
            b.mm(ps_qk[w][:], brot_hi[0:1, 128 * ot:128 * (ot + 1)], onesrow_b[:],
                 start=False, stop=True)
            if ot < 8:
                b.smul(qev[w][:], ps_qk[w][:], 0.125)
                b.dma("scalar", qh_dram[2 * ot, :, 512 * n:512 * (n + 1)], qev[w][0:64, :])
                b.dma("scalar", qh_dram[2 * ot + 1, :, 512 * n:512 * (n + 1)], qev[w][64:128, :])
            else:
                b.copy("scalar", qev[w][:], ps_qk[w][:])
                h0 = 2 * (ot - 8)
                b.dma("scalar", kh_dram[h0, :, 512 * n:512 * (n + 1)], qev[w][0:64, :])
                b.dma("scalar", kh_dram[h0 + 1, :, 512 * n:512 * (n + 1)], qev[w][64:128, :])
    vev = [b.sb("vev", (128, 512), BF16), b.sb("vevb", (128, 512), BF16),
           b.sb("vevc", (128, 512), BF16)]
    for t in range(NT):
        for c in range(2):
            w = rr_i[0] % 3
            rr_i[0] += 1
            for k in range(NE):
                b.mm(ps_qk[w][:], xn[k][:, 128 * t:128 * (t + 1)],
                     wb[k][:, 2048 + 512 * c:2048 + 512 * (c + 1)],
                     start=(k == 0), stop=False)
            b.mm(ps_qk[w][:], consts["ones1"][0:1, 0:128],
                 brot_hi[0:1, 2048 + 512 * c:2048 + 512 * (c + 1)],
                 start=False, stop=True)
            b.copy("scalar", vev[w][:], ps_qk[w][:])
            b.dma("scalar", v_dram[128 * t:128 * (t + 1), 512 * c:512 * (c + 1)], vev[w][:])
    b.pop()
    b.pop()

    b.push()
    ctxn = [b.sb(f"ctxn{p}", (128, T), BF16) for p in range(8)]
    b.push()
    v_aug = b.sb("v_aug", (128, NT * 1040), BF16)
    vaug_v = v_aug[:].rearrange("p (a c) -> p a c", c=65)
    b.memset("vector", vaug_v[:, :, 64:65], 1.0)
    for kt in range(NT):
        dst = v_aug[:, 1040 * kt:1040 * (kt + 1)].rearrange(
            "p (h c) -> p h c", c=65)[:, :, 0:64]
        b.dma("sync", dst, v_dram[128 * kt:128 * (kt + 1), :].rearrange(
            "p (h d) -> p h d", d=64))
    ql_buf = [b.sb(f"ql_buf{i}", (65, T), BF16) for i in range(4)]
    kl_buf = [b.sb(f"kl_buf{i}", (65, T), BF16) for i in range(4)]
    for i in range(4):
        b.memset("vector", kl_buf[i][64:65, :], 1.0)
    psR = [b.ps("psR0", (128, 1024)), b.ps("psR1", (128, 1024)),
           b.ps("psR2", (128, 1024))]
    ps_ctx2 = [b.ps("ps_ctx", (128, 512)), b.ps("ps_ctxB", (128, 512))]
    fbufs = flash_bufs(b)
    loaded = set()

    def load_h(h):
        if h in loaded or h > 15:
            return
        loaded.add(h)
        b.dma("sync", ql_buf[h % 4][0:64, :], qh_dram[h])
        b.dma("sync", kl_buf[h % 4][0:64, :], kh_dram[h])

    def ql_of(h):
        load_h(h)
        return ql_buf[h % 4]

    def latKT_of(h):
        return kl_buf[h % 4]

    def latV_of(h, kt):
        return v_aug[:, 1040 * kt + 65 * h:1040 * kt + 65 * (h + 1)]

    def sample_of(h, gt):
        buf = ql_buf[h % 4]
        kb = kl_buf[h % 4]
        stride = max(1, gt)
        lhsT = buf[0:64, 128 * gt:128 * (gt + 1)]
        rhs = kb[0:64, 0:128 * stride:stride] if gt > 0 else kb[0:64, 0:128]
        return lhsT, rhs, gt == 0

    def ctx_sink(h, qg, ctx_rows, rsb):
        dst = ctxn[h // 2][64 * (h % 2):64 * (h % 2) + 64, 512 * qg:512 * (qg + 1)]
        b.tt("vector", dst, ctx_rows, rsb[:], ALU.mult)

    for p in range(8):
        flash_kmajor(b, consts, fbufs, [2 * p, 2 * p + 1], ql_of, latKT_of,
                     latV_of, lambda qg: 4 * (qg + 1), ctx_sink, sample_of,
                     psR, ps_ctx2, 1.0,
                     qg_filler=(lambda p=p: (load_h(2 * p + 2), load_h(2 * p + 3))))
    b.pop()

    b.push()
    wp = [b.sb(f"wp{k}", (128, E), BF16) for k in range(NE)]
    wldp = b.sb("wldp", (128, E))
    for k in range(NE):
        load_cast(b, wp[k], wproj_in[128 * k:128 * (k + 1), :], wldp)
    bp_hi = b.sb("bp_hi", (1, E), BF16)
    bp_lo = b.sb("bp_lo", (1, E), BF16)
    bpl = b.sb("bpl", (1, E))
    bpl2 = b.sb("bpl2", (1, E))
    b.dma("sync", bpl[:], bproj_in)
    b.copy("vector", bp_hi[:], bpl[:])
    b.tt("vector", bpl2[:], bpl[:], bp_hi[:], ALU.subtract)
    b.copy("vector", bp_lo[:], bpl2[:])
    x_t2 = [b.sb("x_t2", (128, E)), b.sb("x_t2b", (128, E))]
    ps_o = [b.ps(f"ps_o{i}", (128, 512)) for i in range(4)]
    for t in range(NT):
        w = t % 2
        b.dma("sync", x_t2[w][:], x_in[128 * t:128 * (t + 1), :])
        for c in range(2):
            pso = ps_o[2 * w + c]
            for k in range(NE):
                b.mm(pso[:], ctxn[k][:, 128 * t:128 * (t + 1)],
                     wp[k][:, 512 * c:512 * (c + 1)], start=(k == 0), stop=False)
            b.mm(pso[:], consts["ones1"][0:1, 0:128],
                 bp_hi[0:1, 512 * c:512 * (c + 1)], start=False, stop=True)
            b.tt("vector", x_t2[w][:, 512 * c:512 * (c + 1)],
                 x_t2[w][:, 512 * c:512 * (c + 1)], pso[:], ALU.add)
        b.dma("scalar", x_out[128 * t:128 * (t + 1), :], x_t2[w][:])
    b.pop()
    if standalone:
        return b.done()
    return None


def l1_attn_inputs(x_core, inputs):
    H, hd = 16, 64
    theta = 1.0 / 10000.0 ** (np.arange(0, hd, 2, dtype=np.float64) / hd)
    ang = (np.arange(H, dtype=np.float64)[:, None] * theta[None, :])
    cos = np.cos(ang).astype(np.float32).reshape(1, 512)
    sin = np.sin(ang).astype(np.float32).reshape(1, 512)
    return {"x": np.ascontiguousarray(x_core),
            "wattn": np.ascontiguousarray(inputs["L1_attn_W_attn"]),
            "battn": np.ascontiguousarray(inputs["L1_attn_b_attn"].reshape(1, -1)),
            "wproj": np.ascontiguousarray(inputs["L1_attn_W_proj"]),
            "bproj": np.ascontiguousarray(inputs["L1_attn_b_proj"].reshape(1, -1)),
            "cosf": np.broadcast_to(cos, (128, 512)).copy(),
            "sinf": np.broadcast_to(sin, (128, 512)).copy(),
            **host_consts()}




def emit_quant(b, dst_bf, src, nblk, qb):
    w = 16 * nblk
    am, sc, rsc = qb["am"], qb["sc"], qb["rsc"]
    xn, sq, ca, cb, cb2 = (qb["xn"], qb["sq"], qb["ca"], qb["cb"], qb["cb2"])
    blk = lambda ap: ap.rearrange("p (a c) -> p a c", c=16)
    b.red("vector", am[:, 0:nblk].unsqueeze(2), blk(src), op=ALU.max, absval=True)
    b.ts("vector", sc[:, 0:nblk], am[:, 0:nblk], 1.0 / 6.0, 1e-12, op0=ALU.mult, op1=ALU.max)
    b.recip(rsc[:, 0:nblk], sc[:, 0:nblk])
    b.act("scalar", cb[:, 0:w], src, AF.Sign)
    b.tt("vector", blk(xn[:, 0:w]), blk(src),
         rsc[:, 0:nblk].unsqueeze(2).broadcast_to((128, nblk, 16)), ALU.mult)
    b.tt("vector", sq[:, 0:w], xn[:, 0:w], xn[:, 0:w], ALU.mult)
    THR = [(0.0625, 0.5), (0.5625, 0.5), (1.5625, 0.5), (3.0625, 0.5),
           (6.25, 1.0), (12.25, 1.0), (25.0, 2.0)]
    t0_, w0_ = THR[0]
    b.ts("vector", ca[:, 0:w], sq[:, 0:w], t0_, w0_, op0=ALU.is_gt, op1=ALU.mult)
    for ti, wi in THR[1:]:
        b.ts("vector", cb2[:, 0:w], sq[:, 0:w], ti, wi, op0=ALU.is_gt, op1=ALU.mult)
        b.tt("vector", ca[:, 0:w], ca[:, 0:w], cb2[:, 0:w], ALU.add)
    b.tt("vector", ca[:, 0:w], ca[:, 0:w], cb[:, 0:w], ALU.mult)
    b.tt("vector", blk(dst_bf), blk(ca[:, 0:w]),
         sc[:, 0:nblk].unsqueeze(2).broadcast_to((128, nblk, 16)), ALU.mult)


def quant_bufs(b, maxw=2816):
    nb = maxw // 16
    return {"am": b.sb("q_am", (128, nb)), "sc": b.sb("q_sc", (128, nb)),
            "rsc": b.sb("q_rsc", (128, nb)), "xn": b.sb("q_xn", (128, maxw), BF16),
            "sq": b.sb("q_sq", (128, maxw), BF16), "ca": b.sb("q_ca", (128, maxw), BF16),
            "cb": b.sb("q_cb", (128, maxw), BF16),
            "cb2": b.sb("q_cb2", (128, maxw), BF16)}


def build_l1_ffn(b=None, consts=None, x_in=None, x_out=None):
    standalone = b is None
    I = 2816
    NI = I // 128
    if standalone:
        b = B("l1f")
        x_in = b.inp("x", (T, E))
        x_out = b.outp("y", (T, E))
    wg_in = b.inp("wg", (I, E))
    wu_in = b.inp("wu", (I, E))
    wd_in = b.inp("wd", (E, I))
    xqT_dram = b.dram("xqT_st", (NT, 128, E), BF16)
    hqT_dram = b.dram("hqT_st", (NT, 128, I), BF16)
    wdqT_dram = b.dram("wdqT_st", (NI, 128, E), BF16)

    if consts is None:
        consts = emit_consts(b, rb=False)
    qb = quant_bufs(b)
    wgqT = [b.sb(f"wgqT{k}", (128, I), BF16) for k in range(NE)]
    wuqT = [b.sb(f"wuqT{k}", (128, I), BF16) for k in range(NE)]

    b.push()
    qb2 = [quant_bufs(b, maxw=E), quant_bufs(b, maxw=E)]
    xq_t = [b.sb("xq_t", (128, E), BF16), b.sb("xq_t2", (128, E), BF16)]
    xq_acc = [b.sb("xq_acc", (128, E), BF16), b.sb("xq_acc2", (128, E), BF16)]
    ps_tr = b.ps("ps_tr", (128, 512), BF16)

    xpost = [None]

    def x_sink(t, xnf):
        s = t % 2
        if xpost[0] is not None:
            xpost[0]()
            xpost[0] = None
        emit_quant(b, xq_t[s][:], xnf[:], E // 16, qb2[s])

        def mk(t=t, s=s):
            def fn():
                for g4 in range(2):
                    for slot in range(4):
                        i = 4 * g4 + slot
                        b.trans(ps_tr[:, 128 * slot:128 * (slot + 1)],
                                xq_t[s][:, 128 * i:128 * (i + 1)], consts["identb"][:])
                    b.copy("scalar", xq_acc[s][:, 512 * g4:512 * (g4 + 1)], ps_tr[:, 0:512])
                b.dma("scalar", xqT_dram[t], xq_acc[s][:])
            return fn
        xpost[0] = mk()
        if t == NT - 1:
            xpost[0]()
            xpost[0] = None
    emit_ln(b, x_in, None, consts, fp32_sink=x_sink)
    b.pop()

    b.push()
    qbw = [quant_bufs(b), quant_bufs(b)]
    wld = [b.sb("wld", (128, I)), b.sb("wld2", (128, I))]
    wqt = [qbw[0]["xn"], qbw[1]["xn"]]
    ps_trw = b.ps("ps_trw", (128, 512), BF16)
    for wi, (wsrc, wdstT) in enumerate(((wg_in, wgqT), (wu_in, wuqT))):
        for j in range(NI):
            s = j % 2
            b.dma("sync", wld[s][:, 0:E], wsrc[128 * j:128 * (j + 1), :])
            emit_quant(b, wqt[s][:, 0:E], wld[s][:, 0:E], E // 16, qbw[s])
            for k in range(NE):
                slot = k % 4
                b.trans(ps_trw[:, 128 * slot:128 * (slot + 1)],
                        wqt[s][:, 128 * k:128 * (k + 1)], consts["identb"][:])
                b.copy("scalar", wdstT[k][:, 128 * j:128 * (j + 1)],
                       ps_trw[:, 128 * slot:128 * (slot + 1)])
    _wa = b.sb("wdq_acc", (128, I), BF16)
    wdq_acc = [_wa, _wa]
    for j in range(NE):
        s = j % 2
        b.dma("sync", wld[s][:], wd_in[128 * j:128 * (j + 1), :])
        emit_quant(b, wqt[s][:], wld[s][:], I // 16, qbw[s])
        for kk in range(NI):
            slot = kk % 4
            b.trans(ps_trw[:, 128 * slot:128 * (slot + 1)],
                    wqt[s][:, 128 * kk:128 * (kk + 1)], consts["identb"][:])
            b.copy("scalar", wdq_acc[s][:, 128 * kk:128 * (kk + 1)],
                   ps_trw[:, 128 * slot:128 * (slot + 1)])
        for kk in range(NI):
            b.dma("gpsimd", wdqT_dram[kk, :, 128 * j:128 * (j + 1)],
                  wdq_acc[s][:, 128 * kk:128 * (kk + 1)])
    b.pop()

    b.push()
    CH = [(0, 512), (512, 1024), (1024, 1536), (1536, 2048), (2048, 2560), (2560, 2816)]
    qbc = [quant_bufs(b), quant_bufs(b)]
    xq_l = [b.sb("xq_l", (128, E), BF16), b.sb("xq_l2", (128, E), BF16)]
    ps_g = [b.ps(f"ps_g{i}", (128, 512)) for i in range(3)]
    ps_u = [b.ps(f"ps_u{i}", (128, 512)) for i in range(3)]
    ps_trh = b.ps("ps_trh", (128, 512), BF16)
    hsil = [b.sb("hsil", (128, 512)), b.sb("hsil2", (128, 512))]
    usq = [b.sb("usq", (128, 512)), b.sb("usq2", (128, 512))]
    _ht = b.sb("h_t", (128, I))
    h_t = [_ht, _ht]
    _ha = b.sb("hq_acc", (128, I), BF16)
    hq_acc = [_ha, _ha]
    post = [None, None]
    for t in range(NT):
        s = t % 2
        b.dma("sync", xq_l[s][:], xqT_dram[t])
        for half in range(2):
            chs = CH[3 * half:3 * half + 3]
            for k in range(NE):
                lhsT = xq_l[s][:, 128 * k:128 * (k + 1)]
                for ci, (c0, c1) in enumerate(chs):
                    b.mm(ps_g[ci][:, 0:c1 - c0], lhsT, wgqT[k][:, c0:c1],
                         start=(k == 0), stop=(k == NE - 1))
                    b.mm(ps_u[ci][:, 0:c1 - c0], lhsT, wuqT[k][:, c0:c1],
                         start=(k == 0), stop=(k == NE - 1))
            if half == 0 and post[0] is not None:
                post[0]()
                post[0] = None
            for ci, (c0, c1) in enumerate(chs):
                w = c1 - c0
                b.act("scalar", hsil[ci % 2][:, 0:w], ps_g[ci][:, 0:w], AF.Silu)
                b.copy("scalar", usq[ci % 2][:, 0:w], ps_u[ci][:, 0:w])
                b.tt("gpsimd", h_t[s][:, c0:c1], hsil[ci % 2][:, 0:w], usq[ci % 2][:, 0:w], ALU.mult)
        hq_t = qbc[s]["xn"]
        emit_quant(b, hq_t[:, 0:I], h_t[s][:], I // 16, qbc[s])

        def mkpost(t=t, s=s, hq_t=hq_t):
            def post_fn():
                for kk0 in range(0, NI, 4):
                    n4 = min(4, NI - kk0)
                    for slot in range(n4):
                        b.trans(ps_trh[:, 128 * slot:128 * (slot + 1)],
                                hq_t[:, 128 * (kk0 + slot):128 * (kk0 + slot + 1)],
                                consts["identb"][:])
                    b.copy("scalar", hq_acc[s][:, 128 * kk0:128 * (kk0 + n4)],
                           ps_trh[:, 0:128 * n4])
                b.dma("scalar", hqT_dram[t], hq_acc[s][:])
            return post_fn
        post[0], post[1] = post[1], mkpost()
    for p in post:
        if p is not None:
            p()
    b.pop()

    b.push()
    wdqT = [b.sb(f"wdqT{kk}", (128, E), BF16) for kk in range(NI)]
    for kk in range(NI):
        b.dma("sync", wdqT[kk][:], wdqT_dram[kk])
    hq_l = [b.sb("hq_l", (128, I), BF16), b.sb("hq_l2", (128, I), BF16)]
    x_t2 = [b.sb("x_t2", (128, E)), b.sb("x_t2b", (128, E))]
    ps_o0 = b.ps("ps_o0", (128, 512))
    ps_o1 = b.ps("ps_o1", (128, 512))
    for t in range(NT):
        s = t % 2
        b.dma("sync", hq_l[s][:], hqT_dram[t])
        b.dma("sync", x_t2[s][:], x_in[128 * t:128 * (t + 1), :])
        for kk in range(NI):
            lhsT = hq_l[s][:, 128 * kk:128 * (kk + 1)]
            b.mm(ps_o0[:], lhsT, wdqT[kk][:, 0:512], start=(kk == 0), stop=(kk == NI - 1))
            b.mm(ps_o1[:], lhsT, wdqT[kk][:, 512:1024], start=(kk == 0), stop=(kk == NI - 1))
        b.tt("vector", x_t2[s][:, 0:512], x_t2[s][:, 0:512], ps_o0[:], ALU.add)
        b.tt("vector", x_t2[s][:, 512:1024], x_t2[s][:, 512:1024], ps_o1[:], ALU.add)
        b.dma("scalar", x_out[128 * t:128 * (t + 1), :], x_t2[s][:])
    b.pop()
    if standalone:
        return b.done()
    return None


def l1_ffn_inputs(x_core, inputs):
    return {"x": np.ascontiguousarray(x_core),
            "wg": np.ascontiguousarray(inputs["L1_ffn_w_gate"]),
            "wu": np.ascontiguousarray(inputs["L1_ffn_w_up"]),
            "wd": np.ascontiguousarray(inputs["L1_ffn_w_down"]),
            **host_consts(rb=False)}


def build_module():
    b = B("mod")
    x0 = b.inp("x", (T, E))
    y = b.outp("y", (T, E))
    x1 = b.dram("x1_st", (T, E))
    x2 = b.dram("x2_st", (T, E))
    x3 = b.dram("x3_st", (T, E))
    consts = emit_consts(b, rb=True)
    for fn, xi, xo in ((build_l0_attn, x0, x1), (build_l0_ffn, x1, x2),
                       (build_l1_attn, x2, x3), (build_l1_ffn, x3, y)):
        depth = len(b.scopes)
        b.push()
        fn(b, consts, xi, xo)
        while len(b.scopes) > depth:
            b.pop()
    return b.done()


def module_inputs(x_core, inputs):
    d = {}
    d.update(l0_attn_inputs(x_core, inputs))
    d.update(l0_ffn_inputs(x_core, inputs))
    d.update(l1_attn_inputs(x_core, inputs))
    d.update(l1_ffn_inputs(x_core, inputs))
    d["x"] = np.ascontiguousarray(x_core)
    return d


def kernel(**inputs):
    x = np.ascontiguousarray(inputs["L0_attn_hidden_states"], dtype=np.float32)
    n_cores = 8
    nc = build_module()
    in_maps = [module_inputs(x[c], inputs) for c in range(n_cores)]
    res = run_bass_kernel_spmd(nc, in_maps, core_ids=list(range(n_cores)))
    return np.stack([np.asarray(res.results[c]["y"]) for c in range(n_cores)])


# version 35 (final)
# speedup vs baseline: 1.0100x; 1.0078x over previous
import numpy as np
import ml_dtypes
import concourse.bass as bass
import concourse.mybir as mybir
from concourse.bass_utils import run_bass_kernel_spmd
from contextlib import ExitStack

F32 = mybir.dt.float32
BF16 = mybir.dt.bfloat16
I32 = mybir.dt.int32
AF = mybir.ActivationFunctionType
ALU = mybir.AluOpType
AX = mybir.AxisListType

T = 4096
E = 1024
NT = T // 128
NE = E // 128
NEG = -1.0e30

ENGINES = ["sync", "vector", "scalar", "tensor", "gpsimd"]
DMA_ISSUERS = ["sync", "scalar", "gpsimd"]
DMA_SEMS_PER_QUEUE = 6


def _span(ap):
    lo = ap.offset
    hi = ap.offset
    for step, count in ap.ap:
        if count > 1:
            if step >= 0:
                hi += step * (count - 1)
            else:
                lo += step * (count - 1)
    return (ap.tensor.name, lo, hi + 1)


class FW:
    def __init__(self, nc):
        self.nc = nc
        self.prog = {e: [] for e in ENGINES}
        self.n = {e: 0 for e in ENGINES}
        self.marked = {e: set() for e in ENGINES}
        self.dma_sems = {e: [f"d_{e}_{k}" for k in range(DMA_SEMS_PER_QUEUE)]
                         for e in DMA_ISSUERS}
        self.dma_rr = {e: 0 for e in DMA_ISSUERS}
        self.dma_n = {s: 0 for e in DMA_ISSUERS for s in self.dma_sems[e]}
        self.mem = {}
        self.wm = {e: {} for e in ENGINES}
        self.fence_pending = []
        self.tensor_fences = {}
        self._finalized = False

    def _edges_for(self, reads, writes):
        edges = []

        def overlap(lst, lo, hi):
            return [(d, i) for (l, h, d, i) in lst if not (hi <= l or h <= lo)]

        spans_r = [_span(ap) for ap in reads]
        spans_w = [_span(ap) for ap in writes]
        for name, lo, hi in spans_r:
            rec = self.mem.get(name)
            if rec:
                edges += overlap(rec["w"], lo, hi)
        for name, lo, hi in spans_w:
            rec = self.mem.get(name)
            if rec:
                edges += overlap(rec["w"], lo, hi)
                edges += overlap(rec["r"], lo, hi)
        return edges, spans_r, spans_w

    def _record_access(self, spans_r, spans_w, domain, idx):
        for name, lo, hi in spans_r:
            rec = self.mem.setdefault(name, {"w": [], "r": []})
            rec["r"].append((lo, hi, domain, idx))
        for name, lo, hi in spans_w:
            rec = self.mem.setdefault(name, {"w": [], "r": []})
            rec["w"] = [e for e in rec["w"] if not (lo <= e[0] and e[1] <= hi)]
            rec["r"] = [e for e in rec["r"] if not (lo <= e[0] and e[1] <= hi)]
            rec["w"].append((lo, hi, domain, idx))

    def _resolve_edges(self, eng, edges):
        waits = []
        best = {}
        for d, i in edges:
            if d == "tensor" and eng == "tensor":
                continue
            if best.get(d, -1) < i:
                best[d] = i
        for d, i in sorted(best.items()):
            if self.wm[eng].get(d, -1) >= i:
                continue
            self.wm[eng][d] = i
            if d in ENGINES:
                self.marked[d].add(i)
            waits.append((d, i))
        return waits

    def emit(self, eng, emitfn, reads=(), writes=(), is_dma=False):
        assert not self._finalized
        edges, spans_r, spans_w = self._edges_for(reads, writes)
        if self.fence_pending:
            edges = list(edges) + list(self.fence_pending)
        if self.tensor_fences:
            for sp in spans_r + spans_w:
                f = self.tensor_fences.get(sp[0])
                if f:
                    edges = list(edges) + f
        waits = self._resolve_edges(eng, edges)
        if is_dma:
            k = self.dma_rr[eng]
            self.dma_rr[eng] = (k + 1) % DMA_SEMS_PER_QUEUE
            q = self.dma_sems[eng][k]
            if self.dma_n[q] > 0:
                waits += self._resolve_edges(eng, [(q, self.dma_n[q])])
            self.dma_n[q] += 16
            domain, idx = q, self.dma_n[q]
        else:
            domain, idx = eng, self.n[eng]
        self.prog[eng].append((waits, emitfn, {"domain": domain, "idx": idx,
                                               "is_dma": is_dma}))
        if not is_dma:
            self.n[eng] += 1
        self._record_access(spans_r, spans_w, domain, idx)
        return (domain, idx)

    def all_dma_tokens(self):
        return [(q, n) for q, n in self.dma_n.items() if n > 0]

    def fence(self):
        toks = self.fence_tokens()
        self.fence_pending = toks

    def fence_tokens(self):
        toks = [(e, self.n[e] - 1) for e in ENGINES if self.n[e] > 0]
        toks += self.all_dma_tokens()
        return toks

    def finalize_and_register(self, block, tail_wait=None):
        self._finalized = True
        nc = self.nc
        sems = {}
        for e in ENGINES:
            if self.marked[e]:
                sems[e] = nc.alloc_semaphore("sem_" + e)
        for q in self.dma_n:
            if self.dma_n[q]:
                sems[q] = nc.alloc_semaphore("sem_" + q)
        incval = {}
        for e in ENGINES:
            cum = {}
            c = 0
            for i in sorted(self.marked[e]):
                c += 1
                cum[i] = c
            incval[e] = cum

        def wait_value(d, i):
            if d in ENGINES:
                return incval[d][i]
            return i

        def make_section(ename):
            stream = self.prog[ename]
            marked = self.marked[ename]

            def section(engobj):
                for waits, emitfn, meta in stream:
                    for (d, i) in waits:
                        engobj.wait_ge(sems[d], wait_value(d, i))
                    if emitfn is None:
                        continue
                    inst = emitfn(engobj)
                    if meta["is_dma"]:
                        inst.then_inc(sems[meta["domain"]], 16)
                    elif meta["idx"] in marked:
                        inst.then_inc(sems[ename], 1)
                if tail_wait and tail_wait[0] == ename:
                    for (d, i) in tail_wait[1]:
                        engobj.wait_ge(sems[d], wait_value(d, i))
            return section

        for ename in ENGINES:
            if self.prog[ename] or (tail_wait and tail_wait[0] == ename):
                getattr(block, ename)(make_section(ename))


class B:

    def __init__(self, name):
        self.nc = bass.Bass("TRN2", target_bir_lowering=False)
        self.fw = FW(self.nc)
        self.stack = ExitStack()
        self.scopes = [self.stack]
        self.names = set()
        self.n_inst = 0

    def push(self):
        st = ExitStack()
        self.scopes.append(st)

    def pop(self):
        self._lazy_fence = self.fw.fence_tokens()
        self.scopes.pop().close()

    def _nm(self, base):
        i = 0
        nm = base
        while nm in self.names:
            i += 1
            nm = f"{base}_{i}"
        self.names.add(nm)
        return nm

    def inp(self, name, shape, dt=F32):
        return self.nc.dram_tensor(name, list(shape), dt, kind="ExternalInput").ap()

    def outp(self, name, shape, dt=F32):
        return self.nc.dram_tensor(name, list(shape), dt, kind="ExternalOutput").ap()

    def dram(self, name, shape, dt=F32):
        return self.nc.dram_tensor(name, list(shape), dt, kind="Internal").ap()

    def sb(self, name, shape, dt=F32):
        h = self.scopes[-1].enter_context(self.nc.sbuf_tensor(self._nm(name), list(shape), dt))
        if getattr(self, "_lazy_fence", None):
            self.fw.tensor_fences[h.name] = list(self._lazy_fence)
        return h

    def ps(self, name, shape, dt=F32):
        h = self.scopes[-1].enter_context(self.nc.psum_tensor(self._nm(name), list(shape), dt))
        if getattr(self, "_lazy_fence", None):
            self.fw.tensor_fences[h.name] = list(self._lazy_fence)
        return h

    def dma(self, eng, out, in_):
        return self.fw.emit(eng, lambda e: e.dma_start(out=out, in_=in_),
                            reads=[in_], writes=[out], is_dma=True)

    def mm(self, out, lhsT, rhs, start=True, stop=True, tile_position=None):
        return self.fw.emit("tensor",
                            lambda e: e.matmul(out, lhsT, rhs, start=start, stop=stop,
                                               tile_position=tile_position),
                            reads=[lhsT, rhs], writes=[out])

    def trans(self, out_ps, in_, ident):
        return self.fw.emit("tensor", lambda e: e.transpose(out_ps, in_, ident),
                            reads=[in_, ident], writes=[out_ps])

    def act(self, eng, out, in_, func, bias=0.0, scale=1.0, accum=None):
        r = [in_]
        w = [out]
        if isinstance(bias, bass.AP):
            r.append(bias)
        if isinstance(scale, bass.AP):
            r.append(scale)
        if accum is not None:
            w.append(accum)
        return self.fw.emit(eng, lambda e: e.activation(out, in_, func, bias=bias,
                                                        scale=scale, accum_out=accum),
                            reads=r, writes=w)

    def copy(self, eng, out, in_):
        if eng == "scalar":
            return self.fw.emit(eng, lambda e: e.copy(out, in_), reads=[in_], writes=[out])
        return self.fw.emit(eng, lambda e: e.tensor_copy(out, in_), reads=[in_], writes=[out])

    def tt(self, eng, out, in0, in1, op):
        return self.fw.emit(eng, lambda e: e.tensor_tensor(out, in0, in1, op=op),
                            reads=[in0, in1], writes=[out])

    def ts(self, eng, out, in0, s1, s2=None, op0=ALU.mult, op1=ALU.bypass):
        r = [in0]
        if isinstance(s1, bass.AP):
            r.append(s1)
        if isinstance(s2, bass.AP):
            r.append(s2)
        if s2 is None:
            return self.fw.emit(eng, lambda e: e.tensor_single_scalar(out, in0, s1, op=op0),
                                reads=r, writes=[out])
        return self.fw.emit(eng, lambda e: e.tensor_scalar(out, in0, s1, s2, op0=op0, op1=op1),
                            reads=r, writes=[out])

    def stt(self, eng, out, in0, s, in1, op0, op1):
        r = [in0, in1]
        if isinstance(s, bass.AP):
            r.append(s)
        return self.fw.emit(eng, lambda e: e.scalar_tensor_tensor(out, in0, s, in1,
                                                                  op0=op0, op1=op1),
                            reads=r, writes=[out])

    def red(self, eng, out, in_, op=ALU.max, absval=False, neg=False):
        return self.fw.emit(eng, lambda e: e.tensor_reduce(out, in_, axis=AX.X, op=op,
                                                           apply_absolute_value=absval,
                                                           negate=neg),
                            reads=[in_], writes=[out])

    def memset(self, eng, ap, val):
        return self.fw.emit(eng, lambda e: e.memset(ap, val), reads=[], writes=[ap])

    def smul(self, out, in_, c):
        return self.fw.emit("scalar", lambda e, o=out, i=in_: e.mul(o, i, c),
                            reads=[in_], writes=[out])

    def cpred(self, out, mask, data):
        return self.fw.emit("vector",
                            lambda e, o=out, m=mask, d=data: e.copy_predicated(o, m, d),
                            reads=[mask, data], writes=[out])

    def recip(self, out, in_):
        return self.fw.emit("vector", lambda e: e.reciprocal(out, in_),
                            reads=[in_], writes=[out])

    def done(self, tail_eng="sync"):
        while len(self.scopes) > 1:
            self.scopes.pop().close()
        block = self.stack.enter_context(self.nc.Block())
        self.fw.finalize_and_register(block, tail_wait=(tail_eng, self.fw.all_dma_tokens()))
        self.n_inst = sum(len(v) for v in self.fw.prog.values())
        self.stack.close()
        return self.nc


def alloc_xnT(b):
    xn_all = b.sb("xnT_all", (128, NE * T), BF16)
    return [xn_all[:, T * i:T * (i + 1)] for i in range(NE)]


def emit_ln(b, x_dram, xn_tiles, consts, fp32_sink=None, x_tiles_out=None):
    identb = consts["identb"]
    S = []
    for s in range(2):
        d = {}
        d["x_t"] = b.sb("ln_x", (128, E))
        d["sqs"] = b.sb("ln_sq", (128, E), BF16)
        d["ssq"] = b.sb("ln_ssq", (128, 1))
        d["srow"] = b.sb("ln_s", (128, 1))
        d["m"] = b.sb("ln_m", (128, 1))
        d["msq"] = b.sb("ln_msq", (128, 1))
        d["var"] = b.sb("ln_var", (128, 1))
        d["sd"] = b.sb("ln_sd", (128, 1))
        d["rs"] = b.sb("ln_rs", (128, 1))
        d["bl"] = b.sb("ln_bl", (128, 1))
        if xn_tiles is not None:
            d["xn"] = b.sb("ln_xn", (128, E), BF16)
        if fp32_sink:
            d["xnf"] = b.sb("ln_xnf", (128, E))
        S.append(d)
    eps_t = b.sb("ln_eps", (128, 1))
    b.memset("vector", eps_t[:], 1e-5)
    ps_tr = b.ps("ln_pstr", (128, 512), BF16) if xn_tiles is not None else None
    for t in range(NT):
        d = S[t % 2]
        x_t, sqs, ssq, srow = d["x_t"], d["sqs"], d["ssq"], d["srow"]
        m, msq, var, sd, rs, bl = d["m"], d["msq"], d["var"], d["sd"], d["rs"], d["bl"]
        b.dma("sync", x_t[:], x_dram[128 * t:128 * (t + 1), :])
        if x_tiles_out is not None:
            b.dma("scalar", x_tiles_out[128 * t:128 * (t + 1), :], x_t[:])
        b.act("scalar", sqs[:], x_t[:], AF.Square, accum=ssq[:, 0:1])
        b.red("vector", srow[:, 0:1], x_t[:], op=ALU.add)
        b.ts("vector", m[:], srow[:], 1.0 / E)
        b.tt("vector", msq[:], m[:], m[:], ALU.mult)
        b.stt("vector", var[:], ssq[:], 1.0 / E, msq[:], ALU.mult, ALU.subtract)
        b.act("scalar", sd[:], var[:], AF.Sqrt, bias=eps_t[:, 0:1])
        b.recip(rs[:], sd[:])
        b.stt("vector", bl[:], m[:], -1.0, rs[:], ALU.mult, ALU.mult)
        if xn_tiles is not None:
            xn = d["xn"]
            b.act("scalar", xn[:], x_t[:], AF.Identity, bias=bl[:, 0:1], scale=rs[:, 0:1])
        if fp32_sink:
            xnf = d["xnf"]
            b.act("scalar", xnf[:], x_t[:], AF.Identity, bias=bl[:, 0:1], scale=rs[:, 0:1])
            fp32_sink(t, xnf)
        if xn_tiles is not None:
            xn_all = xn_tiles[0].tensor
            for g4 in range(2):
                for slot in range(4):
                    i = 4 * g4 + slot
                    b.trans(ps_tr[:, 128 * slot:128 * (slot + 1)],
                            xn[:, 128 * i:128 * (i + 1)], identb[:])
                dst = bass.AP(xn_all, 4 * g4 * T + 128 * t,
                              [xn_tiles[0].ap[0], [T, 4], [1, 128]])
                b.fw.emit("scalar", lambda e, d=dst, s=ps_tr[:, 0:512]: e.copy(d, s),
                          reads=[ps_tr[:, 0:512]], writes=[dst])


def load_cast(b, dst_bf, src_dram, tmp, eng="gpsimd", scale=None):
    b.dma("sync", tmp[:], src_dram)
    if scale is None:
        b.copy(eng, dst_bf[:], tmp[:])
    else:
        b.smul(dst_bf[:], tmp[:], scale)


def emit_consts(b, rb=True):
    c = {}
    ident_in = b.inp("c_ident", (128, 128))
    c["identf"] = b.sb("identf", (128, 128))
    c["identb"] = b.sb("identb", (128, 128), BF16)
    b.dma("sync", c["identf"][:], ident_in)
    b.copy("gpsimd", c["identb"][:], c["identf"][:])
    if rb:
        rbkq_in = b.inp("c_rbkq", (128, 128))
        c["rbkqf"] = b.sb("rbkqf", (128, 128))
        c["rbkqb"] = b.sb("rbkqb", (128, 128), BF16)
        c["rbqkb"] = b.sb("rbqkb", (128, 128), BF16)
        c["fullnegb"] = b.sb("fullnegb", (128, 128), BF16)
        b.dma("sync", c["rbkqf"][:], rbkq_in)
        b.copy("gpsimd", c["rbkqb"][:], c["rbkqf"][:])
        b.memset("vector", c["fullnegb"][:], NEG)
        b.push()
        ps = b.ps("c_ps", (128, 128), BF16)
        b.trans(ps[:], c["rbkqb"][:], c["identb"][:])
        b.copy("scalar", c["rbqkb"][:], ps[:])
        b.pop()
    c["ones1"] = b.sb("ones1", (1, 128), BF16)
    b.memset("vector", c["ones1"][:], 1.0)
    return c


def host_consts(rb=True):
    ident = np.eye(128, dtype=np.float32)
    d = {"c_ident": ident}
    if rb:
        d["c_rbkq"] = np.where(np.arange(128)[:, None] > np.arange(128)[None, :],
                               np.float32(NEG), np.float32(0.0))
    return d


def flash_bufs(b):
    return {"mcol": [b.sb("fl_mcol", (128, 128)), b.sb("fl_mcol2", (128, 128))],
            "PT": [b.sb("fl_pt", (128, 1024), BF16), b.sb("fl_pt2", (128, 1024), BF16)],
            "rd": (lambda _r=b.sb("fl_rd", (1, 512)): [_r, _r])(),
            "rdb": (lambda _r=b.sb("fl_rdb", (1, 512), BF16): [_r, _r])(),
            "rsb": (lambda _r=b.sb("fl_rsb", (64, 512)): [_r, _r])()}


def flash_kmajor(b, consts, bufs, h_pair, ql_of, latKT_of, latV_of, kmax_of,
                 ctx_sink, sample_of, psR, ps_ctx2, sample_scale, qg_filler=None):
    identb = consts["identb"]
    rbkqb = consts["rbkqb"]
    mcol, PT, rd, rdb, rsb = (bufs["mcol"], bufs["PT"], bufs["rd"], bufs["rdb"],
                              bufs["rsb"])
    slots = list(enumerate(h_pair))
    ql = {}
    kts = {}
    for s, h in slots:
        ql[s] = ql_of(h)
        kts[s] = latKT_of(h)
    for qg in range(8):
        for tq in range(4):
            for s, h in slots:
                gt = 4 * qg + tq
                slot = 2 * tq + s
                reg = psR[0][:, 128 * slot:128 * (slot + 1)]
                s_lhsT, s_rhs, masked = sample_of(h, gt)
                b.mm(reg, s_lhsT, s_rhs, start=True, stop=not masked)
                if masked:
                    b.mm(reg, identb[:], consts["rbqkb"][:], start=False, stop=True)
                b.red("vector", mcol[s][:, 32 * tq:32 * tq + 1], reg,
                      op=ALU.max, neg=True)
        for s, h in slots:
            b.trans(psR[1][:, 128 * s:128 * (s + 1)], mcol[s][:], consts["identf"][:])
            for tq in range(4):
                b.copy("vector",
                       ql[s][64:65, 512 * qg + 128 * tq:512 * qg + 128 * (tq + 1)],
                       psR[1][32 * tq:32 * tq + 1, 128 * s:128 * (s + 1)])
    pending = []

    def flush_pending():
        while pending:
            pending.pop(0)()

    for qg in range(8):
        kmax = kmax_of(qg)
        jobs = []
        kt = 0
        while kt < kmax:
            g = min(2, kmax - kt)
            for s, _h in slots:
                jobs.append((s, kt, g))
            kt += g
        J = len(jobs)

        def st(j, qg=qg):
            s, kt0, g = jobs[j]
            ps = psR[j % 3]
            for jj in range(g):
                ktj = kt0 + jj
                d = ktj - 4 * qg
                extras = []
                if 0 <= d <= 3:
                    extras.append((128 * d, rbkqb))
                    for tq in range(max(0, d)):
                        extras.append((128 * tq, consts["fullnegb"]))
                b.mm(ps[:, 512 * jj:512 * (jj + 1)],
                     kts[s][:, 128 * ktj:128 * (ktj + 1)],
                     ql[s][:, 512 * qg:512 * (qg + 1)], start=True, stop=not extras)
                for ei, (off, mk) in enumerate(extras):
                    b.mm(ps[:, 512 * jj + off:512 * jj + off + 128],
                         identb[:], mk[:], start=False, stop=(ei == len(extras) - 1))

        def exp_pv(j, qg=qg, kmaxl=None):
            s, kt0, g = jobs[j]
            ps = psR[j % 3]
            pt = PT[s]
            h = h_pair[s]
            b.act("scalar", pt[:, 0:512 * g], ps[:, 0:512 * g], AF.Exp,
                  scale=sample_scale)
            for jj in range(g):
                ktj = kt0 + jj
                b.mm(ps_ctx2[s][0:65, :], latV_of(h, ktj), pt[:, 512 * jj:512 * (jj + 1)],
                     start=(ktj == 0), stop=(ktj == kmax - 1))

        st(0)
        if J > 1:
            st(1)
        flush_pending()
        if qg_filler is not None and qg == 4:
            qg_filler()
        if J > 2:
            st(2)
        for j in range(J):
            exp_pv(j)
            if j + 3 < J:
                st(j + 3)

        def mknorm(s, h, qg=qg):
            def norm():
                b.recip(rd[s][:], ps_ctx2[s][64:65, :])
                b.copy("vector", rdb[s][:], rd[s][:])
                b.mm(ps_ctx2[s][64:128, :], consts["ones1"][0:1, 0:64], rdb[s][:],
                     tile_position=(0, 64))
                b.copy("vector", rsb[s][:], ps_ctx2[s][64:128, :])
                ctx_sink(h, qg, ps_ctx2[s][0:64, :], rsb[s][:])
            return norm
        for s, h in slots:
            pending.append(mknorm(s, h))
    flush_pending()


def build_l0_attn(b=None, consts=None, x_in=None, x_out=None):
    standalone = b is None
    if standalone:
        b = B("l0a")
        x_in = b.inp("x", (T, E))
        x_out = b.outp("y", (T, E))
    wqT_in = b.inp("wqT", (E, E))
    wkvT_in = b.inp("wkvT", (E, 128))
    apk_in = b.inp("apk", (8, 128, 128))
    vup_in = b.inp("vup", (8, 128, 128))
    woutT_in = b.inp("woutT", (E, E))
    qlat_dram = b.dram("qlat_st", (16, 64, T), BF16)

    if consts is None:
        consts = emit_consts(b)
    wo = [b.sb(f"wo{k}", (128, E), BF16) for k in range(NE)]
    vup = b.sb("vup_outer", (128, 1024), BF16)
    latKT = b.sb("latKT", (65, T), BF16)
    latV = b.sb("latV", (128, 65 * NT), BF16)

    b.push()
    xn = alloc_xnT(b)
    emit_ln(b, x_in, xn, consts)

    wtmp = b.sb("wtmp", (128, E))
    wq = [b.sb(f"wq{k}", (128, E), BF16) for k in range(NE)]
    for k in range(NE):
        load_cast(b, wq[k], wqT_in[128 * k:128 * (k + 1), :], wtmp)
    wkv = [b.sb(f"wkv{k}", (128, 128), BF16) for k in range(NE)]
    wtmp2 = b.sb("wtmp2", (128, 128))
    for k in range(NE):
        load_cast(b, wkv[k], wkvT_in[128 * k:128 * (k + 1), :], wtmp2)
    apk = b.sb("apk_s", (128, 1024), BF16)
    atmp = b.sb("atmp", (128, 1024))
    for p in range(8):
        b.dma("sync", atmp[:, 128 * p:128 * (p + 1)], apk_in[p])
    b.smul(apk[:], atmp[:], 0.125)
    for p in range(8):
        b.dma("sync", atmp[:, 128 * p:128 * (p + 1)], vup_in[p])
    b.copy("gpsimd", vup[:], atmp[:])
    for k in range(NE):
        load_cast(b, wo[k], woutT_in[128 * k:128 * (k + 1), :], wtmp)

    ps_q = [b.ps("ps_q", (128, 512)), b.ps("ps_qb", (128, 512))]
    ps_l = [b.ps("ps_l", (128, 512)), b.ps("ps_lb", (128, 512))]
    qt_pair = [b.sb("qt_pair", (128, 512), BF16), b.sb("qt_pairb", (128, 512), BF16)]
    ql_sb = [b.sb("ql_sb", (128, 512), BF16), b.sb("ql_sbb", (128, 512), BF16)]
    for p in range(8):
        for n in range(8):
            w = n % 2
            for k in range(NE):
                b.mm(ps_q[w][:], wq[k][:, 128 * p:128 * (p + 1)],
                     xn[k][:, 512 * n:512 * (n + 1)], start=(k == 0), stop=(k == NE - 1))
            b.copy("vector", qt_pair[w][:], ps_q[w][:])
            b.mm(ps_l[w][:], apk[:, 128 * p:128 * (p + 1)], qt_pair[w][:])
            b.copy("vector", ql_sb[w][:], ps_l[w][:])
            b.dma("scalar", qlat_dram[2 * p, :, 512 * n:512 * (n + 1)], ql_sb[w][0:64, :])
            b.dma("scalar", qlat_dram[2 * p + 1, :, 512 * n:512 * (n + 1)], ql_sb[w][64:128, :])

    latVf = b.sb("latVf", (64, T), BF16)
    b.memset("vector", latKT[64:65, :], 1.0)
    b.memset("vector", latV[:], 1.0)
    ps_kv = [b.ps("ps_kv", (128, 512)), b.ps("ps_kvb", (128, 512))]
    for n in range(8):
        w = n % 2
        for k in range(NE):
            b.mm(ps_kv[w][:], wkv[k][:], xn[k][:, 512 * n:512 * (n + 1)],
                 start=(k == 0), stop=(k == NE - 1))
        b.copy("scalar", latKT[0:64, 512 * n:512 * (n + 1)], ps_kv[w][0:64, :])
        b.copy("scalar", latVf[:, 512 * n:512 * (n + 1)], ps_kv[w][64:128, :])
    ps_tv = b.ps("ps_tv", (128, 128), BF16)
    for t in range(NT):
        b.trans(ps_tv[:, 0:64], latVf[:, 128 * t:128 * (t + 1)], consts["identb"][0:64, 0:64])
        b.copy("scalar", latV[:, 65 * t:65 * t + 64], ps_tv[:, 0:64])

    b.pop()
    b.push()
    headsT = [b.sb(f"headsT{p}", (128, T), BF16) for p in range(8)]
    b.push()
    ql_buf = [b.sb(f"ql_buf{i}", (65, T), BF16) for i in range(4)]
    psR = [b.ps("psR0", (128, 1024)), b.ps("psR1", (128, 1024)),
           b.ps("psR2", (128, 1024))]
    ps_ctx2 = [b.ps("ps_ctx", (128, 512)), b.ps("ps_ctxB", (128, 512))]
    ctxn = b.sb("ctxn", (128, T), BF16)
    ps_vu = ps_ctx2[0]

    loaded = set()

    def load_h(h):
        if h in loaded or h > 15:
            return
        loaded.add(h)
        b.dma("sync", ql_buf[h % 4][0:64, :], qlat_dram[h])

    def ql_of(h):
        load_h(h)
        return ql_buf[h % 4]

    def sample_of(h, gt):
        buf = ql_buf[h % 4]
        stride = max(1, gt)
        lhsT = buf[0:64, 128 * gt:128 * (gt + 1)]
        rhs = latKT[0:64, 0:128 * stride:stride] if gt > 0 else latKT[0:64, 0:128]
        return lhsT, rhs, gt == 0

    def ctx_sink(h, qg, ctx_rows, rsb):
        dst = ctxn[64 * (h % 2):64 * (h % 2) + 64, 512 * qg:512 * (qg + 1)]
        b.tt("vector", dst, ctx_rows, rsb[:], ALU.mult)

    fbufs = flash_bufs(b)
    for p in range(8):
        flash_kmajor(b, consts, fbufs, [2 * p, 2 * p + 1], ql_of,
                     lambda hh: latKT,
                     lambda hh, kt: latV[:, 65 * kt:65 * (kt + 1)],
                     lambda qg: 4 * (qg + 1), ctx_sink, sample_of,
                     psR, ps_ctx2, 1.0,
                     qg_filler=(lambda p=p: (load_h(2 * p + 2), load_h(2 * p + 3))))
        for n in range(8):
            b.mm(ps_vu[:], vup[:, 128 * p:128 * (p + 1)], ctxn[:, 512 * n:512 * (n + 1)])
            b.copy("vector", headsT[p][:, 512 * n:512 * (n + 1)], ps_vu[:])

    b.pop()
    b.push()
    x_t2 = [b.sb("x_t2", (128, E)), b.sb("x_t2b", (128, E))]
    ps_o = [b.ps(f"ps_o{i}", (128, 512)) for i in range(4)]
    for t in range(NT):
        w = t % 2
        b.dma("sync", x_t2[w][:], x_in[128 * t:128 * (t + 1), :])
        for k in range(NE):
            b.mm(ps_o[2 * w][:], headsT[k][:, 128 * t:128 * (t + 1)], wo[k][:, 0:512],
                 start=(k == 0), stop=(k == NE - 1))
            b.mm(ps_o[2 * w + 1][:], headsT[k][:, 128 * t:128 * (t + 1)], wo[k][:, 512:1024],
                 start=(k == 0), stop=(k == NE - 1))
        b.tt("vector", x_t2[w][:, 0:512], x_t2[w][:, 0:512], ps_o[2 * w][:], ALU.add)
        b.tt("vector", x_t2[w][:, 512:1024], x_t2[w][:, 512:1024], ps_o[2 * w + 1][:], ALU.add)
        b.dma("scalar", x_out[128 * t:128 * (t + 1), :], x_t2[w][:])
    b.pop()
    if standalone:
        return b.done()
    return None


def l0_attn_inputs(x_core, inputs):
    return {
        "x": np.ascontiguousarray(x_core),
        "wqT": np.ascontiguousarray(inputs["L0_attn_Wq"].T),
        "wkvT": np.ascontiguousarray(np.concatenate(
            [inputs["L0_attn_Wk_down"].T, inputs["L0_attn_Wv_down"].T], axis=1)),
        "apk": pack_pairs_stack(inputs["L0_attn_q_to_latent"]),
        "vup": pack_pairs_diag(inputs["L0_attn_v_up"]),
        "woutT": np.ascontiguousarray(inputs["L0_attn_Wout"].T),
        **host_consts(),
    }


def pack_pairs_stack(A):
    out = np.zeros((8, 128, 128), dtype=np.float32)
    for p in range(8):
        out[p, 0:64, 0:64] = A[2 * p]
        out[p, 64:128, 64:128] = A[2 * p + 1]
    return out


def pack_pairs_diag(V):
    return pack_pairs_stack(V)




def build_l0_ffn(b=None, consts=None, x_in=None, x_out=None):
    standalone = b is None
    if standalone:
        b = B("l0f")
        x_in = b.inp("x", (T, E))
        x_out = b.outp("y", (T, E))
    wins = {}
    for pfx in "qkv":
        wins[pfx] = {
            "w1T": b.inp(f"{pfx}_w1T", (E, E)),
            "w2": b.inp(f"{pfx}_w2", (E, E)),
            "nwT": b.inp(f"{pfx}_nwT", (E, 15)),
            "nb": b.inp(f"{pfx}_nb", (1, 15)),
            "b1": b.inp(f"{pfx}_b1", (1, E)),
            "b2": b.inp(f"{pfx}_b2", (16, E)),
        }
    iota16_in = b.inp("iota16", (16, 1))
    e2_in = b.inp("e2", (16, E))
    hid_dram = b.dram("hid_st", (NE, 128, T), BF16)
    qT_dram = b.dram("qT_st", (NE, 128, T), BF16)
    kT_dram = b.dram("kT_st", (NE, 128, T), BF16)
    vT_dram = b.dram("vT_st", (NE, 128, T), BF16)

    if consts is None:
        consts = emit_consts(b, rb=False)
    onesrow_b = b.sb("onesrow_b", (1, 512), BF16)
    b.memset("vector", onesrow_b[:], 1.0)
    onesrow_f = b.sb("onesrow_f", (1, 512))
    b.memset("vector", onesrow_f[:], 1.0)
    onescol_b = b.sb("onescol_b", (128, 1), BF16)
    b.memset("vector", onescol_b[:], 1.0)
    iota16 = b.sb("iota16_s", (16, 1))
    b.dma("sync", iota16[:], iota16_in)
    kTs = [b.sb(f"kTs{k}", (128, 128), BF16) for k in range(NE)]

    b.push()
    xn = alloc_xnT(b)
    S_tok45 = b.sb("stok45", (128, 45 * NT))
    onehotT = {p: b.sb(f"oh_{p}", (16, T), BF16)[:, :] for p in "qkv"}
    e2b = b.sb("e2b", (16, E), BF16)
    e2t = b.sb("e2t", (16, E))
    b.dma("sync", e2t[:], e2_in)
    b.copy("gpsimd", e2b[:], e2t[:])
    b.push()
    nwT = [b.sb(f"nwTall{k}", (128, 45)) for k in range(NE)]
    for k in range(NE):
        for fi, pfx in enumerate("qkv"):
            b.dma("sync", nwT[k][:, 15 * fi:15 * (fi + 1)],
                  wins[pfx]["nwT"][128 * k:128 * (k + 1), :])
    nb_all = b.sb("nb_all", (1, 45))
    for fi, pfx in enumerate("qkv"):
        b.dma("sync", nb_all[0:1, 15 * fi:15 * (fi + 1)], wins[pfx]["nb"])
    xnTf = [[b.sb(f"xnTf{k}", (128, 512)) for k in range(NE)],
            [b.sb(f"xnTfb{k}", (128, 512)) for k in range(NE)]]
    ps_trf = b.ps("ps_trf", (128, 512))
    ps_sc = b.ps("ps_sc", (128, 512))
    sc_ev = b.sb("sc_ev", (45, 512))

    def fp32_sink(t, xnf):
        n = t // 4
        j = t % 4
        w = (t // 4) % 2
        for i in range(NE):
            slot = i % 4
            b.trans(ps_trf[:, 128 * slot:128 * (slot + 1)],
                    xnf[:, 128 * i:128 * (i + 1)], consts["identf"][:])
            b.copy("scalar", xnTf[w][i][:, 128 * j:128 * (j + 1)],
                   ps_trf[:, 128 * slot:128 * (slot + 1)])
        if j == 3:
            for k in range(NE):
                b.mm(ps_sc[0:45, 0:512], nwT[k][:], xnTf[w][k][:],
                     start=(k == 0), stop=False)
            b.mm(ps_sc[0:45, 0:512], nb_all[:], onesrow_f[:],
                 start=False, stop=True)
            b.copy("scalar", sc_ev[:], ps_sc[0:45, 0:512])
            for j2 in range(4):
                tt = 4 * n + j2
                b.trans(ps_trf[0:128, 0:45], sc_ev[:, 128 * j2:128 * (j2 + 1)],
                        consts["identf"][0:45, 0:45])
                b.copy("scalar", S_tok45[:, 45 * tt:45 * (tt + 1)],
                       ps_trf[0:128, 0:45])
    emit_ln(b, x_in, xn, consts, fp32_sink=fp32_sink)

    lf_dram = b.dram("lf_st", (3, 1, T), F32)
    g = [b.sb(f"rt_g{i}", (128, 32)) for i in range(4)]
    gi = [b.sb(f"rt_gi{i}", (128, 32), I32) for i in range(3)]
    sel1 = b.sb("rt_s1", (128, 32))
    sel2 = b.sb("rt_s2", (128, 32))
    sel3 = b.sb("rt_s3", (128, 32))
    tmpa = b.sb("rt_a", (128, 32))
    tmpb = b.sb("rt_b", (128, 32))
    leaf = b.sb("rt_leaf", (128, 32))
    lf_t = b.sb("rt_lft", (32, 128))
    lfrow = b.sb("rt_lfrow", (1, T))
    lfrow_b = b.sb("rt_lfrowb", (1, T), BF16)
    ps_rt = b.ps("ps_rt", (128, 512))

    def sel(out, cond_i32, on_true, on_false):
        b.copy("vector", out, on_false)
        b.cpred(out, cond_i32, on_true)

    for fi, pfx in enumerate("qkv"):
        sc = lambda i: S_tok45[:, 15 * fi + i::45]
        b.ts("vector", g[0][:], sc(0), 0.0, op0=ALU.is_ge)
        b.ts("vector", gi[0][:], sc(0), 0.0, op0=ALU.is_ge)
        sel(sel1[:], gi[0][:], sc(2), sc(1))
        b.ts("vector", g[1][:], sel1[:], 0.0, op0=ALU.is_ge)
        b.ts("vector", gi[1][:], sel1[:], 0.0, op0=ALU.is_ge)
        sel(tmpa[:], gi[0][:], sc(6), sc(4))
        sel(tmpb[:], gi[0][:], sc(5), sc(3))
        sel(sel2[:], gi[1][:], tmpa[:], tmpb[:])
        b.ts("vector", g[2][:], sel2[:], 0.0, op0=ALU.is_ge)
        b.ts("vector", gi[2][:], sel2[:], 0.0, op0=ALU.is_ge)
        sel(tmpa[:], gi[2][:], sc(8), sc(7))
        sel(tmpb[:], gi[2][:], sc(10), sc(9))
        sel(sel3[:], gi[1][:], tmpb[:], tmpa[:])
        sel(tmpa[:], gi[2][:], sc(12), sc(11))
        sel(tmpb[:], gi[2][:], sc(14), sc(13))
        sel(sel1[:], gi[1][:], tmpb[:], tmpa[:])
        sel(sel2[:], gi[0][:], sel1[:], sel3[:])
        b.ts("vector", g[3][:], sel2[:], 0.0, op0=ALU.is_ge)
        b.stt("vector", leaf[:], g[0][:], 2.0, g[1][:], ALU.mult, ALU.add)
        b.stt("vector", leaf[:], leaf[:], 2.0, g[2][:], ALU.mult, ALU.add)
        b.stt("vector", leaf[:], leaf[:], 2.0, g[3][:], ALU.mult, ALU.add)
        b.trans(ps_rt[0:32, 0:128], leaf[:], consts["identf"][:])
        b.copy("scalar", lf_t[:], ps_rt[0:32, 0:128])
        b.dma("scalar", lf_dram[fi].rearrange("o (a b) -> (o a) b", a=32), lf_t[:])
        b.dma("sync", lfrow[:], lf_dram[fi])
        b.copy("vector", lfrow_b[:], lfrow[:])
        for n in range(8):
            b.mm(ps_rt[0:16, 0:512], consts["ones1"][0:1, 0:16],
                 lfrow_b[0:1, 512 * n:512 * (n + 1)])
            b.ts("vector", onehotT[pfx][:, 512 * n:512 * (n + 1)],
                 ps_rt[0:16, 0:512], iota16[:, 0:1], op0=ALU.is_equal)
    b.pop()

    hb = {}
    for pfx in "qkv":
        hb[pfx] = {}
        bb = hb[pfx]
        bb["b1hi"] = b.sb(f"b1hi_{pfx}", (1, E), BF16)
        bb["b1lo"] = b.sb(f"b1lo_{pfx}", (1, E), BF16)
        bb["b2hi"] = b.sb(f"b2hi_{pfx}", (16, E), BF16)
        bb["b2lo"] = b.sb(f"b2lo_{pfx}", (16, E), BF16)
    b.push()
    btmp = b.sb("btmp", (16, E))
    btmp2 = b.sb("btmp2", (16, E))
    for pfx in "qkv":
        bb = hb[pfx]
        b.dma("sync", btmp[0:1, :], wins[pfx]["b1"])
        b.copy("vector", bb["b1hi"][:], btmp[0:1, :])
        b.tt("vector", btmp2[0:1, :], btmp[0:1, :], bb["b1hi"][:], ALU.subtract)
        b.copy("vector", bb["b1lo"][:], btmp2[0:1, :])
        b.dma("sync", btmp[:], wins[pfx]["b2"])
        b.copy("vector", bb["b2hi"][:], btmp[:])
        b.tt("vector", btmp2[:], btmp[:], bb["b2hi"][:], ALU.subtract)
        b.copy("vector", bb["b2lo"][:], btmp2[:])
    b.pop()

    ps_p = [b.ps(f"ps_p{i}", (128, 512)) for i in range(NE)]
    w1set = [[b.sb(f"w1a{k}", (128, E), BF16) for k in range(NE)],
             [b.sb(f"w1b{k}", (128, E), BF16) for k in range(NE)]]
    w2 = [b.sb(f"w2s{k}", (128, E), BF16) for k in range(NE)]
    hev = [b.sb(f"hev{i}", (128, 512), BF16) for i in range(4)]
    hch = [[b.sb(f"hch{k}", (128, 512), BF16) for k in range(NE)],
           [b.sb(f"hchb{k}", (128, 512), BF16) for k in range(NE)]]
    pev = [b.sb("pev", (128, 512), BF16), b.sb("pevb", (128, 512), BF16)]
    for fi, pfx in enumerate("qkv"):
        w1 = w1set[fi % 2]
        for k in range(NE):
            b.dma("gpsimd", w1[k][:], wins[pfx]["w1T"][128 * k:128 * (k + 1), :])
        for k in range(NE):
            b.dma("gpsimd", w2[k][:], wins[pfx]["w2"][128 * k:128 * (k + 1), :])
        for n in range(8):
            for i in range(NE):
                w = (n * NE + i) % 4
                for k in range(NE):
                    b.mm(ps_p[w][:], w1[k][:, 128 * i:128 * (i + 1)],
                         xn[k][:, 512 * n:512 * (n + 1)], start=(k == 0), stop=False)
                b.mm(ps_p[w][:], hb[pfx]["b1hi"][0:1, 128 * i:128 * (i + 1)], onesrow_b[:],
                     start=False, stop=False)
                b.mm(ps_p[w][:], e2b[:, 128 * i:128 * (i + 1)],
                     onehotT[pfx][:, 512 * n:512 * (n + 1)], start=False, stop=True)
                b.act("scalar", hev[w][:], ps_p[w][:], AF.Relu)
                b.dma("scalar", hid_dram[i, :, 512 * n:512 * (n + 1)], hev[w][:])
        for n in range(8):
            w = n % 2
            for k in range(NE):
                b.dma("sync", hch[w][k][:], hid_dram[k, :, 512 * n:512 * (n + 1)])
            for i in range(NE):
                for k in range(NE):
                    b.mm(ps_p[i][:], w2[k][:, 128 * i:128 * (i + 1)], hch[w][k][:],
                         start=(k == 0), stop=False)
                b.mm(ps_p[i][:], hb[pfx]["b2hi"][:, 128 * i:128 * (i + 1)],
                     onehotT[pfx][:, 512 * n:512 * (n + 1)], start=False, stop=True)
            for i in range(NE):
                if pfx == "q":
                    b.smul(pev[w][:], ps_p[i][:], 0.03125)
                    b.dma("scalar", qT_dram[i, :, 512 * n:512 * (n + 1)], pev[w][:])
                elif pfx == "k":
                    b.copy("scalar", pev[w][:], ps_p[i][:])
                    b.dma("scalar", kT_dram[i, :, 512 * n:512 * (n + 1)], pev[w][:])
                    b.copy("vector", kTs[i][:, 16 * n:16 * (n + 1)], pev[w][:, 0:512:32])
                else:
                    b.copy("scalar", pev[w][:], ps_p[i][:])
                    b.dma("scalar", vT_dram[i, :, 512 * n:512 * (n + 1)], pev[w][:])

    b.pop()

    b.push()
    v_tok = b.sb("v_tok", (128, NT * E), BF16)
    b.push()
    vch = b.sb("vch", (128, T), BF16)
    ps_tv2 = b.ps("ps_tv2", (128, 128), BF16)
    for i in range(NE):
        b.dma("sync", vch[:], vT_dram[i])
        for tt in range(NT):
            b.trans(ps_tv2[:], vch[:, 128 * tt:128 * (tt + 1)], consts["identb"][:])
            b.copy("scalar",
                   v_tok[:, 1024 * tt + 128 * i:1024 * tt + 128 * (i + 1)],
                   ps_tv2[:])
    b.pop()

    qTg2 = [[b.sb(f"qTg{k}", (128, 512), BF16) for k in range(NE)],
            [b.sb(f"qTgb{k}", (128, 512), BF16) for k in range(NE)]]
    kTg = [b.sb(f"kTg{k}", (128, T), BF16) for k in range(NE)]
    mcol = b.sb("at_mcol", (128, 128))
    negm = b.sb("at_negm", (1, 512), BF16)
    PT_half2 = [b.sb("at_pthalf", (128, 512 * 16), BF16),
                b.sb("at_pthalf2", (128, 512 * 16), BF16)]
    ps_bigA = b.ps("at_psbigA", (128, 1024))
    ps_bigB = b.ps("at_psbigB", (128, 1024))
    ps_ctx = b.ps("at_psctx", (128, 1280))
    ps_sm = b.ps("at_pssm", (128, 512))
    rr = b.sb("at_rr", (128, 1))
    ctx_acc = b.sb("at_ctxacc", (128, 4100), BF16)
    x_t32 = [b.sb("at_xt", (128, E)), b.sb("at_xt2", (128, E))]
    for k in range(NE):
        b.dma("sync", kTg[k][:], kT_dram[k])
    pend = [None]
    for tq in range(2):
        b.dma("gpsimd", x_t32[tq][:], x_in[128 * tq:128 * (tq + 1), :])
    for qg in range(8):
        qTg = qTg2[qg % 2]
        for k in range(NE):
            b.dma("sync", qTg[k][:], qT_dram[k, :, 512 * qg:512 * (qg + 1)])
        for tq in range(4):
            for k in range(NE):
                b.mm(ps_sm[:, 128 * tq:128 * (tq + 1)],
                     qTg[k][:, 128 * tq:128 * (tq + 1)], kTs[k][:],
                     start=(k == 0), stop=(k == NE - 1))
            b.red("vector", mcol[:, 32 * tq:32 * tq + 1],
                  ps_sm[:, 128 * tq:128 * (tq + 1)], op=ALU.max, neg=True)
        if pend[0] is not None:
            pend[0]()
            pend[0] = None
        b.trans(ps_ctx[:, 1152:1280], mcol[:], consts["identf"][:])
        for tq in range(4):
            b.copy("scalar", negm[0:1, 128 * tq:128 * (tq + 1)],
                   ps_ctx[32 * tq:32 * tq + 1, 1152:1280])
        for half in range(2):
            kt0 = 16 * half
            PT_half = PT_half2[half]

            def emit_st(gi):
                ps = ps_bigA if gi % 2 == 0 else ps_bigB
                for j in range(2):
                    ktj = kt0 + 2 * gi + j
                    for k in range(NE):
                        b.mm(ps[:, 512 * j:512 * (j + 1)],
                             kTg[k][:, 128 * ktj:128 * (ktj + 1)], qTg[k][:],
                             start=(k == 0), stop=False)
                    b.mm(ps[:, 512 * j:512 * (j + 1)], onesrow_b[0:1, 0:128],
                         negm[:], start=False, stop=True)

            def emit_exp(gi):
                ps = ps_bigA if gi % 2 == 0 else ps_bigB
                b.act("scalar", PT_half[:, 1024 * gi:1024 * (gi + 1)],
                      ps[:, 0:1024], AF.Exp)

            emit_st(0)
            for gi in range(8):
                if gi + 1 < 8:
                    emit_st(gi + 1)
                emit_exp(gi)
            for tq in range(4):
                for kt in range(kt0, kt0 + 16):
                    lhsT = PT_half[:, 512 * (kt - kt0) + 128 * tq:
                                   512 * (kt - kt0) + 128 * (tq + 1)]
                    va = v_tok[:, 1024 * kt:1024 * (kt + 1)]
                    b.mm(ps_ctx[:, 0:512], lhsT, va[:, 0:512],
                         start=(kt == kt0), stop=(kt == kt0 + 15))
                    b.mm(ps_ctx[:, 512:1024], lhsT, va[:, 512:1024],
                         start=(kt == kt0), stop=(kt == kt0 + 15))
                    b.mm(ps_ctx[:, 1024:1025], lhsT, onescol_b[:, 0:1],
                         start=(kt == kt0), stop=(kt == kt0 + 15))
                if half == 0:
                    b.copy("vector", ctx_acc[:, 1025 * tq:1025 * (tq + 1)],
                           ps_ctx[:, 0:1025])
                else:
                    b.tt("vector", ctx_acc[:, 1025 * tq:1025 * (tq + 1)],
                         ctx_acc[:, 1025 * tq:1025 * (tq + 1)], ps_ctx[:, 0:1025],
                         ALU.add)
        def mktail(qg=qg):
            def tail():
                for tq in range(4):
                    t_glob = 4 * qg + tq
                    xb = x_t32[tq % 2]
                    b.recip(rr[:], ctx_acc[:, 1025 * tq + 1024:1025 * tq + 1025])
                    b.stt("vector", xb[:], ctx_acc[:, 1025 * tq:1025 * tq + 1024],
                          rr[:, 0:1], xb[:], ALU.mult, ALU.add)
                    b.dma("scalar", x_out[128 * t_glob:128 * (t_glob + 1), :], xb[:])
                    nt = t_glob + 2
                    if nt < T // 128:
                        b.dma("gpsimd", x_t32[tq % 2][:],
                              x_in[128 * nt:128 * (nt + 1), :])
            return tail
        pend[0] = mktail()
    pend[0]()
    b.pop()
    if standalone:
        return b.done()
    return None


def l0_ffn_inputs(x_core, inputs):
    BIG = 4096.0
    ind = (np.arange(16)[:, None] == (np.arange(E) // 64)[None, :])
    e2 = (BIG * (ind.astype(np.float32) - 1.0)).astype(np.float32)
    d = {"x": np.ascontiguousarray(x_core),
         "iota16": np.arange(16, dtype=np.float32).reshape(16, 1),
         "e2": e2, **host_consts(rb=False)}
    for pfx in "qkv":
        d[f"{pfx}_w1T"] = np.ascontiguousarray(
            inputs[f"L0_ffn_{pfx}_w1"].transpose(1, 0, 2).reshape(E, E))
        d[f"{pfx}_w2"] = np.ascontiguousarray(
            inputs[f"L0_ffn_{pfx}_w2"].reshape(E, E))
        d[f"{pfx}_nwT"] = np.ascontiguousarray(inputs[f"L0_ffn_{pfx}_node_w"].T)
        d[f"{pfx}_nb"] = np.ascontiguousarray(
            inputs[f"L0_ffn_{pfx}_node_b"].reshape(1, 15))
        d[f"{pfx}_b1"] = np.ascontiguousarray(
            inputs[f"L0_ffn_{pfx}_b1"].reshape(1, E))
        d[f"{pfx}_b2"] = np.ascontiguousarray(inputs[f"L0_ffn_{pfx}_b2"])
    return d




def build_l1_attn(b=None, consts=None, x_in=None, x_out=None):
    standalone = b is None
    if standalone:
        b = B("l1a")
        x_in = b.inp("x", (T, E))
        x_out = b.outp("y", (T, E))
    wattn_in = b.inp("wattn", (E, 3 * E))
    battn_in = b.inp("battn", (1, 3 * E))
    wproj_in = b.inp("wproj", (E, E))
    bproj_in = b.inp("bproj", (1, E))
    cosf_in = b.inp("cosf", (128, 512))
    sinf_in = b.inp("sinf", (128, 512))
    qh_dram = b.dram("qh_st", (16, 64, T), BF16)
    kh_dram = b.dram("kh_st", (16, 64, T), BF16)
    v_dram = b.dram("v_st", (T, E), BF16)

    if consts is None:
        consts = emit_consts(b)
    onesrow_b = b.sb("onesrow_b", (1, 512), BF16)
    b.memset("vector", onesrow_b[:], 1.0)

    b.push()
    xn = alloc_xnT(b)
    b.push()
    emit_ln(b, x_in, xn, consts)
    b.pop()

    b.push()
    brot_hi = b.sb("brot_hi", (1, 3 * E), BF16)
    brot_lo = b.sb("brot_lo", (1, 3 * E), BF16)
    cosf = b.sb("cosf_s", (128, 512))
    sinf = b.sb("sinf_s", (128, 512))
    b.dma("sync", cosf[:], cosf_in)
    b.dma("sync", sinf[:], sinf_in)
    wb = [b.sb(f"wb{k}", (128, 3 * E), BF16) for k in range(NE)]
    wld = b.sb("wld", (128, 3 * E))
    t1 = b.sb("rot_t1", (128, 512))
    t2 = b.sb("rot_t2", (128, 512))
    t3 = b.sb("rot_t3", (128, 512))
    t4 = b.sb("rot_t4", (128, 512))
    for k in range(NE):
        b.dma("sync", wld[:], wattn_in[128 * k:128 * (k + 1), :])
        for half in range(2):
            o0 = 1024 * half
            ev = wld[:, o0:o0 + 1024:2]
            od = wld[:, o0 + 1:o0 + 1024:2]
            b.tt("vector", t1[:], ev, cosf[:], ALU.mult)
            b.tt("vector", t2[:], od, sinf[:], ALU.mult)
            b.tt("vector", t3[:], ev, sinf[:], ALU.mult)
            b.tt("vector", t4[:], od, cosf[:], ALU.mult)
            b.tt("vector", wb[k][:, o0:o0 + 1024:2], t1[:], t2[:], ALU.subtract)
            b.tt("vector", wb[k][:, o0 + 1:o0 + 1024:2], t3[:], t4[:], ALU.add)
        b.copy("gpsimd", wb[k][:, 2048:3072], wld[:, 2048:3072])
    bld = b.sb("bld", (1, 3 * E))
    brot = b.sb("brot", (1, 3 * E))
    blo = b.sb("blo", (1, 3 * E))
    b.dma("sync", bld[:], battn_in)
    for half in range(2):
        o0 = 1024 * half
        ev = bld[0:1, o0:o0 + 1024:2]
        od = bld[0:1, o0 + 1:o0 + 1024:2]
        b.tt("vector", t1[0:1, :], ev, cosf[0:1, :], ALU.mult)
        b.tt("vector", t2[0:1, :], od, sinf[0:1, :], ALU.mult)
        b.tt("vector", t3[0:1, :], ev, sinf[0:1, :], ALU.mult)
        b.tt("vector", t4[0:1, :], od, cosf[0:1, :], ALU.mult)
        b.tt("vector", brot[0:1, o0:o0 + 1024:2], t1[0:1, :], t2[0:1, :], ALU.subtract)
        b.tt("vector", brot[0:1, o0 + 1:o0 + 1024:2], t3[0:1, :], t4[0:1, :], ALU.add)
    b.copy("vector", brot[0:1, 2048:3072], bld[0:1, 2048:3072])
    b.copy("vector", brot_hi[:], brot[:])
    b.tt("vector", blo[:], brot[:], brot_hi[:], ALU.subtract)
    b.copy("vector", brot_lo[:], blo[:])

    ps_qk = [b.ps(f"ps_qk{i}", (128, 512)) for i in range(3)]
    qev = [b.sb("qev", (128, 512), BF16), b.sb("qev2", (128, 512), BF16),
           b.sb("qev3", (128, 512), BF16)]
    rr_i = [0]
    for ot in range(16):
        for n in range(8):
            w = rr_i[0] % 3
            rr_i[0] += 1
            for k in range(NE):
                b.mm(ps_qk[w][:], wb[k][:, 128 * ot:128 * (ot + 1)],
                     xn[k][:, 512 * n:512 * (n + 1)], start=(k == 0), stop=False)
            b.mm(ps_qk[w][:], brot_hi[0:1, 128 * ot:128 * (ot + 1)], onesrow_b[:],
                 start=False, stop=True)
            if ot < 8:
                b.smul(qev[w][:], ps_qk[w][:], 0.125)
                b.dma("scalar", qh_dram[2 * ot, :, 512 * n:512 * (n + 1)], qev[w][0:64, :])
                b.dma("scalar", qh_dram[2 * ot + 1, :, 512 * n:512 * (n + 1)], qev[w][64:128, :])
            else:
                b.copy("scalar", qev[w][:], ps_qk[w][:])
                h0 = 2 * (ot - 8)
                b.dma("scalar", kh_dram[h0, :, 512 * n:512 * (n + 1)], qev[w][0:64, :])
                b.dma("scalar", kh_dram[h0 + 1, :, 512 * n:512 * (n + 1)], qev[w][64:128, :])
    vev = [b.sb("vev", (128, 512), BF16), b.sb("vevb", (128, 512), BF16),
           b.sb("vevc", (128, 512), BF16)]
    for t in range(NT):
        for c in range(2):
            w = rr_i[0] % 3
            rr_i[0] += 1
            for k in range(NE):
                b.mm(ps_qk[w][:], xn[k][:, 128 * t:128 * (t + 1)],
                     wb[k][:, 2048 + 512 * c:2048 + 512 * (c + 1)],
                     start=(k == 0), stop=False)
            b.mm(ps_qk[w][:], consts["ones1"][0:1, 0:128],
                 brot_hi[0:1, 2048 + 512 * c:2048 + 512 * (c + 1)],
                 start=False, stop=True)
            b.copy("scalar", vev[w][:], ps_qk[w][:])
            b.dma("scalar", v_dram[128 * t:128 * (t + 1), 512 * c:512 * (c + 1)], vev[w][:])
    b.pop()
    b.pop()

    b.push()
    ctxn = [b.sb(f"ctxn{p}", (128, T), BF16) for p in range(8)]
    b.push()
    v_aug = b.sb("v_aug", (128, NT * 1040), BF16)
    vaug_v = v_aug[:].rearrange("p (a c) -> p a c", c=65)
    b.memset("vector", vaug_v[:, :, 64:65], 1.0)
    for kt in range(NT):
        dst = v_aug[:, 1040 * kt:1040 * (kt + 1)].rearrange(
            "p (h c) -> p h c", c=65)[:, :, 0:64]
        b.dma("sync", dst, v_dram[128 * kt:128 * (kt + 1), :].rearrange(
            "p (h d) -> p h d", d=64))
    ql_buf = [b.sb(f"ql_buf{i}", (65, T), BF16) for i in range(4)]
    kl_buf = [b.sb(f"kl_buf{i}", (65, T), BF16) for i in range(4)]
    for i in range(4):
        b.memset("vector", kl_buf[i][64:65, :], 1.0)
    psR = [b.ps("psR0", (128, 1024)), b.ps("psR1", (128, 1024)),
           b.ps("psR2", (128, 1024))]
    ps_ctx2 = [b.ps("ps_ctx", (128, 512)), b.ps("ps_ctxB", (128, 512))]
    fbufs = flash_bufs(b)
    loaded = set()

    def load_h(h):
        if h in loaded or h > 15:
            return
        loaded.add(h)
        b.dma("sync", ql_buf[h % 4][0:64, :], qh_dram[h])
        b.dma("sync", kl_buf[h % 4][0:64, :], kh_dram[h])

    def ql_of(h):
        load_h(h)
        return ql_buf[h % 4]

    def latKT_of(h):
        return kl_buf[h % 4]

    def latV_of(h, kt):
        return v_aug[:, 1040 * kt + 65 * h:1040 * kt + 65 * (h + 1)]

    def sample_of(h, gt):
        buf = ql_buf[h % 4]
        kb = kl_buf[h % 4]
        stride = max(1, gt)
        lhsT = buf[0:64, 128 * gt:128 * (gt + 1)]
        rhs = kb[0:64, 0:128 * stride:stride] if gt > 0 else kb[0:64, 0:128]
        return lhsT, rhs, gt == 0

    def ctx_sink(h, qg, ctx_rows, rsb):
        dst = ctxn[h // 2][64 * (h % 2):64 * (h % 2) + 64, 512 * qg:512 * (qg + 1)]
        b.tt("vector", dst, ctx_rows, rsb[:], ALU.mult)

    for p in range(8):
        flash_kmajor(b, consts, fbufs, [2 * p, 2 * p + 1], ql_of, latKT_of,
                     latV_of, lambda qg: 4 * (qg + 1), ctx_sink, sample_of,
                     psR, ps_ctx2, 1.0,
                     qg_filler=(lambda p=p: (load_h(2 * p + 2), load_h(2 * p + 3))))
    b.pop()

    b.push()
    wp = [b.sb(f"wp{k}", (128, E), BF16) for k in range(NE)]
    wldp = b.sb("wldp", (128, E))
    for k in range(NE):
        load_cast(b, wp[k], wproj_in[128 * k:128 * (k + 1), :], wldp)
    bp_hi = b.sb("bp_hi", (1, E), BF16)
    bp_lo = b.sb("bp_lo", (1, E), BF16)
    bpl = b.sb("bpl", (1, E))
    bpl2 = b.sb("bpl2", (1, E))
    b.dma("sync", bpl[:], bproj_in)
    b.copy("vector", bp_hi[:], bpl[:])
    b.tt("vector", bpl2[:], bpl[:], bp_hi[:], ALU.subtract)
    b.copy("vector", bp_lo[:], bpl2[:])
    x_t2 = [b.sb("x_t2", (128, E)), b.sb("x_t2b", (128, E))]
    ps_o = [b.ps(f"ps_o{i}", (128, 512)) for i in range(4)]
    for t in range(NT):
        w = t % 2
        b.dma("sync", x_t2[w][:], x_in[128 * t:128 * (t + 1), :])
        for c in range(2):
            pso = ps_o[2 * w + c]
            for k in range(NE):
                b.mm(pso[:], ctxn[k][:, 128 * t:128 * (t + 1)],
                     wp[k][:, 512 * c:512 * (c + 1)], start=(k == 0), stop=False)
            b.mm(pso[:], consts["ones1"][0:1, 0:128],
                 bp_hi[0:1, 512 * c:512 * (c + 1)], start=False, stop=True)
            b.tt("vector", x_t2[w][:, 512 * c:512 * (c + 1)],
                 x_t2[w][:, 512 * c:512 * (c + 1)], pso[:], ALU.add)
        b.dma("scalar", x_out[128 * t:128 * (t + 1), :], x_t2[w][:])
    b.pop()
    if standalone:
        return b.done()
    return None


def l1_attn_inputs(x_core, inputs):
    H, hd = 16, 64
    theta = 1.0 / 10000.0 ** (np.arange(0, hd, 2, dtype=np.float64) / hd)
    ang = (np.arange(H, dtype=np.float64)[:, None] * theta[None, :])
    cos = np.cos(ang).astype(np.float32).reshape(1, 512)
    sin = np.sin(ang).astype(np.float32).reshape(1, 512)
    return {"x": np.ascontiguousarray(x_core),
            "wattn": np.ascontiguousarray(inputs["L1_attn_W_attn"]),
            "battn": np.ascontiguousarray(inputs["L1_attn_b_attn"].reshape(1, -1)),
            "wproj": np.ascontiguousarray(inputs["L1_attn_W_proj"]),
            "bproj": np.ascontiguousarray(inputs["L1_attn_b_proj"].reshape(1, -1)),
            "cosf": np.broadcast_to(cos, (128, 512)).copy(),
            "sinf": np.broadcast_to(sin, (128, 512)).copy(),
            **host_consts()}




def emit_quant(b, dst_bf, src, nblk, qb):
    w = 16 * nblk
    am, sc, rsc = qb["am"], qb["sc"], qb["rsc"]
    xn, sq, ca, cb, cb2 = (qb["xn"], qb["sq"], qb["ca"], qb["cb"], qb["cb2"])
    blk = lambda ap: ap.rearrange("p (a c) -> p a c", c=16)
    b.red("vector", am[:, 0:nblk].unsqueeze(2), blk(src), op=ALU.max, absval=True)
    b.ts("vector", sc[:, 0:nblk], am[:, 0:nblk], 1.0 / 6.0, 1e-12, op0=ALU.mult, op1=ALU.max)
    b.recip(rsc[:, 0:nblk], sc[:, 0:nblk])
    b.act("scalar", cb[:, 0:w], src, AF.Sign)
    b.tt("vector", blk(xn[:, 0:w]), blk(src),
         rsc[:, 0:nblk].unsqueeze(2).broadcast_to((128, nblk, 16)), ALU.mult)
    b.tt("vector", sq[:, 0:w], xn[:, 0:w], xn[:, 0:w], ALU.mult)
    THR = [(0.0625, 0.5), (0.5625, 0.5), (1.5625, 0.5), (3.0625, 0.5),
           (6.25, 1.0), (12.25, 1.0), (25.0, 2.0)]
    t0_, w0_ = THR[0]
    b.ts("vector", ca[:, 0:w], sq[:, 0:w], t0_, w0_, op0=ALU.is_gt, op1=ALU.mult)
    for ti, wi in THR[1:]:
        b.ts("vector", cb2[:, 0:w], sq[:, 0:w], ti, wi, op0=ALU.is_gt, op1=ALU.mult)
        b.tt("vector", ca[:, 0:w], ca[:, 0:w], cb2[:, 0:w], ALU.add)
    b.tt("vector", ca[:, 0:w], ca[:, 0:w], cb[:, 0:w], ALU.mult)
    b.tt("vector", blk(dst_bf), blk(ca[:, 0:w]),
         sc[:, 0:nblk].unsqueeze(2).broadcast_to((128, nblk, 16)), ALU.mult)


def quant_bufs(b, maxw=2816):
    nb = maxw // 16
    return {"am": b.sb("q_am", (128, nb)), "sc": b.sb("q_sc", (128, nb)),
            "rsc": b.sb("q_rsc", (128, nb)), "xn": b.sb("q_xn", (128, maxw), BF16),
            "sq": b.sb("q_sq", (128, maxw), BF16), "ca": b.sb("q_ca", (128, maxw), BF16),
            "cb": b.sb("q_cb", (128, maxw), BF16),
            "cb2": b.sb("q_cb2", (128, maxw), BF16)}


def build_l1_ffn(b=None, consts=None, x_in=None, x_out=None):
    standalone = b is None
    I = 2816
    NI = I // 128
    if standalone:
        b = B("l1f")
        x_in = b.inp("x", (T, E))
        x_out = b.outp("y", (T, E))
    wg_in = b.inp("wg", (I, E))
    wu_in = b.inp("wu", (I, E))
    wd_in = b.inp("wd", (E, I))
    xqT_dram = b.dram("xqT_st", (NT, 128, E), BF16)
    hqT_dram = b.dram("hqT_st", (NT, 128, I), BF16)
    wdqT_dram = b.dram("wdqT_st", (NI, 128, E), BF16)

    if consts is None:
        consts = emit_consts(b, rb=False)
    qb = quant_bufs(b)
    wgqT = [b.sb(f"wgqT{k}", (128, I), BF16) for k in range(NE)]
    wuqT = [b.sb(f"wuqT{k}", (128, I), BF16) for k in range(NE)]

    b.push()
    qb2 = [quant_bufs(b, maxw=E), quant_bufs(b, maxw=E)]
    xq_t = [b.sb("xq_t", (128, E), BF16), b.sb("xq_t2", (128, E), BF16)]
    xq_acc = [b.sb("xq_acc", (128, E), BF16), b.sb("xq_acc2", (128, E), BF16)]
    ps_tr = b.ps("ps_tr", (128, 512), BF16)

    xpost = [None]

    def x_sink(t, xnf):
        s = t % 2
        if xpost[0] is not None:
            xpost[0]()
            xpost[0] = None
        emit_quant(b, xq_t[s][:], xnf[:], E // 16, qb2[s])

        def mk(t=t, s=s):
            def fn():
                for g4 in range(2):
                    for slot in range(4):
                        i = 4 * g4 + slot
                        b.trans(ps_tr[:, 128 * slot:128 * (slot + 1)],
                                xq_t[s][:, 128 * i:128 * (i + 1)], consts["identb"][:])
                    b.copy("scalar", xq_acc[s][:, 512 * g4:512 * (g4 + 1)], ps_tr[:, 0:512])
                b.dma("scalar", xqT_dram[t], xq_acc[s][:])
            return fn
        xpost[0] = mk()
        if t == NT - 1:
            xpost[0]()
            xpost[0] = None
    emit_ln(b, x_in, None, consts, fp32_sink=x_sink)
    b.pop()

    b.push()
    qbw = [quant_bufs(b), quant_bufs(b)]
    wld = [b.sb("wld", (128, I)), b.sb("wld2", (128, I))]
    wqt = [qbw[0]["xn"], qbw[1]["xn"]]
    ps_trw = b.ps("ps_trw", (128, 512), BF16)
    for wi, (wsrc, wdstT) in enumerate(((wg_in, wgqT), (wu_in, wuqT))):
        for j in range(NI):
            s = j % 2
            b.dma("sync", wld[s][:, 0:E], wsrc[128 * j:128 * (j + 1), :])
            emit_quant(b, wqt[s][:, 0:E], wld[s][:, 0:E], E // 16, qbw[s])
            for k in range(NE):
                slot = k % 4
                b.trans(ps_trw[:, 128 * slot:128 * (slot + 1)],
                        wqt[s][:, 128 * k:128 * (k + 1)], consts["identb"][:])
                b.copy("scalar", wdstT[k][:, 128 * j:128 * (j + 1)],
                       ps_trw[:, 128 * slot:128 * (slot + 1)])
    _wa = b.sb("wdq_acc", (128, I), BF16)
    wdq_acc = [_wa, _wa]
    for j in range(NE):
        s = j % 2
        b.dma("sync", wld[s][:], wd_in[128 * j:128 * (j + 1), :])
        emit_quant(b, wqt[s][:], wld[s][:], I // 16, qbw[s])
        for kk in range(NI):
            slot = kk % 4
            b.trans(ps_trw[:, 128 * slot:128 * (slot + 1)],
                    wqt[s][:, 128 * kk:128 * (kk + 1)], consts["identb"][:])
            b.copy("scalar", wdq_acc[s][:, 128 * kk:128 * (kk + 1)],
                   ps_trw[:, 128 * slot:128 * (slot + 1)])
        for kk in range(NI):
            b.dma("gpsimd", wdqT_dram[kk, :, 128 * j:128 * (j + 1)],
                  wdq_acc[s][:, 128 * kk:128 * (kk + 1)])
    b.pop()

    b.push()
    CH = [(0, 512), (512, 1024), (1024, 1536), (1536, 2048), (2048, 2560), (2560, 2816)]
    qbc = [quant_bufs(b), quant_bufs(b)]
    xq_l = [b.sb("xq_l", (128, E), BF16), b.sb("xq_l2", (128, E), BF16)]
    ps_g = [b.ps(f"ps_g{i}", (128, 512)) for i in range(3)]
    ps_u = [b.ps(f"ps_u{i}", (128, 512)) for i in range(3)]
    ps_trh = b.ps("ps_trh", (128, 512), BF16)
    hsil = [b.sb("hsil", (128, 512)), b.sb("hsil2", (128, 512))]
    usq = [b.sb("usq", (128, 512)), b.sb("usq2", (128, 512))]
    _ht = b.sb("h_t", (128, I))
    h_t = [_ht, _ht]
    _ha = b.sb("hq_acc", (128, I), BF16)
    hq_acc = [_ha, _ha]
    post = [None, None]
    for t in range(NT):
        s = t % 2
        b.dma("sync", xq_l[s][:], xqT_dram[t])
        for half in range(2):
            chs = CH[3 * half:3 * half + 3]
            for k in range(NE):
                lhsT = xq_l[s][:, 128 * k:128 * (k + 1)]
                for ci, (c0, c1) in enumerate(chs):
                    b.mm(ps_g[ci][:, 0:c1 - c0], lhsT, wgqT[k][:, c0:c1],
                         start=(k == 0), stop=(k == NE - 1))
                    b.mm(ps_u[ci][:, 0:c1 - c0], lhsT, wuqT[k][:, c0:c1],
                         start=(k == 0), stop=(k == NE - 1))
            if half == 0 and post[0] is not None:
                post[0]()
                post[0] = None
            for ci, (c0, c1) in enumerate(chs):
                w = c1 - c0
                b.act("scalar", hsil[ci % 2][:, 0:w], ps_g[ci][:, 0:w], AF.Silu)
                b.copy("scalar", usq[ci % 2][:, 0:w], ps_u[ci][:, 0:w])
                b.tt("gpsimd", h_t[s][:, c0:c1], hsil[ci % 2][:, 0:w], usq[ci % 2][:, 0:w], ALU.mult)
        hq_t = qbc[s]["xn"]
        emit_quant(b, hq_t[:, 0:I], h_t[s][:], I // 16, qbc[s])

        def mkpost(t=t, s=s, hq_t=hq_t):
            def post_fn():
                for kk0 in range(0, NI, 4):
                    n4 = min(4, NI - kk0)
                    for slot in range(n4):
                        b.trans(ps_trh[:, 128 * slot:128 * (slot + 1)],
                                hq_t[:, 128 * (kk0 + slot):128 * (kk0 + slot + 1)],
                                consts["identb"][:])
                    b.copy("scalar", hq_acc[s][:, 128 * kk0:128 * (kk0 + n4)],
                           ps_trh[:, 0:128 * n4])
                b.dma("scalar", hqT_dram[t], hq_acc[s][:])
            return post_fn
        post[0], post[1] = post[1], mkpost()
    for p in post:
        if p is not None:
            p()
    b.pop()

    b.push()
    wdqT = [b.sb(f"wdqT{kk}", (128, E), BF16) for kk in range(NI)]
    for kk in range(NI):
        b.dma("gpsimd", wdqT[kk][:], wdqT_dram[kk])
    hq_l = [b.sb("hq_l", (128, I), BF16), b.sb("hq_l2", (128, I), BF16)]
    x_t2 = [b.sb("x_t2", (128, E)), b.sb("x_t2b", (128, E))]
    ps_o0 = b.ps("ps_o0", (128, 512))
    ps_o1 = b.ps("ps_o1", (128, 512))
    for t in range(NT):
        s = t % 2
        b.dma("sync", hq_l[s][:], hqT_dram[t])
        b.dma("sync", x_t2[s][:], x_in[128 * t:128 * (t + 1), :])
        for kk in range(NI):
            lhsT = hq_l[s][:, 128 * kk:128 * (kk + 1)]
            b.mm(ps_o0[:], lhsT, wdqT[kk][:, 0:512], start=(kk == 0), stop=(kk == NI - 1))
            b.mm(ps_o1[:], lhsT, wdqT[kk][:, 512:1024], start=(kk == 0), stop=(kk == NI - 1))
        b.tt("vector", x_t2[s][:, 0:512], x_t2[s][:, 0:512], ps_o0[:], ALU.add)
        b.tt("vector", x_t2[s][:, 512:1024], x_t2[s][:, 512:1024], ps_o1[:], ALU.add)
        b.dma("scalar", x_out[128 * t:128 * (t + 1), :], x_t2[s][:])
    b.pop()
    if standalone:
        return b.done()
    return None


def l1_ffn_inputs(x_core, inputs):
    return {"x": np.ascontiguousarray(x_core),
            "wg": np.ascontiguousarray(inputs["L1_ffn_w_gate"]),
            "wu": np.ascontiguousarray(inputs["L1_ffn_w_up"]),
            "wd": np.ascontiguousarray(inputs["L1_ffn_w_down"]),
            **host_consts(rb=False)}


def build_module():
    b = B("mod")
    x0 = b.inp("x", (T, E))
    y = b.outp("y", (T, E))
    x1 = b.dram("x1_st", (T, E))
    x2 = b.dram("x2_st", (T, E))
    x3 = b.dram("x3_st", (T, E))
    consts = emit_consts(b, rb=True)
    for fn, xi, xo in ((build_l0_attn, x0, x1), (build_l0_ffn, x1, x2),
                       (build_l1_attn, x2, x3), (build_l1_ffn, x3, y)):
        depth = len(b.scopes)
        b.push()
        fn(b, consts, xi, xo)
        while len(b.scopes) > depth:
            b.pop()
    return b.done()


def module_inputs(x_core, inputs):
    d = {}
    d.update(l0_attn_inputs(x_core, inputs))
    d.update(l0_ffn_inputs(x_core, inputs))
    d.update(l1_attn_inputs(x_core, inputs))
    d.update(l1_ffn_inputs(x_core, inputs))
    d["x"] = np.ascontiguousarray(x_core)
    return d


def kernel(**inputs):
    x = np.ascontiguousarray(inputs["L0_attn_hidden_states"], dtype=np.float32)
    n_cores = 8
    nc = build_module()
    in_maps = [module_inputs(x[c], inputs) for c in range(n_cores)]
    res = run_bass_kernel_spmd(nc, in_maps, core_ids=list(range(n_cores)))
    return np.stack([np.asarray(res.results[c]["y"]) for c in range(n_cores)])
